# version 14
# baseline (speedup 1.0000x reference)
import contextlib

import numpy as np
import ml_dtypes

import concourse.bass as bass
import concourse.mybir as mybir
from concourse.bass_utils import run_bass_kernel_spmd

F32 = mybir.dt.float32
BF16 = mybir.dt.bfloat16
AF = mybir.ActivationFunctionType
ALU = mybir.AluOpType
AX = mybir.AxisListType

HID = 1024
NHL = 8
HD = 64
C = 128
SC = 512
KCH = HID // 128

BF = ml_dtypes.bfloat16


class Sem:
    __slots__ = ("h", "count", "name")
    def __init__(self, h, name):
        self.h = h
        self.count = 0
        self.name = name


class Buf:
    __slots__ = ("last_w", "readers", "name")
    def __init__(self, name=""):
        self.last_w = None
        self.readers = []
        self.name = name


class Eng:
    def __init__(self, prog, name):
        self.prog = prog
        self.name = name
        self.sem = prog.new_sem("ctr_" + name)
        self.marks = {}
        self.q = []

    def _wait(self, toks):
        need = {}
        for sem, val in toks:
            sid = id(sem)
            if self.marks.get(sid, 0) >= val:
                continue
            cur = need.get(sid)
            if cur is None or cur[1] < val:
                need[sid] = (sem, val)
        for sid, (sem, val) in need.items():
            self.marks[sid] = val
            self.q.append(lambda e, s=sem.h, v=val: e.wait_ge(s, v))

    def _collect(self, reads, writes):
        toks = []
        for b in reads:
            if b.last_w:
                toks.append(b.last_w)
        for b in writes:
            if b.last_w:
                toks.append(b.last_w)
            toks.extend(b.readers)
        return toks

    def _record(self, tok, reads, writes):
        for b in reads:
            b.readers.append(tok)
        for b in writes:
            b.last_w = tok
            b.readers = []

    def op(self, fn, reads=(), writes=(), sem=None, inc=1):
        self._wait(self._collect(reads, writes))
        usem = sem if sem is not None else self.sem
        usem.count += inc
        cnt = usem.count
        def run(e, fn=fn, s=usem.h, v=inc):
            fn(e).then_inc(s, v)
        self.q.append(run)
        tok = (usem, cnt)
        self._record(tok, reads, writes)
        return tok

    def group(self, fns, reads=(), writes=()):
        self._wait(self._collect(reads, writes))
        self.sem.count += 1
        cnt = self.sem.count
        last = len(fns) - 1
        for i, fn in enumerate(fns):
            if i == last:
                self.q.append(lambda e, fn=fn, s=self.sem.h: fn(e).then_inc(s, 1))
            else:
                self.q.append(lambda e, fn=fn: fn(e))
        tok = (self.sem, cnt)
        self._record(tok, reads, writes)
        return tok

    def dma_batch(self, fns, reads=(), writes=(), sem=None):
        self._wait(self._collect(reads, writes))
        for fn in fns:
            sem.count += 16
            self.q.append(lambda e, fn=fn, s=sem.h: fn(e).then_inc(s, 16))
        tok = (sem, sem.count)
        self._record(tok, reads, writes)
        return tok

    def barrier_wait(self, toks):
        self._wait(toks)


class Arena:
    def __init__(self, t, ncols):
        self.t = t
        self.ncols = ncols
        self.off = 0
        self.perm_end = 0
        self.hwm = 0

    def alloc(self, cols, parts=128):
        o = self.off
        assert o + cols <= self.ncols, "arena overflow (%d + %d > %d)" % (o, cols, self.ncols)
        self.off += cols
        self.hwm = max(self.hwm, self.off)
        return self.t[0:parts, o:o + cols]

    def mark_perm(self):
        self.perm_end = self.off

    def push(self):
        return self.off

    def pop(self, saved):
        self.off = saved

    def reset(self):
        self.off = self.perm_end


class Prog:
    def __init__(self, nc, es, f32_cols=28000, bf16_cols=96000):
        self.nc = nc
        self.es = es
        self.all_sems = []
        self.sync = Eng(self, "sync")
        self.scalar = Eng(self, "scalar")
        self.vector = Eng(self, "vector")
        self.tensor = Eng(self, "tensor")
        self.gpsimd = Eng(self, "gpsimd")
        self.engines = [self.sync, self.scalar, self.vector, self.tensor, self.gpsimd]
        self.af = Arena(es.enter_context(nc.sbuf_tensor("arena_f32", [128, f32_cols], F32)), f32_cols)
        self.ab = Arena(es.enter_context(nc.sbuf_tensor("arena_bf16", [128, bf16_cols], BF16)), bf16_cols)
        self.psum = [es.enter_context(nc.psum_tensor("pb%d" % i, [128, 512], F32)) for i in range(7)]
        self.psum_bf = es.enter_context(nc.psum_tensor("pb7", [128, 1024], BF16))
        self.bank = [Buf("bank%d" % i) for i in range(8)]

    def sb(self, name, shape, dtype):
        return self.es.enter_context(self.nc.sbuf_tensor(name, list(shape), dtype))

    def new_sem(self, name):
        s = Sem(self.es.enter_context(self.nc.semaphore(name)), name)
        self.all_sems.append(s)
        return s

    def f32(self, cols, parts=128):
        return self.af.alloc(cols, parts)

    def bf16(self, cols, parts=128):
        return self.ab.alloc(cols, parts)

    def barrier(self, extra=()):
        toks = [(e.sem, e.sem.count) for e in self.engines]
        toks += [(s, s.count) for s in extra]
        for e in self.engines:
            e.barrier_wait(toks)

    def finalize(self):
        with self.nc.Block() as block:
            @block.sync
            def _(e):
                for f in self.sync.q:
                    f(e)
            @block.scalar
            def _(e):
                for f in self.scalar.q:
                    f(e)
            @block.vector
            def _(e):
                for f in self.vector.q:
                    f(e)
            @block.tensor
            def _(e):
                for f in self.tensor.q:
                    f(e)
            @block.gpsimd
            def _(e):
                for f in self.gpsimd.q:
                    f(e)


def mm3(psum_ap, a_hi, a_lo, b_hi, b_lo, tp=None, start=True, stop=True):
    def mk(lhs, rhs, st, sp):
        return lambda e: e.matmul(psum_ap, lhs, rhs, start=st, stop=sp, tile_position=tp)
    return [mk(a_hi, b_hi, start, False),
            mk(a_hi, b_lo, False, False),
            mk(a_lo, b_hi, False, stop)]


def split_psum(P, psum_ap, bank, hi_ap, lo_ap, hi_b, lo_b, extra=()):
    P.scalar.op(lambda e: e.copy(out=hi_ap, in_=psum_ap),
                reads=[bank, *extra], writes=[hi_b])
    P.vector.op(lambda e: e.tensor_tensor(out=lo_ap, in0=psum_ap, in1=hi_ap, op=ALU.subtract),
                reads=[bank, hi_b], writes=[lo_b])


def split_sbuf(P, src_ap, src_b, hi_ap, lo_ap, hi_b, lo_b, lo_eng=None):
    P.scalar.op(lambda e: e.copy(out=hi_ap, in_=src_ap), reads=[src_b], writes=[hi_b])
    (lo_eng or P.vector).op(
        lambda e: e.tensor_tensor(out=lo_ap, in0=src_ap, in1=hi_ap, op=ALU.subtract),
        reads=[src_b, hi_b], writes=[lo_b])


class NS:
    def __init__(self, **kw):
        self.__dict__.update(kw)


def declare_drams(nc, cfg):
    d = NS()
    L = cfg.L
    TOWN = L // 2
    ein, eout, intl = "ExternalInput", "ExternalOutput", "Internal"
    d.x_seq = nc.dram_tensor("x_seq", [L, HID], F32, kind=ein).ap()
    d.x_own = nc.dram_tensor("x_own", [TOWN, HID], F32, kind=ein).ap()
    d.c_hm = nc.dram_tensor("c_hm", [128, 2], F32, kind=ein).ap()
    d.wq = nc.dram_tensor("wq", [HID, 512], F32, kind=ein).ap()
    d.wk = nc.dram_tensor("wk", [HID, 512], F32, kind=ein).ap()
    d.wv = nc.dram_tensor("wv", [HID, 512], F32, kind=ein).ap()
    d.wgk1 = nc.dram_tensor("wgk1", [HID, 16], F32, kind=ein).ap()
    d.wgk2 = nc.dram_tensor("wgk2", [16, 512], F32, kind=ein).ap()
    d.bgk2t = nc.dram_tensor("bgk2t", [128, 4], F32, kind=ein).ap()
    d.wgp = nc.dram_tensor("wgp", [HID, HID], F32, kind=ein).ap()
    d.wop = nc.dram_tensor("wop", [HID, HID], F32, kind=ein).ap()
    d.router = nc.dram_tensor("router", [HID, 16], F32, kind=ein).ap()
    d.gate_up = nc.dram_tensor("gate_up", [16, HID, HID], F32, kind=ein).ap()
    d.down = nc.dram_tensor("down", [16, 512, HID], F32, kind=ein).ap()
    d.gubias = nc.dram_tensor("gubias", [128, 128], F32, kind=ein).ap()
    d.dbias = nc.dram_tensor("dbias", [16, HID], F32, kind=ein).ap()
    d.c_idf = nc.dram_tensor("c_idf", [128, 128], F32, kind=ein).ap()
    d.c_idb = nc.dram_tensor("c_idb", [128, 128], BF16, kind=ein).ap()
    d.c_triu = nc.dram_tensor("c_triu", [128, 128], F32, kind=ein).ap()
    d.c_ones = nc.dram_tensor("c_ones", [1, 128], BF16, kind=ein).ap()
    d.c_sel = nc.dram_tensor("c_sel", [16, 2048], BF16, kind=ein).ap()
    d.c_eps = nc.dram_tensor("c_eps", [128, 1], F32, kind=ein).ap()
    d.c_iota16 = nc.dram_tensor("c_iota16", [128, 16], F32, kind=ein).ap()
    d.c_cap16 = nc.dram_tensor("c_cap16", [16, 1], F32, kind=ein).ap()
    d.c_l17 = nc.dram_tensor("c_l17", [17, 17], F32, kind=ein).ap()
    d.c_gamma = nc.dram_tensor("c_gamma", [128, HID], F32, kind=ein).ap()
    d.c_beta = nc.dram_tensor("c_beta", [128, HID], F32, kind=ein).ap()
    d.y = nc.dram_tensor("y", [TOWN, HID], F32, kind=eout).ap()
    if cfg.upto == "attn":
        d.cc_in = nc.dram_tensor("cc_in", [L, 512], F32, kind=eout).ap()
        d.cc_out = None
    else:
        d.cc_in = nc.dram_tensor("cc_in", [L, 512], F32, kind=intl).ap()
        d.npc = 4 if L >= 8192 else 2
        d.psz = L // d.npc
        d.cc_out = [nc.dram_tensor("cc_out%d" % i, [2 * d.psz, 512], F32, kind=intl).ap()
                    for i in range(d.npc)]
    d.wsgu = nc.dram_tensor("wsgu", [16, KCH, 128, 2048], BF16, kind=intl).ap()
    d.wsdn = nc.dram_tensor("wsdn", [16, 4, 128, 2048], BF16, kind=intl).ap()
    d.x1h_sp = nc.dram_tensor("x1h_sp", [KCH, 128, TOWN], BF16, kind=intl).ap()
    d.scth_sp = nc.dram_tensor("scth_sp", [16, TOWN], BF16, kind=intl).ap()
    d.sctl_sp = nc.dram_tensor("sctl_sp", [16, TOWN], BF16, kind=intl).ap()
    d.x1tm_sp = nc.dram_tensor("x1tm_sp", [TOWN, HID], F32, kind=intl).ap()
    d.weq_sp = nc.dram_tensor("weq_sp", [TOWN, 4], F32, kind=intl).ap()
    CAP_ = 768 if TOWN >= 2048 else TOWN
    d.y_sp = nc.dram_tensor("y_sp", [16 * CAP_, HID], F32, kind=intl).ap()
    d.x1g_all = nc.dram_tensor("x1g_all", [16 * CAP_, HID], F32, kind=intl).ap()
    d.slotf_sp = nc.dram_tensor("slotf_sp", [16, TOWN], F32, kind=intl).ap()
    d.idx_sp = nc.dram_tensor("idx_sp", [16 * 48, 16], mybir.dt.int32, kind=intl).ap()
    d.x1l_sp = nc.dram_tensor("x1l_sp", [KCH, 128, TOWN], BF16, kind=intl).ap()
    if cfg.dbg:
        d.dbg_x1 = nc.dram_tensor("dbg_x1", [TOWN, HID], F32, kind=eout).ap()
        d.dbg_sc = nc.dram_tensor("dbg_sc", [TOWN, 16], F32, kind=eout).ap()
    return d


def emit_consts(P, d, cfg):
    c = NS()
    csem = P.new_sem("c_ld")
    nload = [0]

    def ld(ap_dst, ap_src):
        nload[0] += 1
        P.sync.op(lambda e: e.dma_start(out=ap_dst, in_=ap_src), sem=csem, inc=16)

    c.idf = P.f32(128); ld(c.idf, d.c_idf[:, :])
    c.idb = P.bf16(128); ld(c.idb, d.c_idb[:, :])
    c.triu = P.f32(128); ld(c.triu, d.c_triu[:, :])
    c.ones = P.bf16(128, parts=1); ld(c.ones, d.c_ones[:, :])
    c.eps = P.f32(1); ld(c.eps, d.c_eps[:, :])
    c.hm = P.f32(2); ld(c.hm, d.c_hm[:, :])
    c.iota16 = P.f32(16); ld(c.iota16, d.c_iota16[:, :])
    c.cap16 = P.f32(1, parts=16); ld(c.cap16, d.c_cap16[:, :])
    c.l17 = P.f32(17, parts=17); ld(c.l17, d.c_l17[:, :])

    c.bgk2 = P.f32(4); ld(c.bgk2, d.bgk2t[:, :])
    c.gub = P.f32(128); ld(c.gub, d.gubias[:, :])
    tok = (csem, 16 * nload[0])
    raw_buf = Buf("craw")
    raw_buf.last_w = tok
    c.ld_tok = tok
    raw = P.f32(HID)
    rawsem = P.new_sem("c_raw")

    def spl(cols, parts, loads):
        for dst_sl, src_ap in loads:
            P.sync.op(lambda e, dst_sl=dst_sl, src_ap=src_ap: e.dma_start(
                out=raw[0:parts, dst_sl], in_=src_ap), sem=rawsem, inc=16,
                reads=[], writes=[raw_buf])
        hi = P.bf16(cols, parts); lo = P.bf16(cols, parts)
        hb, lb = Buf(), Buf()
        split_sbuf(P, raw[0:parts, 0:cols], raw_buf, hi, lo, hb, lb, lo_eng=P.gpsimd)
        return hi, lo, hb, lb

    c.wgk1h, c.wgk1l, c.wgk1h_b, c.wgk1l_b = spl(
        128, 128, [(slice(16 * k, 16 * k + 16), d.wgk1[128 * k:128 * (k + 1), :])
                   for k in range(KCH)])
    c.wgk2h, c.wgk2l, c.wgk2h_b, c.wgk2l_b = spl(512, 16, [(slice(0, 512), d.wgk2[:, :])])
    c.rth, c.rtl, c.rth_b, c.rtl_b = spl(
        128, 128, [(slice(16 * k, 16 * k + 16), d.router[128 * k:128 * (k + 1), :])
                   for k in range(KCH)])
    c.gub1 = P.f32(128)
    c.gub1_b = Buf()
    cb = Buf("consts")
    cb.last_w = (csem, 16 * nload[0])
    c.cb = cb
    P.scalar.op(lambda e: e.add(out=c.gub1, in_=c.gub, add=1.0),
                reads=[cb], writes=[c.gub1_b])
    return c


def emit_attn(P, d, c, cfg):
    L = cfg.L
    nsc = L // SC
    nch = L // C
    ten, vec, act, gps, syn = P.tensor, P.vector, P.scalar, P.gpsimd, P.sync

    stg = [P.f32(HID) for _ in range(2)]
    stg_sem = [P.new_sem("stg%d" % i) for i in range(2)]
    stg_buf = [Buf("stg%d" % i) for i in range(2)]
    wts = {}
    for wname, src in (("q", d.wq), ("k", d.wk), ("v", d.wv)):
        hi = P.bf16(KCH * 512); lo = P.bf16(KCH * 512)
        hb, lb = Buf(wname + "h"), Buf(wname + "l")
        for k in range(KCH):
            s = (len(wts) * KCH + k) % 2
            syn.op(lambda e, k=k, s=s, src=src: e.dma_start(
                out=stg[s][:, 0:512], in_=src[128 * k:128 * (k + 1), :]),
                reads=[], writes=[stg_buf[s]], sem=stg_sem[s], inc=16)
            act.op(lambda e, k=k, s=s, hi=hi: e.copy(out=hi[:, 512 * k:512 * (k + 1)],
                                                     in_=stg[s][:, 0:512]),
                   reads=[stg_buf[s]], writes=[hb])
            gps.op(lambda e, k=k, s=s, hi=hi, lo=lo: e.tensor_tensor(
                out=lo[:, 512 * k:512 * (k + 1)], in0=stg[s][:, 0:512],
                in1=hi[:, 512 * k:512 * (k + 1)], op=ALU.subtract),
                reads=[stg_buf[s], hb], writes=[lb])
        wts[wname] = (hi, lo, hb, lb)

    if cfg.upto != "attn":
        CAPa = 768 if (L // 2) >= 2048 else (L // 2)
        zz0 = P.f32(HID)
        zz0_b = Buf()
        vec.op(lambda e: e.memset(zz0, 0.0), writes=[zz0_b])
        c.x1g_buf = Buf("x1g")
        x1g_pre = P.new_sem("x1g_pre")
        P.scalar.dma_batch([lambda e, r=r: e.dma_start(
            out=d.x1g_all[128 * r:128 * (r + 1), :], in_=zz0)
            for r in range(16 * CAPa // 128)],
            reads=[zz0_b], writes=[c.x1g_buf], sem=x1g_pre)
        c.x1g_pre = x1g_pre

    x_t = [P.f32(HID) for _ in range(2)]
    x_sem = [P.new_sem("x%d" % i) for i in range(2)]
    x_buf = [Buf("x%d" % i) for i in range(2)]
    xTh = [P.bf16(KCH * 512)]; xTh.append(xTh[0])
    xTl = [P.bf16(KCH * 512)]; xTl.append(xTl[0])
    xT_hb = [Buf()]; xT_hb.append(xT_hb[0])
    xT_lb = [Buf()]; xT_lb.append(xT_lb[0])
    qh = [P.bf16(2048) for _ in range(2)]; ql = [P.bf16(2048) for _ in range(2)]
    kh = [P.bf16(2048) for _ in range(2)]; kl = [P.bf16(2048) for _ in range(2)]
    vv = [P.f32(2048) for _ in range(2)]
    gg = [P.f32(2048) for _ in range(2)]
    qh_b = [Buf() for _ in range(2)]; ql_b = [Buf() for _ in range(2)]
    kh_b = [Buf() for _ in range(2)]; kl_b = [Buf() for _ in range(2)]
    vv_b = [Buf() for _ in range(2)]; gg_b = [Buf() for _ in range(2)]
    z1h = [P.bf16(512, parts=16)]; z1h.append(z1h[0])
    z1l = [P.bf16(512, parts=16)]; z1l.append(z1l[0])
    z1h_b = [Buf()]; z1h_b.append(z1h_b[0])
    z1l_b = [Buf()]; z1l_b.append(z1l_b[0])
    sgt = [P.f32(512) for _ in range(2)]
    sgt_b = [Buf() for _ in range(2)]

    A = [P.f32(C) for _ in range(2)]; A_b = [Buf() for _ in range(2)]
    eNA = [P.f32(C) for _ in range(2)]; eNA_b = [Buf() for _ in range(2)]
    eATM = [P.f32(C) for _ in range(2)]; eATM_b = [Buf() for _ in range(2)]
    eacc = [P.f32(1) for _ in range(2)]; eacc_b = [Buf() for _ in range(2)]
    Vt = [P.f32(C) for _ in range(2)]; Vt_b = [Buf() for _ in range(2)]
    Wc = [P.f32(C) for _ in range(2)]; Wc_b = [Buf() for _ in range(2)]
    VTh = [P.bf16(C) for _ in range(2)]; VTh_b = [Buf() for _ in range(2)]
    VTl = [P.bf16(C) for _ in range(2)]; VTl_b = [Buf() for _ in range(2)]
    WTh = [P.bf16(C) for _ in range(2)]; WTh_b = [Buf() for _ in range(2)]
    WTl = [P.bf16(C) for _ in range(2)]; WTl_b = [Buf() for _ in range(2)]
    kT2h = [P.bf16(C) for _ in range(2)]; kT2h_b = [Buf() for _ in range(2)]
    kT2l = [P.bf16(C) for _ in range(2)]; kT2l_b = [Buf() for _ in range(2)]
    PTa = [P.f32(C) for _ in range(2)]; PTa_b = [Buf() for _ in range(2)]
    PTb = [P.f32(C) for _ in range(2)]; PTb_b = [Buf() for _ in range(2)]
    PTah = [P.bf16(C) for _ in range(2)]; PTah_b = [Buf() for _ in range(2)]
    PTal = [P.bf16(C) for _ in range(2)]; PTal_b = [Buf() for _ in range(2)]
    PTbh = [P.bf16(C) for _ in range(2)]; PTbh_b = [Buf() for _ in range(2)]
    PTbl = [P.bf16(C) for _ in range(2)]; PTbl_b = [Buf() for _ in range(2)]
    eacr = [P.f32(C, parts=1) for _ in range(2)]; eacr_b = [Buf() for _ in range(2)]
    eacrh = [P.bf16(C, parts=1) for _ in range(2)]; eacrh_b = [Buf() for _ in range(2)]
    eacrl = [P.bf16(C, parts=1) for _ in range(2)]; eacrl_b = [Buf() for _ in range(2)]
    tmpI = [P.f32(C) for _ in range(2)]; tmpI_b = [Buf() for _ in range(2)]
    tmpS = [P.f32(HD) for _ in range(2)]; tmpS_b = [Buf() for _ in range(2)]
    tmpO = [P.f32(C) for _ in range(2)]; tmpO_b = [Buf() for _ in range(2)]
    STf = [[P.f32(HD) for _ in range(2)] for _ in range(4)]
    STh = [[P.bf16(HD) for _ in range(2)] for _ in range(4)]
    STl = [[P.bf16(HD) for _ in range(2)] for _ in range(4)]
    STf_b = [[Buf() for _ in range(2)] for _ in range(4)]
    STh_b = [[Buf() for _ in range(2)] for _ in range(4)]
    STl_b = [[Buf() for _ in range(2)] for _ in range(4)]
    for hp in range(4):
        for pz in range(2):
            vec.op(lambda e, t=STf[hp][pz]: e.memset(t, 0.0), writes=[STf_b[hp][pz]])
            vec.op(lambda e, t=STh[hp][pz]: e.memset(t, 0.0), writes=[STh_b[hp][pz]])
            vec.op(lambda e, t=STl[hp][pz]: e.memset(t, 0.0), writes=[STl_b[hp][pz]])

    out_asm = [P.f32(512) for _ in range(2)]
    oa_buf = [Buf("oa%d" % i) for i in range(2)]
    oa_sem = [P.new_sem("oa%d" % i) for i in range(2)]
    cc_tot = [0]
    npc = getattr(d, "npc", 0) if cfg.upto != "attn" else 0
    cc_sems = [P.new_sem("ccp%d" % i) for i in range(npc)]
    ccout_bufs = [Buf("cco%d" % i) for i in range(npc)]

    pk = [0]

    def proj_bank():
        b = pk[0] % 2
        pk[0] += 1
        return b

    def emit_proj(sc):
        pi = sc % 2
        for t4 in range(4):
            slot = (sc * 4 + t4) % 2
            syn.op(lambda e, sc=sc, t4=t4, slot=slot: e.dma_start(
                out=x_t[slot], in_=d.x_seq[sc * SC + t4 * 128: sc * SC + t4 * 128 + 128, :]),
                writes=[x_buf[slot]], sem=x_sem[slot], inc=16)
            for khalf in range(2):
                b = proj_bank()
                fns = []
                for kk in range(4 * khalf, 4 * khalf + 4):
                    fns.append(lambda e, kk=kk, b=b, slot=slot: e.transpose(
                        P.psum[b][:, (kk % 4) * 128:(kk % 4) * 128 + 128],
                        x_t[slot][:, 128 * kk:128 * kk + 128], c.idf))
                ten.group(fns, reads=[x_buf[slot], c.cb], writes=[P.bank[b]])
                for kk in range(4 * khalf, 4 * khalf + 4):
                    ps = P.psum[b][:, (kk % 4) * 128:(kk % 4) * 128 + 128]
                    co = 512 * kk + 128 * t4
                    act.op(lambda e, ps=ps, co=co, pi=pi: e.copy(
                        out=xTh[pi][:, co:co + 128], in_=ps),
                        reads=[P.bank[b]], writes=[xT_hb[pi]])
                    vec.op(lambda e, ps=ps, co=co, pi=pi: e.tensor_tensor(
                        out=xTl[pi][:, co:co + 128], in0=ps,
                        in1=xTh[pi][:, co:co + 128], op=ALU.subtract),
                        reads=[P.bank[b], xT_hb[pi]], writes=[xT_lb[pi]])
        for wname, oh, ol, ob_h, ob_l, of32, ob_f in (
                ("q", qh, ql, qh_b, ql_b, None, None),
                ("k", kh, kl, kh_b, kl_b, None, None),
                ("v", None, None, None, None, vv, vv_b)):
            whi, wlo, whb, wlb = wts[wname]
            for m in range(4):
                b = proj_bank()
                fns = []
                for kk in range(KCH):
                    fns += mm3(P.psum[b][:, :],
                               whi[:, 512 * kk + 128 * m: 512 * kk + 128 * m + 128],
                               wlo[:, 512 * kk + 128 * m: 512 * kk + 128 * m + 128],
                               xTh[pi][:, 512 * kk:512 * kk + 512],
                               xTl[pi][:, 512 * kk:512 * kk + 512],
                               start=(kk == 0), stop=(kk == KCH - 1))
                ten.group(fns, reads=[whb, wlb, xT_hb[pi], xT_lb[pi]], writes=[P.bank[b]])
                if of32 is None:
                    split_psum(P, P.psum[b][:, :], P.bank[b],
                               oh[pi][:, 512 * m:512 * m + 512],
                               ol[pi][:, 512 * m:512 * m + 512], ob_h[pi], ob_l[pi])
                else:
                    act.op(lambda e, b=b, m=m: e.copy(
                        out=of32[pi][:, 512 * m:512 * m + 512], in_=P.psum[b][:, :]),
                        reads=[P.bank[b]], writes=[ob_f[pi]])
        b = proj_bank()
        fns = []
        for kk in range(KCH):
            fns += mm3(P.psum[b][0:16, :],
                       c.wgk1h[:, 16 * kk:16 * kk + 16], c.wgk1l[:, 16 * kk:16 * kk + 16],
                       xTh[pi][:, 512 * kk:512 * kk + 512], xTl[pi][:, 512 * kk:512 * kk + 512],
                       start=(kk == 0), stop=(kk == KCH - 1))
        ten.group(fns, reads=[c.wgk1h_b, c.wgk1l_b, xT_hb[pi], xT_lb[pi], c.cb],
                  writes=[P.bank[b]])
        split_psum(P, P.psum[b][0:16, :], P.bank[b], z1h[pi], z1l[pi], z1h_b[pi], z1l_b[pi])
        for m in range(4):
            b = proj_bank()
            ten.group(mm3(P.psum[b][:, :],
                          c.wgk2h[:, 128 * m:128 * m + 128], c.wgk2l[:, 128 * m:128 * m + 128],
                          z1h[pi], z1l[pi]),
                      reads=[c.wgk2h_b, c.wgk2l_b, z1h_b[pi], z1l_b[pi], c.cb],
                      writes=[P.bank[b]])
            act.op(lambda e, b=b, m=m: e.activation(
                out=sgt[pi], in_=P.psum[b][:, :], func=AF.Sigmoid,
                bias=c.bgk2[:, m:m + 1], scale=1.0),
                reads=[P.bank[b], c.cb], writes=[sgt_b[pi]])
            act.op(lambda e, m=m: e.activation(
                out=gg[pi][:, 512 * m:512 * m + 512], in_=sgt[pi], func=AF.Ln),
                reads=[sgt_b[pi]], writes=[gg_b[pi]])

    def scan_chunk(cg, hp):
        sc = cg // 4
        pi = sc % 2
        pz = cg % 2
        po = 1 - pz
        sl = (cg % 4) * 128
        o = 512 * hp + sl
        q_hi = qh[pi][:, o:o + 128]; q_lo = ql[pi][:, o:o + 128]
        k_hi = kh[pi][:, o:o + 128]; k_lo = kl[pi][:, o:o + 128]
        v_f = vv[pi][:, o:o + 128]
        g_f = gg[pi][:, o:o + 128]
        bT = P.bank[2 + pz]
        psT = P.psum[2 + pz]
        bPa, bPb = P.bank[4], P.bank[5]
        bO = P.bank[6]
        vec.op(lambda e: e.tensor_tensor_scan(out=A[pz], data0=g_f, data1=g_f,
                                              initial=0.0, op0=ALU.add, op1=ALU.bypass),
               reads=[gg_b[pi]], writes=[A_b[pz]])
        act.op(lambda e: e.activation(out=eNA[pz], in_=A[pz], func=AF.Exp, scale=-1.0 / 16),
               reads=[A_b[pz]], writes=[eNA_b[pz]])
        act.op(lambda e: e.activation(out=eacc[pz], in_=A[pz][:, 127:128], func=AF.Exp,
                                      scale=1.0 / 16),
               reads=[A_b[pz]], writes=[eacc_b[pz]])
        gps.op(lambda e: e.tensor_tensor(out=Vt[pz], in0=v_f, in1=eNA[pz], op=ALU.mult),
               reads=[vv_b[pi], eNA_b[pz]], writes=[Vt_b[pz]])
        gps.op(lambda e: e.tensor_scalar(out=Wc[pz], in0=Vt[pz], scalar1=eacc[pz],
                                         scalar2=None, op0=ALU.mult),
               reads=[Vt_b[pz], eacc_b[pz]], writes=[Wc_b[pz]])
        ten.op(lambda e: e.transpose(psT[:, 0:128], A[pz], c.idf),
               reads=[A_b[pz], c.cb], writes=[P.bank[2 + pz]])
        ten.op(lambda e: e.transpose(psT[0:1, 384:512], A[pz][:, 127:128], c.idf),
               reads=[A_b[pz], c.cb], writes=[P.bank[2 + pz]])
        ten.op(lambda e: e.transpose(psT[:, 128:256], Vt[pz], c.idf),
               reads=[Vt_b[pz], c.cb], writes=[P.bank[2 + pz]])
        ten.op(lambda e: e.transpose(psT[:, 256:384], Wc[pz], c.idf),
               reads=[Wc_b[pz], c.cb], writes=[P.bank[2 + pz]])
        act.op(lambda e: e.activation(out=eATM[pz], in_=psT[:, 0:128], func=AF.Exp,
                                      scale=1.0 / 16),
               reads=[P.bank[2 + pz]], writes=[eATM_b[pz]])
        act.op(lambda e: e.activation(out=eacr[pz], in_=psT[0:1, 384:512], func=AF.Exp,
                                      scale=1.0 / 16),
               reads=[P.bank[2 + pz]], writes=[eacr_b[pz]])
        split_sbuf(P, eacr[pz], eacr_b[pz], eacrh[pz], eacrl[pz], eacrh_b[pz], eacrl_b[pz])
        split_psum(P, psT[:, 128:256], P.bank[2 + pz], VTh[pz], VTl[pz], VTh_b[pz], VTl_b[pz])
        split_psum(P, psT[:, 256:384], P.bank[2 + pz], WTh[pz], WTl[pz], WTh_b[pz], WTl_b[pz])
        cb7 = 512 * pz
        ten.op(lambda e: e.transpose(P.psum_bf[:, cb7:cb7 + 128], k_hi, c.idb),
               reads=[kh_b[pi], c.cb], writes=[P.bank[7]])
        ten.op(lambda e: e.transpose(P.psum_bf[:, cb7 + 128:cb7 + 256], k_lo, c.idb),
               reads=[kl_b[pi], c.cb], writes=[P.bank[7]])
        act.op(lambda e: e.copy(out=kT2h[pz], in_=P.psum_bf[:, cb7:cb7 + 128]),
               reads=[P.bank[7]], writes=[kT2h_b[pz]])
        vec.op(lambda e: e.tensor_copy(kT2l[pz], P.psum_bf[:, cb7 + 128:cb7 + 256]),
               reads=[P.bank[7]], writes=[kT2l_b[pz]])
        ten.group(mm3(P.psum[4][:, 0:128], k_hi[0:64, :], k_lo[0:64, :],
                      q_hi[0:64, :], q_lo[0:64, :], tp=(0, 0)),
                  reads=[kh_b[pi], kl_b[pi], qh_b[pi], ql_b[pi]], writes=[bPa])
        ten.group(mm3(P.psum[5][:, 0:128], k_hi[64:128, :], k_lo[64:128, :],
                      q_hi[64:128, :], q_lo[64:128, :], tp=(64, 0)),
                  reads=[kh_b[pi], kl_b[pi], qh_b[pi], ql_b[pi]], writes=[bPb])
        vec.op(lambda e: e.tensor_tensor(out=PTa[pz], in0=P.psum[4][:, 0:128],
                                         in1=c.triu, op=ALU.mult),
               reads=[bPa, c.cb], writes=[PTa_b[pz]])
        vec.op(lambda e: e.tensor_tensor(out=PTb[pz], in0=P.psum[5][:, 0:128],
                                         in1=c.triu, op=ALU.mult),
               reads=[bPb, c.cb], writes=[PTb_b[pz]])
        split_sbuf(P, PTa[pz], PTa_b[pz], PTah[pz], PTal[pz], PTah_b[pz], PTal_b[pz],
                   lo_eng=P.gpsimd)
        split_sbuf(P, PTb[pz], PTb_b[pz], PTbh[pz], PTbl[pz], PTbh_b[pz], PTbl_b[pz],
                   lo_eng=P.gpsimd)
        fns = mm3(P.psum[6][:, 0:64], PTah[pz], PTal[pz], VTh[pz][:, 0:64], VTl[pz][:, 0:64])
        fns += mm3(P.psum[6][:, 64:128], PTbh[pz], PTbl[pz], VTh[pz][:, 64:128], VTl[pz][:, 64:128])
        ten.group(fns, reads=[PTah_b[pz], PTal_b[pz], PTbh_b[pz], PTbl_b[pz],
                              VTh_b[pz], VTl_b[pz]], writes=[bO])
        ten.group(mm3(P.psum[4][:, 0:64], q_hi[0:64, :], q_lo[0:64, :],
                      STh[hp][po][0:64, :], STl[hp][po][0:64, :], tp=(0, 0)),
                  reads=[qh_b[pi], ql_b[pi], STh_b[hp][po], STl_b[hp][po]], writes=[bPa])
        ten.group(mm3(P.psum[5][:, 0:64], q_hi[64:128, :], q_lo[64:128, :],
                      STh[hp][po][64:128, :], STl[hp][po][64:128, :], tp=(64, 0)),
                  reads=[qh_b[pi], ql_b[pi], STh_b[hp][po], STl_b[hp][po]], writes=[bPb])
        fns = mm3(P.psum[6][0:64, 128:192], kT2h[pz][:, 0:64], kT2l[pz][:, 0:64],
                  WTh[pz][:, 0:64], WTl[pz][:, 0:64], tp=(0, 0))
        fns += mm3(P.psum[6][64:128, 128:192], kT2h[pz][:, 64:128], kT2l[pz][:, 64:128],
                   WTh[pz][:, 64:128], WTl[pz][:, 64:128], tp=(0, 64))
        ten.group(fns, reads=[kT2h_b[pz], kT2l_b[pz], WTh_b[pz], WTl_b[pz]], writes=[bO])
        fns = [
            lambda e: e.matmul(P.psum[6][0:64, 192:256], c.ones[0:1, 0:64],
                               eacrh[pz][0:1, 0:64], start=True, stop=False, tile_position=(0, 0)),
            lambda e: e.matmul(P.psum[6][0:64, 192:256], c.ones[0:1, 0:64],
                               eacrl[pz][0:1, 0:64], start=False, stop=True, tile_position=(0, 0)),
            lambda e: e.matmul(P.psum[6][64:128, 192:256], c.ones[0:1, 64:128],
                               eacrh[pz][0:1, 64:128], start=True, stop=False, tile_position=(0, 64)),
            lambda e: e.matmul(P.psum[6][64:128, 192:256], c.ones[0:1, 64:128],
                               eacrl[pz][0:1, 64:128], start=False, stop=True, tile_position=(0, 64)),
        ]
        ten.group(fns, reads=[eacrh_b[pz], eacrl_b[pz], c.cb], writes=[bO])
        vec.op(lambda e: e.tensor_tensor(out=tmpS[pz], in0=STf[hp][po],
                                         in1=P.psum[6][:, 192:256], op=ALU.mult),
               reads=[STf_b[hp][po], bO], writes=[tmpS_b[pz]])
        vec.op(lambda e: e.tensor_tensor(out=STf[hp][pz], in0=tmpS[pz],
                                         in1=P.psum[6][:, 128:192], op=ALU.add),
               reads=[tmpS_b[pz], bO], writes=[STf_b[hp][pz]])
        split_sbuf(P, STf[hp][pz], STf_b[hp][pz], STh[hp][pz], STl[hp][pz],
                   STh_b[hp][pz], STl_b[hp][pz], lo_eng=P.gpsimd)
        slot = cg % 2
        act.op(lambda e: e.copy(out=tmpI[pz][:, 0:64], in_=P.psum[4][:, 0:64]),
               reads=[bPa], writes=[tmpI_b[pz]])
        act.op(lambda e: e.copy(out=tmpI[pz][:, 64:128], in_=P.psum[5][:, 0:64]),
               reads=[bPb], writes=[tmpI_b[pz]])
        vec.op(lambda e: e.tensor_tensor(out=tmpO[pz], in0=P.psum[6][:, 0:128],
                                         in1=tmpI[pz], op=ALU.add),
               reads=[bO, tmpI_b[pz]], writes=[tmpO_b[pz]])
        vec.op(lambda e, slot=slot, hp=hp: e.tensor_tensor(
            out=out_asm[slot][:, 128 * hp:128 * hp + 128], in0=tmpO[pz],
            in1=eATM[pz], op=ALU.mult),
            reads=[tmpO_b[pz], eATM_b[pz]], writes=[oa_buf[slot]])

    emit_proj(0)
    for sc in range(nsc):
        if sc + 1 < nsc:
            emit_proj(sc + 1)
        for c4 in range(4):
            cg = sc * 4 + c4
            for hp in range(4):
                scan_chunk(cg, hp)
            slot = cg % 2
            gps.op(lambda e, cg=cg, slot=slot: e.dma_start(
                out=d.cc_in[cg * 128:cg * 128 + 128, :], in_=out_asm[slot]),
                reads=[oa_buf[slot]], sem=oa_sem[slot], inc=16)
            cc_tot[0] += 1
            if npc and cc_tot[0] % (d.psz // 128) == 0:
                pc = cc_tot[0] // (d.psz // 128) - 1
                gps.barrier_wait([(oa_sem[i], oa_sem[i].count) for i in range(2)])
                gps.op(lambda e, pc=pc: e.collective_compute(
                    "AllGather", ALU.bypass,
                    ins=[d.cc_in[pc * d.psz:(pc + 1) * d.psz, :]],
                    outs=[d.cc_out[pc][:, :]],
                    replica_groups=[[0, 1], [2, 3], [4, 5], [6, 7]]),
                    writes=[ccout_bufs[pc]], sem=cc_sems[pc], inc=1)

    return NS(oa_sem=oa_sem, n_even=(cc_tot[0] + 1) // 2, n_odd=cc_tot[0] // 2,
              stg=stg, stg_sem=stg_sem, stg_buf=stg_buf, ccout_bufs=ccout_bufs)

def emit_ag(P, d, cfg, at):
    cc_sem = P.new_sem("cc")
    P.gpsimd.barrier_wait([(at.oa_sem[i], at.oa_sem[i].count) for i in range(2)])
    ccout_buf = Buf("ccout")
    P.gpsimd.op(lambda e: e.collective_compute(
        "AllGather", ALU.bypass, ins=[d.cc_in[:, :]], outs=[d.cc_out[:, :]],
        replica_groups=[[0, 1], [2, 3], [4, 5], [6, 7]]),
        writes=[ccout_buf], sem=cc_sem, inc=1)
    at.ccout_buf = ccout_buf
    at.cc_sem = cc_sem
    return at


def build(cfg):
    nc = bass.Bass("TRN2", target_bir_lowering=False)
    d = declare_drams(nc, cfg)
    es = contextlib.ExitStack()
    with es:
        P = Prog(nc, es, f32_cols=cfg.f32_cols, bf16_cols=cfg.bf16_cols)
        c = emit_consts(P, d, cfg)
        P.af.mark_perm()
        P.ab.mark_perm()
        at = emit_attn(P, d, c, cfg)
        if cfg.upto == "attn":
            P.gpsimd.barrier_wait([(at.oa_sem[i], at.oa_sem[i].count) for i in range(2)])
        else:
            wst_sem = emit_ffn_split(P, d, c, cfg, at)
            P.barrier(extra=[at.oa_sem[0], at.oa_sem[1], wst_sem[0], wst_sem[1],
                             c.x1g_pre])
            P.af.reset()
            P.ab.reset()
            emit_post(P, d, c, cfg, at)
            if cfg.upto == "post":
                P.barrier(extra=at.post_sems)
            else:
                P.barrier(extra=at.post_sems)
                P.af.reset()
                P.ab.reset()
                emit_ffn(P, d, c, cfg, at)
        P.finalize()
    return nc


class Cfg:
    def __init__(self, **kw):
        self.L = 8192
        self.upto = "full"
        self.dbg = False
        self.f32_cols = 22000
        self.bf16_cols = 60800
        self.__dict__.update(kw)


def make_inputs(core, inp, cfg):
    p, h = core // 2, core % 2
    L = cfg.L
    f32 = np.float32
    x = np.ascontiguousarray(inp["L0_attn_x"][p, :L], dtype=f32)
    cs = slice(512 * h, 512 * h + 512)
    hm = 1.0 if h == 0 else 0.0
    chm = np.zeros((128, 2), np.float32)
    chm[:, 0] = hm
    chm[:, 1] = 1.0 - hm
    m = {
        "x_seq": x,
        "x_own": np.ascontiguousarray(x[(L // 2) * h:(L // 2) * h + L // 2]),
        "c_hm": chm,
        "wq": np.ascontiguousarray(inp["L0_attn_Wq"][:, cs]),
        "wk": np.ascontiguousarray(inp["L0_attn_Wk"][:, cs]),
        "wv": np.ascontiguousarray(inp["L0_attn_Wv"][:, cs]),
        "wgk1": np.ascontiguousarray(inp["L0_attn_Wgk1"]),
        "wgk2": np.ascontiguousarray(inp["L0_attn_Wgk2"][:, cs]),
        "bgk2t": np.ascontiguousarray(inp["L0_attn_bgk2"][cs].reshape(4, 128).T),
        "wgp": np.ascontiguousarray(inp["L0_attn_Wg"]),
        "wop": np.ascontiguousarray(inp["L0_attn_Wout"]),
        "router": np.ascontiguousarray(inp["L0_ffn_router_weight"]),
        "gate_up": np.ascontiguousarray(inp["L0_ffn_gate_up_proj"]),
        "down": np.ascontiguousarray(inp["L0_ffn_down_proj"]),
        "dbias": np.ascontiguousarray(inp["L0_ffn_down_proj_bias"]),
        "c_idf": np.eye(128, dtype=f32),
        "c_idb": np.eye(128, dtype=BF),
        "c_triu": np.triu(np.ones((128, 128), dtype=f32)),
        "c_ones": np.ones((1, 128), dtype=BF),
        "c_eps": np.full((128, 1), 1e-5, dtype=f32),
        "c_gamma": np.tile(inp["L0_attn_gamma"][None, :], (128, 1)).astype(f32),
        "c_beta": np.tile(inp["L0_attn_beta"][None, :], (128, 1)).astype(f32),
    }
    sel = np.zeros((16, 2048), dtype=BF)
    for e in range(16):
        sel[e, e * 128:(e + 1) * 128] = 1
    m["c_sel"] = sel
    m["c_iota16"] = np.tile(np.arange(16, dtype=f32)[None, :], (128, 1))
    l17 = np.zeros((17, 17), dtype=f32)
    for jj in range(16):
        l17[:jj, jj] = 1.0
        l17[16, jj] = 1.0
        l17[jj, 16] = 1.0
    m["c_l17"] = l17
    TOWN_ = L // 2
    CAP_ = 768 if TOWN_ >= 2048 else TOWN_
    m["c_cap16"] = (np.arange(16, dtype=f32) * CAP_).reshape(16, 1)
    gub = inp["L0_ffn_gate_up_proj_bias"]
    gb = gub[:, 0::2].reshape(16, 4, 128)
    ub = gub[:, 1::2].reshape(16, 4, 128)
    m["gubias"] = np.concatenate([gb, ub], axis=1).transpose(2, 0, 1).reshape(128, 128).copy()
    return m


_BUILD_CACHE = {}


def kernel(**inputs):
    cfg = Cfg()
    key = ("full", cfg.L)
    if key not in _BUILD_CACHE:
        _BUILD_CACHE[key] = build(cfg)
    nc = _BUILD_CACHE[key]
    in_maps = [make_inputs(cc, inputs, cfg) for cc in range(8)]
    res = run_bass_kernel_spmd(nc, in_maps, core_ids=list(range(8)))
    TOWN = cfg.L // 2
    out = np.empty((4, 8192, 1024), dtype=np.float32)
    for cc in range(8):
        p, h = cc // 2, cc % 2
        out[p, TOWN * h:TOWN * h + TOWN] = res.results[cc]["y"]
    return out

def emit_post(P, d, c, cfg, at):
    L = cfg.L
    TOWN = L // 2
    ntile = TOWN // 128
    ten, vec, act, gps, syn, scl = P.tensor, P.vector, P.scalar, P.gpsimd, P.sync, P.scalar
    c.scTres = P.bf16(TOWN, parts=16); c.scTres_b = Buf("scTres")
    scTl = [P.bf16(128, parts=16) for _ in range(2)]
    scTl_b = [Buf("scTl%d" % i) for i in range(2)]
    P.af.mark_perm(); P.ab.mark_perm()
    sct_st = P.new_sem("sct_st")
    sct_st2 = [P.new_sem("sct2_%d" % i) for i in range(2)]
    c.gamma = P.f32(HID)
    c.beta = P.f32(HID)
    gb_sem = P.new_sem("gb_ld")
    gb_buf = Buf("gb")
    syn.op(lambda e: e.dma_start(out=c.gamma, in_=d.c_gamma[:, :]), sem=gb_sem, inc=16,
           writes=[gb_buf])
    syn.op(lambda e: e.dma_start(out=c.beta, in_=d.c_beta[:, :]), sem=gb_sem, inc=16,
           writes=[gb_buf])
    c.gb_buf = gb_buf
    weq_st = [P.new_sem("weq_st%d" % i) for i in range(2)]
    x1tm_st = [P.new_sem("x1tm_st%d" % i) for i in range(2)]
    TOWN_ = TOWN
    CAPp = 768 if TOWN_ >= 2048 else TOWN_
    carry = P.f32(1, parts=16); carry_b = Buf()
    x1sc_sem = [[P.new_sem("x1sc%d_%d" % (i, j)) for j in range(2)] for i in range(2)]
    vec.op(lambda e: e.memset(carry, 0.0), writes=[carry_b])

    att0 = [P.f32(HID) for _ in range(2)]
    att1 = [P.f32(HID) for _ in range(2)]
    att0_sem = [P.new_sem("at0%d" % i) for i in range(2)]
    att1_sem = [P.new_sem("at1%d" % i) for i in range(2)]
    att0_buf = [Buf() for _ in range(2)]
    att1_buf = [Buf() for _ in range(2)]
    cc2 = [co.rearrange("(s r) c -> r s c", s=2) for co in d.cc_out]
    xt2 = [P.f32(HID) for _ in range(2)]
    xt2_sem = [P.new_sem("xt2_%d" % i) for i in range(2)]
    xt2_buf = [Buf() for _ in range(2)]

    stg, stg_sem, stg_buf = xt2, xt2_sem, xt2_buf
    wgh = P.bf16(KCH * HID); wgl = P.bf16(KCH * HID)
    woh = P.bf16(KCH * HID); wol = P.bf16(KCH * HID)
    wgh_b, wgl_b, woh_b, wol_b = Buf(), Buf(), Buf(), Buf()
    for wi, (src, hi, lo, hb, lb) in enumerate(
            ((d.wgp, wgh, wgl, wgh_b, wgl_b), (d.wop, woh, wol, woh_b, wol_b))):
        for k in range(KCH):
            s = (wi * KCH + k) % 2
            syn.op(lambda e, k=k, s=s, src=src: e.dma_start(
                out=stg[s], in_=src[128 * k:128 * (k + 1), :]),
                writes=[stg_buf[s]], sem=stg_sem[s], inc=16)
            act.op(lambda e, k=k, s=s, hi=hi: e.copy(
                out=hi[:, HID * k:HID * (k + 1)], in_=stg[s]),
                reads=[stg_buf[s]], writes=[hb])
            gps.op(lambda e, k=k, s=s, hi=hi, lo=lo: e.tensor_tensor(
                out=lo[:, HID * k:HID * (k + 1)], in0=stg[s],
                in1=hi[:, HID * k:HID * (k + 1)], op=ALU.subtract),
                reads=[stg_buf[s], hb], writes=[lb])

    def mkslot(i):
        S = NS()
        S.st6 = P.f32(12); S.st6_b = Buf()
        S.mv = P.f32(2); S.mv_b = Buf()
        S.sd = P.f32(1); S.sd_b = Buf()
        S.istd = P.f32(1); S.istd_b = Buf()
        S.attg = P.f32(HID); S.attg_b = Buf()
        S.sg = P.f32(HID); S.sg_b = Buf()
        S.gated = P.f32(HID); S.gated_b = Buf()
        S.rr = P.f32(HID); S.rr_b = Buf()
        S.x1 = P.f32(HID); S.x1_b = Buf()
        S.xTh2 = P.bf16(HID); S.xTh2_b = Buf()
        S.xTl2 = P.bf16(HID); S.xTl2_b = Buf()
        S.gTh = P.bf16(HID); S.gTh_b = Buf()
        S.gTl = P.bf16(HID); S.gTl_b = Buf()
        S.x1gh = P.bf16(HID); S.x1gh_b = Buf()
        S.x1gl = P.bf16(HID); S.x1gl_b = Buf()
        S.lg = P.f32(16); S.lg_b = Buf()
        S.m1 = P.f32(1); S.m1_b = Buf()
        S.m2 = P.f32(1); S.m2_b = Buf()
        S.eq1 = P.f32(16); S.eq1_b = Buf()
        S.eq2 = P.f32(16); S.eq2_b = Buf()
        S.mskd = P.f32(16); S.mskd_b = Buf()
        S.dd = P.f32(1); S.dd_b = Buf()
        S.w1 = P.f32(1); S.w1_b = Buf()
        S.w2 = P.f32(1); S.w2_b = Buf()
        S.sct = P.f32(16); S.sct_b = Buf()
        S.scores = P.f32(16); S.scores_b = Buf()
        S.maskp = P.f32(128, parts=16); S.maskp_b = Buf()
        S.cump = P.f32(128, parts=16); S.cump_b = Buf()
        S.slotm = P.f32(128, parts=16); S.slotm_b = Buf()
        S.slotT = P.f32(16); S.slotT_b = Buf()
        S.t16a = P.f32(16); S.t16a_b = Buf()
        S.f1p = P.f32(1); S.f1p_b = Buf()
        S.f2p = P.f32(1); S.f2p_b = Buf()
        S.if1p = P.sb("if1p_%d" % i, [128, 1], mybir.dt.int32); S.if1p_b = Buf()
        S.if2p = P.sb("if2p_%d" % i, [128, 1], mybir.dt.int32); S.if2p_b = Buf()
        S.weq = P.f32(4); S.weq_b = Buf()
        S.scTl = scTl[i]; S.scTl_b = scTl_b[i]
        return S

    slots = [mkslot(0), mkslot(1)]
    dbg_sem = [P.new_sem("dbgp%d" % i) for i in range(2)]
    dbg_sem2 = [P.new_sem("dbgq%d" % i) for i in range(2)]

    pk = [0]

    def bankn():
        b = pk[0] % 6
        pk[0] += 1
        return b

    def trsplit(src_ap, src_b, hi, lo, hi_b, lo_b):
        for half in range(2):
            b = bankn()
            fns = []
            for k in range(4 * half, 4 * half + 4):
                fns.append(lambda e, k=k, b=b: e.transpose(
                    P.psum[b][:, (k % 4) * 128:(k % 4) * 128 + 128],
                    src_ap[:, 128 * k:128 * k + 128], c.idf))
            ten.group(fns, reads=[src_b, c.cb], writes=[P.bank[b]])
            for k in range(4 * half, 4 * half + 4):
                ps = P.psum[b][:, (k % 4) * 128:(k % 4) * 128 + 128]
                act.op(lambda e, ps=ps, k=k: e.copy(out=hi[:, 128 * k:128 * k + 128], in_=ps),
                       reads=[P.bank[b]], writes=[hi_b])
                vec.op(lambda e, ps=ps, k=k: e.tensor_tensor(
                    out=lo[:, 128 * k:128 * k + 128], in0=ps,
                    in1=hi[:, 128 * k:128 * k + 128], op=ALU.subtract),
                    reads=[P.bank[b], hi_b], writes=[lo_b])

    def layernorm(S, src_ap, src_b, out_ap, out_b, gamma_beta):
        vec.op(lambda e: e.bn_stats(out=S.st6[:, 0:6], in_=src_ap[:, 0:512]),
               reads=[src_b], writes=[S.st6_b])
        vec.op(lambda e: e.bn_stats(out=S.st6[:, 6:12], in_=src_ap[:, 512:1024]),
               reads=[src_b], writes=[S.st6_b])
        vec.op(lambda e: e.bn_aggr(out=S.mv, in_=S.st6), reads=[S.st6_b], writes=[S.mv_b])
        act.op(lambda e: e.activation(out=S.sd, in_=S.mv[:, 1:2], func=AF.Sqrt,
                                      bias=c.eps[:, 0:1]),
               reads=[S.mv_b, c.cb], writes=[S.sd_b])
        vec.op(lambda e: e.reciprocal(out=S.istd, in_=S.sd),
               reads=[S.sd_b], writes=[S.istd_b])
        vec.op(lambda e: e.tensor_scalar(
            out=out_ap, in0=src_ap, scalar1=S.mv[:, 0:1], scalar2=S.istd[:, 0:1],
            op0=ALU.subtract, op1=ALU.mult),
            reads=[src_b, S.mv_b, S.istd_b], writes=[out_b])
        if gamma_beta:
            gps.op(lambda e: e.tensor_tensor(out=out_ap, in0=out_ap, in1=c.gamma,
                                             op=ALU.mult),
                   reads=[out_b, c.gb_buf], writes=[out_b])
            vec.op(lambda e: e.tensor_tensor(out=out_ap, in0=out_ap, in1=c.beta,
                                             op=ALU.add),
                   reads=[out_b, c.gb_buf], writes=[out_b])

    def tile_body(t, pi, S):
        g0, r0 = (t * 128) // d.psz, (t * 128) % d.psz
        g1, r1 = (TOWN + t * 128) // d.psz, (TOWN + t * 128) % d.psz
        syn.op(lambda e, g0=g0, r0=r0, pi=pi: e.dma_start(
            out=att0[pi], in_=cc2[g0][r0:r0 + 128, :, :]),
            reads=[at.ccout_bufs[g0]], writes=[att0_buf[pi]], sem=att0_sem[pi], inc=16)
        syn.op(lambda e, g1=g1, r1=r1, pi=pi: e.dma_start(
            out=att1[pi], in_=cc2[g1][r1:r1 + 128, :, :]),
            reads=[at.ccout_bufs[g1]], writes=[att1_buf[pi]], sem=att1_sem[pi], inc=16)
        syn.op(lambda e, t=t, pi=pi: e.dma_start(
            out=xt2[pi], in_=d.x_own[t * 128:t * 128 + 128, :]),
            writes=[xt2_buf[pi]], sem=xt2_sem[pi], inc=16)
        vec.op(lambda e, pi=pi: e.tensor_scalar(
            out=att0[pi], in0=att0[pi], scalar1=c.hm[:, 0:1], scalar2=None, op0=ALU.mult),
            reads=[att0_buf[pi], c.cb], writes=[att0_buf[pi]])
        vec.op(lambda e, pi=pi: e.scalar_tensor_tensor(
            out=att0[pi], in0=att1[pi], scalar=c.hm[:, 1:2], in1=att0[pi],
            op0=ALU.mult, op1=ALU.add),
            reads=[att1_buf[pi], att0_buf[pi], c.cb], writes=[att0_buf[pi]])
        layernorm(S, att0[pi], att0_buf[pi], S.attg, S.attg_b, True)
        trsplit(xt2[pi], xt2_buf[pi], S.xTh2, S.xTl2, S.xTh2_b, S.xTl2_b)
        for half in range(2):
            b = bankn()
            fns = []
            for k in range(KCH):
                fns += mm3(P.psum[b][:, :],
                           S.xTh2[:, 128 * k:128 * k + 128],
                           S.xTl2[:, 128 * k:128 * k + 128],
                           wgh[:, HID * k + 512 * half: HID * k + 512 * half + 512],
                           wgl[:, HID * k + 512 * half: HID * k + 512 * half + 512],
                           start=(k == 0), stop=(k == KCH - 1))
            ten.group(fns, reads=[S.xTh2_b, S.xTl2_b, wgh_b, wgl_b], writes=[P.bank[b]])
            act.op(lambda e, b=b, half=half: e.activation(
                out=S.sg[:, 512 * half:512 * half + 512], in_=P.psum[b][:, :], func=AF.Silu),
                reads=[P.bank[b]], writes=[S.sg_b])
        vec.op(lambda e: e.tensor_tensor(out=S.gated, in0=S.attg, in1=S.sg, op=ALU.mult),
               reads=[S.attg_b, S.sg_b], writes=[S.gated_b])
        trsplit(S.gated, S.gated_b, S.gTh, S.gTl, S.gTh_b, S.gTl_b)
        for half in range(2):
            b = bankn()
            fns = []
            for k in range(KCH):
                fns += mm3(P.psum[b][:, :],
                           S.gTh[:, 128 * k:128 * k + 128],
                           S.gTl[:, 128 * k:128 * k + 128],
                           woh[:, HID * k + 512 * half: HID * k + 512 * half + 512],
                           wol[:, HID * k + 512 * half: HID * k + 512 * half + 512],
                           start=(k == 0), stop=(k == KCH - 1))
            ten.group(fns, reads=[S.gTh_b, S.gTl_b, woh_b, wol_b], writes=[P.bank[b]])
            vec.op(lambda e, b=b, half=half, pi=pi: e.tensor_tensor(
                out=S.rr[:, 512 * half:512 * half + 512], in0=P.psum[b][:, :],
                in1=xt2[pi][:, 512 * half:512 * half + 512], op=ALU.add),
                reads=[P.bank[b], xt2_buf[pi]], writes=[S.rr_b])
        layernorm(S, S.rr, S.rr_b, S.x1, S.x1_b, False)
        for half in range(2):
            b = bankn()
            fns = []
            for k in range(4 * half, 4 * half + 4):
                fns.append(lambda e, k=k, b=b: e.transpose(
                    P.psum[b][:, (k % 4) * 128:(k % 4) * 128 + 128],
                    S.x1[:, 128 * k:128 * k + 128], c.idf))
            ten.group(fns, reads=[S.x1_b, c.cb], writes=[P.bank[b]])
            for k in range(4 * half, 4 * half + 4):
                ps = P.psum[b][:, (k % 4) * 128:(k % 4) * 128 + 128]
                co = 128 * k
                act.op(lambda e, ps=ps, co=co: e.copy(out=S.x1gh[:, co:co + 128], in_=ps),
                       reads=[P.bank[b]], writes=[S.x1gh_b])
                vec.op(lambda e, ps=ps, co=co: e.tensor_tensor(
                    out=S.x1gl[:, co:co + 128], in0=ps, in1=S.x1gh[:, co:co + 128],
                    op=ALU.subtract),
                    reads=[P.bank[b], S.x1gh_b], writes=[S.x1gl_b])
        b = bankn()
        fns = []
        for k in range(KCH):
            fns += mm3(P.psum[b][:, 0:16],
                       S.x1gh[:, 128 * k:128 * k + 128],
                       S.x1gl[:, 128 * k:128 * k + 128],
                       c.rth[:, 16 * k:16 * k + 16], c.rtl[:, 16 * k:16 * k + 16],
                       start=(k == 0), stop=(k == KCH - 1))
        ten.group(fns, reads=[S.x1gh_b, S.x1gl_b, c.rth_b, c.rtl_b, c.cb],
                  writes=[P.bank[b]])
        vec.op(lambda e, b=b: e.tensor_copy(S.lg, P.psum[b][:, 0:16]),
               reads=[P.bank[b]], writes=[S.lg_b])
        vec.op(lambda e: e.reduce_max(out=S.m1, in_=S.lg, axis=AX.X),
               reads=[S.lg_b], writes=[S.m1_b])
        vec.op(lambda e: e.tensor_scalar(out=S.eq1, in0=S.lg, scalar1=S.m1[:, 0:1],
                                         scalar2=None, op0=ALU.is_equal),
               reads=[S.lg_b, S.m1_b], writes=[S.eq1_b])
        vec.op(lambda e: e.scalar_tensor_tensor(out=S.mskd, in0=S.eq1, scalar=-1e30,
                                                in1=S.lg, op0=ALU.mult, op1=ALU.add),
               reads=[S.eq1_b, S.lg_b], writes=[S.mskd_b])
        vec.op(lambda e: e.reduce_max(out=S.m2, in_=S.mskd, axis=AX.X),
               reads=[S.mskd_b], writes=[S.m2_b])
        vec.op(lambda e: e.tensor_scalar(out=S.eq2, in0=S.mskd, scalar1=S.m2[:, 0:1],
                                         scalar2=None, op0=ALU.is_equal),
               reads=[S.mskd_b, S.m2_b], writes=[S.eq2_b])
        vec.op(lambda e: e.tensor_tensor(out=S.dd, in0=S.m1, in1=S.m2, op=ALU.subtract),
               reads=[S.m1_b, S.m2_b], writes=[S.dd_b])
        act.op(lambda e: e.activation(out=S.w1, in_=S.dd, func=AF.Sigmoid),
               reads=[S.dd_b], writes=[S.w1_b])
        act.op(lambda e: e.activation(out=S.w2, in_=S.dd, func=AF.Sigmoid, scale=-1.0),
               reads=[S.dd_b], writes=[S.w2_b])
        vec.op(lambda e: e.tensor_scalar(out=S.sct, in0=S.eq1, scalar1=S.w1[:, 0:1],
                                         scalar2=None, op0=ALU.mult),
               reads=[S.eq1_b, S.w1_b], writes=[S.sct_b])
        vec.op(lambda e: e.scalar_tensor_tensor(out=S.scores, in0=S.eq2,
                                                scalar=S.w2[:, 0:1],
                                                in1=S.sct, op0=ALU.mult, op1=ALU.add),
               reads=[S.eq2_b, S.w2_b, S.sct_b], writes=[S.scores_b])
        b = bankn()
        ten.op(lambda e, b=b: e.transpose(P.psum[b][0:16, 0:128], S.scores, c.idf),
               reads=[S.scores_b, c.cb], writes=[P.bank[b]])
        split_psum(P, P.psum[b][0:16, 0:128], P.bank[b],
                   c.scTres[:, t * 128:t * 128 + 128], S.scTl,
                   c.scTres_b, S.scTl_b)
        scl.op(lambda e, t=t: e.dma_start(out=d.scth_sp[:, t * 128:t * 128 + 128],
                                          in_=c.scTres[:, t * 128:t * 128 + 128]),
               reads=[c.scTres_b], sem=sct_st, inc=16)
        scl.op(lambda e, t=t: e.dma_start(out=d.sctl_sp[:, t * 128:t * 128 + 128],
                                          in_=S.scTl),
               reads=[S.scTl_b], sem=sct_st2[pi], inc=16)
        b2 = bankn()
        vec.op(lambda e, b2=b2, b=b: e.tensor_scalar(
            out=S.maskp, in0=P.psum[b][0:16, 0:128], scalar1=0.0, scalar2=None,
            op0=ALU.is_gt),
            reads=[P.bank[b]], writes=[S.maskp_b])
        vec.op(lambda e: e.tensor_tensor_scan(
            out=S.cump, data0=S.maskp, data1=S.maskp, initial=carry[:, 0:1],
            op0=ALU.add, op1=ALU.bypass),
            reads=[S.maskp_b, carry_b], writes=[S.cump_b])
        vec.op(lambda e: e.tensor_copy(carry, S.cump[:, 127:128]),
               reads=[S.cump_b], writes=[carry_b])
        vec.op(lambda e: e.tensor_scalar(out=S.slotm, in0=S.cump, scalar1=c.cap16,
                                         scalar2=1.0, op0=ALU.add, op1=ALU.subtract),
               reads=[S.cump_b, c.cb], writes=[S.slotm_b])
        ten.group([lambda e, b2=b2: e.transpose(P.psum[b2][:, 0:16], S.slotm,
                                                c.idf[0:16, 0:16])],
                  reads=[S.slotm_b, c.cb], writes=[P.bank[b2]])
        act.op(lambda e, b2=b2: e.copy(out=S.slotT, in_=P.psum[b2][:, 0:16]),
               reads=[P.bank[b2]], writes=[S.slotT_b])
        vec.op(lambda e: e.tensor_tensor(out=S.t16a, in0=S.eq1, in1=S.slotT, op=ALU.mult),
               reads=[S.eq1_b, S.slotT_b], writes=[S.t16a_b])
        vec.op(lambda e: e.reduce_sum(out=S.f1p, in_=S.t16a, axis=AX.X),
               reads=[S.t16a_b], writes=[S.f1p_b])
        vec.op(lambda e: e.tensor_tensor(out=S.t16a, in0=S.eq2, in1=S.slotT, op=ALU.mult),
               reads=[S.eq2_b, S.slotT_b], writes=[S.t16a_b])
        vec.op(lambda e: e.reduce_sum(out=S.f2p, in_=S.t16a, axis=AX.X),
               reads=[S.t16a_b], writes=[S.f2p_b])
        vec.op(lambda e, CAPp=CAPp: e.tensor_scalar(
            out=S.f1p, in0=S.f1p, scalar1=float(16 * CAPp - 1), scalar2=0.0,
            op0=ALU.min, op1=ALU.max), reads=[S.f1p_b], writes=[S.f1p_b])
        vec.op(lambda e, CAPp=CAPp: e.tensor_scalar(
            out=S.f2p, in0=S.f2p, scalar1=float(16 * CAPp - 1), scalar2=0.0,
            op0=ALU.min, op1=ALU.max), reads=[S.f2p_b], writes=[S.f2p_b])
        vec.op(lambda e: e.tensor_copy(S.if1p[:, :], S.f1p),
               reads=[S.f1p_b], writes=[S.if1p_b])
        vec.op(lambda e: e.tensor_copy(S.if2p[:, :], S.f2p),
               reads=[S.f2p_b], writes=[S.if2p_b])
        gps.op(lambda e: e.indirect_dma_start(
            out=d.x1g_all[:, :],
            out_offset=bass.IndirectOffsetOnAxis(ap=S.if1p[:, :], axis=0),
            in_=S.x1, in_offset=None),
            reads=[S.x1_b, S.if1p_b, c.x1g_buf], sem=x1sc_sem[pi][0], inc=16)
        gps.op(lambda e: e.indirect_dma_start(
            out=d.x1g_all[:, :],
            out_offset=bass.IndirectOffsetOnAxis(ap=S.if2p[:, :], axis=0),
            in_=S.x1, in_offset=None),
            reads=[S.x1_b, S.if2p_b, c.x1g_buf], sem=x1sc_sem[pi][1], inc=16)
        vec.op(lambda e: e.tensor_copy(S.weq[:, 0:1], S.w1), reads=[S.w1_b], writes=[S.weq_b])
        vec.op(lambda e: e.tensor_copy(S.weq[:, 1:2], S.w2), reads=[S.w2_b], writes=[S.weq_b])
        vec.op(lambda e: e.tensor_copy(S.weq[:, 2:3], S.f1p), reads=[S.f1p_b], writes=[S.weq_b])
        vec.op(lambda e: e.tensor_copy(S.weq[:, 3:4], S.f2p), reads=[S.f2p_b], writes=[S.weq_b])
        scl.op(lambda e, t=t: e.dma_start(out=d.weq_sp[t * 128:t * 128 + 128, :],
                                          in_=S.weq[:, 0:4]),
               reads=[S.weq_b], sem=weq_st[pi], inc=16)
        scl.op(lambda e, t=t: e.dma_start(out=d.x1tm_sp[t * 128:t * 128 + 128, :], in_=S.x1),
               reads=[S.x1_b], sem=x1tm_st[pi], inc=16)
        if cfg.dbg:
            gps.op(lambda e, t=t: e.dma_start(out=d.dbg_x1[t * 128:t * 128 + 128, :],
                                              in_=S.x1),
                   reads=[S.x1_b], sem=dbg_sem[pi], inc=16)
            gps.op(lambda e, t=t: e.dma_start(out=d.dbg_sc[t * 128:t * 128 + 128, :],
                                              in_=S.scores),
                   reads=[S.scores_b], sem=dbg_sem2[pi], inc=16)

    for t in range(ntile):
        tile_body(t, t % 2, slots[t % 2])
    at.post_sems = (weq_st + x1tm_st + [sct_st] + sct_st2
                    + x1sc_sem[0] + x1sc_sem[1])

def emit_ffn_split(P, d, c, cfg, at):
    syn, act, vec, gps = P.sync, P.scalar, P.vector, P.gpsimd
    stg, stg_sem, stg_buf = at.stg, at.stg_sem, at.stg_buf
    wsp = [P.bf16(2048) for _ in range(2)]
    wsp_buf = [Buf() for _ in range(2)]
    wst_sem = [P.new_sem("fst%d" % i) for i in range(2)]
    si = [0]

    def do(load_src, store_dst, deint):
        s = si[0] % 2
        si[0] += 1
        syn.op(lambda e: e.dma_start(out=stg[s], in_=load_src),
               writes=[stg_buf[s]], sem=stg_sem[s], inc=16)
        if deint:
            act.op(lambda e: e.copy(out=wsp[s][:, 0:512], in_=stg[s][:, 0:1024:2]),
                   reads=[stg_buf[s]], writes=[wsp_buf[s]])
            vec.op(lambda e: e.tensor_tensor(out=wsp[s][:, 512:1024], in0=stg[s][:, 0:1024:2],
                                             in1=wsp[s][:, 0:512], op=ALU.subtract),
                   reads=[stg_buf[s], wsp_buf[s]], writes=[wsp_buf[s]])
            act.op(lambda e: e.copy(out=wsp[s][:, 1024:1536], in_=stg[s][:, 1:1024:2]),
                   reads=[stg_buf[s], wsp_buf[s]], writes=[wsp_buf[s]])
            vec.op(lambda e: e.tensor_tensor(out=wsp[s][:, 1536:2048], in0=stg[s][:, 1:1024:2],
                                             in1=wsp[s][:, 1024:1536], op=ALU.subtract),
                   reads=[stg_buf[s], wsp_buf[s]], writes=[wsp_buf[s]])
        else:
            act.op(lambda e: e.copy(out=wsp[s][:, 0:1024], in_=stg[s]),
                   reads=[stg_buf[s]], writes=[wsp_buf[s]])
            vec.op(lambda e: e.tensor_tensor(out=wsp[s][:, 1024:2048], in0=stg[s],
                                             in1=wsp[s][:, 0:1024], op=ALU.subtract),
                   reads=[stg_buf[s], wsp_buf[s]], writes=[wsp_buf[s]])
        gps.op(lambda e: e.dma_start(out=store_dst, in_=wsp[s]),
               reads=[wsp_buf[s]], sem=wst_sem[s], inc=16)

    for e_ in range(16):
        for k in range(KCH):
            do(d.gate_up[e_, 128 * k:128 * (k + 1), :], d.wsgu[e_, k, :, :], True)
        for k2 in range(4):
            do(d.down[e_, 128 * k2:128 * (k2 + 1), :], d.wsdn[e_, k2, :, :], False)
    return wst_sem


def emit_ffn(P, d, c, cfg, at):
    from concourse import library_config
    I32 = mybir.dt.int32
    U32 = mybir.dt.uint32
    L = cfg.L
    TOWN = L // 2
    CAP = 768 if TOWN >= 2048 else TOWN
    NTT = CAP // 256
    T16 = TOWN // 16
    ten, vec, act, gps, syn, scl = P.tensor, P.vector, P.scalar, P.gpsimd, P.sync, P.scalar

    csem3 = P.new_sem("f_c2")
    rawf = P.f32(HID)
    rawf_b = Buf()
    syn.op(lambda e: e.dma_start(out=rawf[0:16, :], in_=d.dbias[:, :]), sem=csem3, inc=16,
           writes=[rawf_b])
    c.dbh = P.bf16(HID, parts=16); c.dbl = P.bf16(HID, parts=16)
    c.dbh_b, c.dbl_b = Buf(), Buf()
    split_sbuf(P, rawf[0:16, :], rawf_b, c.dbh, c.dbl, c.dbh_b, c.dbl_b, lo_eng=P.gpsimd)

    xg = [P.f32(HID) for _ in range(2)]
    xg_b = [Buf() for _ in range(2)]
    xg_sem = [P.new_sem("xg%d" % i) for i in range(2)]
    x1Teh = P.bf16(KCH * CAP); x1Teh_b = Buf()
    x1Tel = P.bf16(KCH * CAP); x1Tel_b = Buf()
    wge = P.bf16(KCH * 2048); wge_b = Buf()
    wge_sem = P.new_sem("wge")
    wde = [P.bf16(4 * 2048) for _ in range(2)]
    wde_b = [Buf() for _ in range(2)]
    wde_sem = [P.new_sem("wde%d" % i) for i in range(2)]
    gatec = P.f32(256); gatec_b = Buf()
    sigc = P.f32(256); sigc_b = Buf()
    gluc = P.f32(256); gluc_b = Buf()
    upc = P.f32(256); upc_b = Buf()
    actf = P.f32(256); actf_b = Buf()
    acth = P.bf16(1024); acth_b = Buf()
    actl = P.bf16(1024); actl_b = Buf()
    yfm = P.f32(256); yfm_b = Buf()
    ytm = [P.f32(HID) for _ in range(2)]
    ytm_b = [Buf() for _ in range(2)]
    ysp_st = [P.new_sem("ysp_st0"), P.new_sem("ysp_st1")]

    def load_wge(e_):
        syn.dma_batch([lambda e, e_=e_, k=k: e.dma_start(
            out=wge[:, 2048 * k:2048 * (k + 1)], in_=d.wsgu[e_, k, :, :])
            for k in range(KCH)], writes=[wge_b], sem=wge_sem)

    def load_wde(e_):
        s = e_ % 2
        syn.dma_batch([lambda e, e_=e_, k2=k2: e.dma_start(
            out=wde[s][:, 2048 * k2:2048 * (k2 + 1)], in_=d.wsdn[e_, k2, :, :])
            for k2 in range(4)], writes=[wde_b[s]], sem=wde_sem[s])

    load_wge(0)
    load_wde(0)
    tk = [0]

    def tbank():
        b = 2 + tk[0] % 2
        tk[0] += 1
        return b

    for e_ in range(16):
        ws = e_ % 2
        for j in range(CAP // 128):
            sj = j % 2
            r0 = CAP * e_ + 128 * j
            syn.op(lambda e, r0=r0, sj=sj: e.dma_start(
                out=xg[sj], in_=d.x1g_all[r0:r0 + 128, :]),
                reads=[c.x1g_buf], writes=[xg_b[sj]], sem=xg_sem[sj], inc=16)
            for half in range(2):
                b = tbank()
                fns = []
                for k in range(4 * half, 4 * half + 4):
                    fns.append(lambda e, k=k, b=b, sj=sj: e.transpose(
                        P.psum[b][:, (k % 4) * 128:(k % 4) * 128 + 128],
                        xg[sj][:, 128 * k:128 * k + 128], c.idf))
                ten.group(fns, reads=[xg_b[sj], c.cb], writes=[P.bank[b]])
                for k in range(4 * half, 4 * half + 4):
                    ps = P.psum[b][:, (k % 4) * 128:(k % 4) * 128 + 128]
                    co = CAP * k + 128 * j
                    act.op(lambda e, ps=ps, co=co: e.copy(out=x1Teh[:, co:co + 128], in_=ps),
                           reads=[P.bank[b]], writes=[x1Teh_b])
                    vec.op(lambda e, ps=ps, co=co: e.tensor_tensor(
                        out=x1Tel[:, co:co + 128], in0=ps, in1=x1Teh[:, co:co + 128],
                        op=ALU.subtract),
                        reads=[P.bank[b], x1Teh_b], writes=[x1Tel_b])
        for tt_ in range(NTT):
            ts_ = tt_ * 256
            for g in range(4):
                bgt = 4 if (g % 2 == 0) else 6
                bup = 5
                fns = []
                for k in range(KCH):
                    fns += mm3(P.psum[bgt][:, 0:256],
                               wge[:, 2048 * k + 128 * g:2048 * k + 128 * g + 128],
                               wge[:, 2048 * k + 512 + 128 * g:2048 * k + 512 + 128 * g + 128],
                               x1Teh[:, CAP * k + ts_:CAP * k + ts_ + 256],
                               x1Tel[:, CAP * k + ts_:CAP * k + ts_ + 256],
                               start=(k == 0), stop=(k == KCH - 1))
                ten.group(fns, reads=[wge_b, x1Teh_b, x1Tel_b], writes=[P.bank[bgt]])
                fns = []
                for k in range(KCH):
                    fns += mm3(P.psum[bup][:, 0:256],
                               wge[:, 2048 * k + 1024 + 128 * g:2048 * k + 1024 + 128 * g + 128],
                               wge[:, 2048 * k + 1536 + 128 * g:2048 * k + 1536 + 128 * g + 128],
                               x1Teh[:, CAP * k + ts_:CAP * k + ts_ + 256],
                               x1Tel[:, CAP * k + ts_:CAP * k + ts_ + 256],
                               start=(k == 0), stop=(k == KCH - 1))
                ten.group(fns, reads=[wge_b, x1Teh_b, x1Tel_b], writes=[P.bank[bup]])
                bcol = 8 * e_ + g
                vec.op(lambda e, bgt=bgt, bcol=bcol: e.tensor_scalar(
                    out=gatec, in0=P.psum[bgt][:, 0:256], scalar1=c.gub[:, bcol:bcol + 1],
                    scalar2=7.0, op0=ALU.add, op1=ALU.min),
                    reads=[P.bank[bgt], c.cb], writes=[gatec_b])
                act.op(lambda e: e.activation(out=sigc, in_=gatec, func=AF.Sigmoid, scale=1.702),
                       reads=[gatec_b], writes=[sigc_b])
                vec.op(lambda e: e.tensor_tensor(out=gluc, in0=gatec, in1=sigc, op=ALU.mult),
                       reads=[gatec_b, sigc_b], writes=[gluc_b])
                ucol = 8 * e_ + 4 + g
                vec.op(lambda e, bup=bup, ucol=ucol: e.tensor_scalar(
                    out=upc, in0=P.psum[bup][:, 0:256], scalar1=c.gub1[:, ucol:ucol + 1],
                    scalar2=8.0, op0=ALU.add, op1=ALU.min),
                    reads=[P.bank[bup], c.cb, c.gub1_b], writes=[upc_b])
                vec.op(lambda e: e.scalar_tensor_tensor(
                    out=actf, in0=upc, scalar=-6.0, in1=gluc, op0=ALU.max, op1=ALU.mult),
                    reads=[upc_b, gluc_b], writes=[actf_b])
                act.op(lambda e, g=g: e.copy(out=acth[:, 256 * g:256 * (g + 1)], in_=actf),
                       reads=[actf_b], writes=[acth_b])
                vec.op(lambda e, g=g: e.tensor_tensor(
                    out=actl[:, 256 * g:256 * (g + 1)], in0=actf,
                    in1=acth[:, 256 * g:256 * (g + 1)], op=ALU.subtract),
                    reads=[actf_b, acth_b], writes=[actl_b])
            if tt_ == NTT - 1 and e_ + 1 < 16:
                load_wge(e_ + 1)
            for oc in range(KCH):
                b = oc % 2
                fns = []
                for k2 in range(4):
                    fns += mm3(P.psum[b][:, 0:256],
                               wde[ws][:, 2048 * k2 + 128 * oc:2048 * k2 + 128 * oc + 128],
                               wde[ws][:, 2048 * k2 + 1024 + 128 * oc:2048 * k2 + 1024 + 128 * oc + 128],
                               acth[:, 256 * k2:256 * (k2 + 1)],
                               actl[:, 256 * k2:256 * (k2 + 1)],
                               start=(k2 == 0), stop=(k2 == 3))
                ten.group(fns, reads=[wde_b[ws], acth_b, actl_b], writes=[P.bank[b]])
                act.op(lambda e, b=b: e.copy(out=yfm, in_=P.psum[b][:, 0:256]),
                       reads=[P.bank[b]], writes=[yfm_b])
                for sub in range(2):
                    b2 = tbank()
                    ten.group([lambda e, b2=b2, sub=sub: e.transpose(
                        P.psum[b2][:, 0:128], yfm[:, 128 * sub:128 * sub + 128], c.idf)],
                        reads=[yfm_b, c.cb], writes=[P.bank[b2]])
                    act.op(lambda e, b2=b2, sub=sub, oc=oc: e.copy(
                        out=ytm[sub][:, 128 * oc:128 * oc + 128], in_=P.psum[b2][:, 0:128]),
                        reads=[P.bank[b2]], writes=[ytm_b[sub]])
            for sub in range(2):
                r = e_ * CAP + ts_ + 128 * sub
                gps.op(lambda e, r=r, sub=sub: e.dma_start(
                    out=d.y_sp[r:r + 128, :], in_=ytm[sub]),
                    reads=[ytm_b[sub]], sem=ysp_st[sub], inc=16)
        if e_ + 1 < 16:
            load_wde(e_ + 1)

    gps.barrier_wait([(ysp_st[0], ysp_st[0].count), (ysp_st[1], ysp_st[1].count)])
    x1f = [P.f32(HID) for _ in range(2)]
    x1f_b = [Buf() for _ in range(2)]
    x1f_sem = [P.new_sem("x1f%d" % i) for i in range(2)]
    weqf = [P.f32(4) for _ in range(2)]
    weqf_b = [Buf() for _ in range(2)]
    weqf_sem = [P.new_sem("wq%d" % i) for i in range(2)]
    sch = [P.bf16(128, parts=16) for _ in range(2)]
    sch_b = [Buf() for _ in range(2)]
    sch_sem = [P.new_sem("sh%d" % i) for i in range(2)]
    scl2 = [P.bf16(128, parts=16) for _ in range(2)]
    scl2_b = [Buf() for _ in range(2)]
    scl2_sem = [P.new_sem("sc%d" % i) for i in range(2)]
    if1 = P.sb("if1", [128, 1], I32); if1_b = Buf()
    if2 = P.sb("if2", [128, 1], I32); if2_b = Buf()
    y1t = [P.f32(HID) for _ in range(2)]
    y1t_b = [Buf() for _ in range(2)]
    y1_sem = [P.new_sem("y1%d" % i) for i in range(2)]
    y2t = [P.f32(HID) for _ in range(2)]
    y2t_b = [Buf() for _ in range(2)]
    y2_sem = [P.new_sem("y2%d" % i) for i in range(2)]
    tc1 = P.f32(HID); tc1_b = Buf()
    rr2 = P.f32(HID); rr2_b = Buf()
    yt = P.f32(HID); yt_b = Buf()
    st6f = P.f32(12); st6f_b = Buf()
    mvf = P.f32(2); mvf_b = Buf()
    sdf = P.f32(1); sdf_b = Buf()
    istdf = P.f32(1); istdf_b = Buf()
    yst = P.new_sem("yst")

    for t in range(TOWN // 128):
        pi = t % 2
        syn.op(lambda e, t=t, pi=pi: e.dma_start(
            out=x1f[pi], in_=d.x1tm_sp[t * 128:t * 128 + 128, :]),
            writes=[x1f_b[pi]], sem=x1f_sem[pi], inc=16)
        syn.op(lambda e, t=t, pi=pi: e.dma_start(
            out=weqf[pi], in_=d.weq_sp[t * 128:t * 128 + 128, :]),
            writes=[weqf_b[pi]], sem=weqf_sem[pi], inc=16)
        syn.op(lambda e, t=t, pi=pi: e.dma_start(
            out=sch[pi], in_=d.scth_sp[:, t * 128:t * 128 + 128]),
            writes=[sch_b[pi]], sem=sch_sem[pi], inc=16)
        syn.op(lambda e, t=t, pi=pi: e.dma_start(
            out=scl2[pi], in_=d.sctl_sp[:, t * 128:t * 128 + 128]),
            writes=[scl2_b[pi]], sem=scl2_sem[pi], inc=16)
        vec.op(lambda e, pi=pi: e.tensor_copy(if1[:, :], weqf[pi][:, 2:3]),
               reads=[weqf_b[pi]], writes=[if1_b])
        vec.op(lambda e, pi=pi: e.tensor_copy(if2[:, :], weqf[pi][:, 3:4]),
               reads=[weqf_b[pi]], writes=[if2_b])
        gps.op(lambda e, pi=pi: e.indirect_dma_start(
            out=y1t[pi], out_offset=None, in_=d.y_sp[:, :],
            in_offset=bass.IndirectOffsetOnAxis(ap=if1[:, :], axis=0)),
            reads=[if1_b], writes=[y1t_b[pi]], sem=y1_sem[pi], inc=16)
        gps.op(lambda e, pi=pi: e.indirect_dma_start(
            out=y2t[pi], out_offset=None, in_=d.y_sp[:, :],
            in_offset=bass.IndirectOffsetOnAxis(ap=if2[:, :], axis=0)),
            reads=[if2_b], writes=[y2t_b[pi]], sem=y2_sem[pi], inc=16)
        for half in range(2):
            b2 = half
            ten.group(mm3(P.psum[b2][:, 0:512],
                          sch[pi], scl2[pi],
                          c.dbh[:, 512 * half:512 * half + 512],
                          c.dbl[:, 512 * half:512 * half + 512]),
                      reads=[sch_b[pi], scl2_b[pi], c.dbh_b, c.dbl_b], writes=[P.bank[b2]])
        vec.op(lambda e, pi=pi: e.tensor_scalar(
            out=tc1, in0=y1t[pi], scalar1=weqf[pi][:, 0:1], scalar2=None, op0=ALU.mult),
            reads=[y1t_b[pi], weqf_b[pi]], writes=[tc1_b])
        vec.op(lambda e, pi=pi: e.scalar_tensor_tensor(
            out=tc1, in0=y2t[pi], scalar=weqf[pi][:, 1:2], in1=tc1,
            op0=ALU.mult, op1=ALU.add),
            reads=[y2t_b[pi], weqf_b[pi], tc1_b], writes=[tc1_b])
        vec.op(lambda e, pi=pi: e.tensor_tensor(out=rr2, in0=tc1, in1=x1f[pi], op=ALU.add),
               reads=[tc1_b, x1f_b[pi]], writes=[rr2_b])
        for half in range(2):
            b2 = half
            vec.op(lambda e, b2=b2, half=half: e.tensor_tensor(
                out=rr2[:, 512 * half:512 * half + 512],
                in0=rr2[:, 512 * half:512 * half + 512],
                in1=P.psum[b2][:, 0:512], op=ALU.add),
                reads=[P.bank[b2], rr2_b], writes=[rr2_b])
        vec.op(lambda e: e.bn_stats(out=st6f[:, 0:6], in_=rr2[:, 0:512]),
               reads=[rr2_b], writes=[st6f_b])
        vec.op(lambda e: e.bn_stats(out=st6f[:, 6:12], in_=rr2[:, 512:1024]),
               reads=[rr2_b], writes=[st6f_b])
        vec.op(lambda e: e.bn_aggr(out=mvf, in_=st6f), reads=[st6f_b], writes=[mvf_b])
        act.op(lambda e: e.activation(out=sdf, in_=mvf[:, 1:2], func=AF.Sqrt,
                                      bias=c.eps[:, 0:1]),
               reads=[mvf_b, c.cb], writes=[sdf_b])
        vec.op(lambda e: e.reciprocal(out=istdf, in_=sdf), reads=[sdf_b], writes=[istdf_b])
        vec.op(lambda e: e.tensor_scalar(
            out=yt, in0=rr2, scalar1=mvf[:, 0:1], scalar2=istdf[:, 0:1],
            op0=ALU.subtract, op1=ALU.mult),
            reads=[rr2_b, mvf_b, istdf_b], writes=[yt_b])
        gps.op(lambda e, t=t: e.dma_start(out=d.y[t * 128:t * 128 + 128, :], in_=yt),
               reads=[yt_b], sem=yst, inc=16)
    P.gpsimd.barrier_wait([(yst, yst.count)])







# version 15
# speedup vs baseline: 1.0204x; 1.0204x over previous
import contextlib

import numpy as np
import ml_dtypes

import concourse.bass as bass
import concourse.mybir as mybir
from concourse.bass_utils import run_bass_kernel_spmd

F32 = mybir.dt.float32
BF16 = mybir.dt.bfloat16
AF = mybir.ActivationFunctionType
ALU = mybir.AluOpType
AX = mybir.AxisListType

HID = 1024
NHL = 8
HD = 64
C = 128
SC = 512
KCH = HID // 128

BF = ml_dtypes.bfloat16


class Sem:
    __slots__ = ("h", "count", "name")
    def __init__(self, h, name):
        self.h = h
        self.count = 0
        self.name = name


class Buf:
    __slots__ = ("last_w", "readers", "name")
    def __init__(self, name=""):
        self.last_w = None
        self.readers = []
        self.name = name


class Eng:
    def __init__(self, prog, name):
        self.prog = prog
        self.name = name
        self.sem = prog.new_sem("ctr_" + name)
        self.marks = {}
        self.q = []

    def _wait(self, toks):
        need = {}
        for sem, val in toks:
            sid = id(sem)
            if self.marks.get(sid, 0) >= val:
                continue
            cur = need.get(sid)
            if cur is None or cur[1] < val:
                need[sid] = (sem, val)
        for sid, (sem, val) in need.items():
            self.marks[sid] = val
            self.q.append(lambda e, s=sem.h, v=val: e.wait_ge(s, v))

    def _collect(self, reads, writes):
        toks = []
        for b in reads:
            if b.last_w:
                toks.append(b.last_w)
        for b in writes:
            if b.last_w:
                toks.append(b.last_w)
            toks.extend(b.readers)
        return toks

    def _record(self, tok, reads, writes):
        for b in reads:
            b.readers.append(tok)
        for b in writes:
            b.last_w = tok
            b.readers = []

    def op(self, fn, reads=(), writes=(), sem=None, inc=1):
        self._wait(self._collect(reads, writes))
        usem = sem if sem is not None else self.sem
        usem.count += inc
        cnt = usem.count
        def run(e, fn=fn, s=usem.h, v=inc):
            fn(e).then_inc(s, v)
        self.q.append(run)
        tok = (usem, cnt)
        self._record(tok, reads, writes)
        return tok

    def group(self, fns, reads=(), writes=()):
        self._wait(self._collect(reads, writes))
        self.sem.count += 1
        cnt = self.sem.count
        last = len(fns) - 1
        for i, fn in enumerate(fns):
            if i == last:
                self.q.append(lambda e, fn=fn, s=self.sem.h: fn(e).then_inc(s, 1))
            else:
                self.q.append(lambda e, fn=fn: fn(e))
        tok = (self.sem, cnt)
        self._record(tok, reads, writes)
        return tok

    def dma_batch(self, fns, reads=(), writes=(), sem=None):
        self._wait(self._collect(reads, writes))
        for fn in fns:
            sem.count += 16
            self.q.append(lambda e, fn=fn, s=sem.h: fn(e).then_inc(s, 16))
        tok = (sem, sem.count)
        self._record(tok, reads, writes)
        return tok

    def barrier_wait(self, toks):
        self._wait(toks)


class Arena:
    def __init__(self, t, ncols):
        self.t = t
        self.ncols = ncols
        self.off = 0
        self.perm_end = 0
        self.hwm = 0

    def alloc(self, cols, parts=128):
        o = self.off
        assert o + cols <= self.ncols, "arena overflow (%d + %d > %d)" % (o, cols, self.ncols)
        self.off += cols
        self.hwm = max(self.hwm, self.off)
        return self.t[0:parts, o:o + cols]

    def mark_perm(self):
        self.perm_end = self.off

    def push(self):
        return self.off

    def pop(self, saved):
        self.off = saved

    def reset(self):
        self.off = self.perm_end


class Prog:
    def __init__(self, nc, es, f32_cols=28000, bf16_cols=96000):
        self.nc = nc
        self.es = es
        self.all_sems = []
        self.sync = Eng(self, "sync")
        self.scalar = Eng(self, "scalar")
        self.vector = Eng(self, "vector")
        self.tensor = Eng(self, "tensor")
        self.gpsimd = Eng(self, "gpsimd")
        self.engines = [self.sync, self.scalar, self.vector, self.tensor, self.gpsimd]
        self.af = Arena(es.enter_context(nc.sbuf_tensor("arena_f32", [128, f32_cols], F32)), f32_cols)
        self.ab = Arena(es.enter_context(nc.sbuf_tensor("arena_bf16", [128, bf16_cols], BF16)), bf16_cols)
        self.psum = [es.enter_context(nc.psum_tensor("pb%d" % i, [128, 512], F32)) for i in range(7)]
        self.psum_bf = es.enter_context(nc.psum_tensor("pb7", [128, 1024], BF16))
        self.bank = [Buf("bank%d" % i) for i in range(8)]

    def sb(self, name, shape, dtype):
        return self.es.enter_context(self.nc.sbuf_tensor(name, list(shape), dtype))

    def new_sem(self, name):
        s = Sem(self.es.enter_context(self.nc.semaphore(name)), name)
        self.all_sems.append(s)
        return s

    def f32(self, cols, parts=128):
        return self.af.alloc(cols, parts)

    def bf16(self, cols, parts=128):
        return self.ab.alloc(cols, parts)

    def barrier(self, extra=()):
        toks = [(e.sem, e.sem.count) for e in self.engines]
        toks += [(s, s.count) for s in extra]
        for e in self.engines:
            e.barrier_wait(toks)

    def finalize(self):
        with self.nc.Block() as block:
            @block.sync
            def _(e):
                for f in self.sync.q:
                    f(e)
            @block.scalar
            def _(e):
                for f in self.scalar.q:
                    f(e)
            @block.vector
            def _(e):
                for f in self.vector.q:
                    f(e)
            @block.tensor
            def _(e):
                for f in self.tensor.q:
                    f(e)
            @block.gpsimd
            def _(e):
                for f in self.gpsimd.q:
                    f(e)


def mm3(psum_ap, a_hi, a_lo, b_hi, b_lo, tp=None, start=True, stop=True):
    def mk(lhs, rhs, st, sp):
        return lambda e: e.matmul(psum_ap, lhs, rhs, start=st, stop=sp, tile_position=tp)
    return [mk(a_hi, b_hi, start, False),
            mk(a_hi, b_lo, False, False),
            mk(a_lo, b_hi, False, stop)]


def split_psum(P, psum_ap, bank, hi_ap, lo_ap, hi_b, lo_b, extra=()):
    P.scalar.op(lambda e: e.copy(out=hi_ap, in_=psum_ap),
                reads=[bank, *extra], writes=[hi_b])
    P.vector.op(lambda e: e.tensor_tensor(out=lo_ap, in0=psum_ap, in1=hi_ap, op=ALU.subtract),
                reads=[bank, hi_b], writes=[lo_b])


def split_sbuf(P, src_ap, src_b, hi_ap, lo_ap, hi_b, lo_b, lo_eng=None):
    P.scalar.op(lambda e: e.copy(out=hi_ap, in_=src_ap), reads=[src_b], writes=[hi_b])
    (lo_eng or P.vector).op(
        lambda e: e.tensor_tensor(out=lo_ap, in0=src_ap, in1=hi_ap, op=ALU.subtract),
        reads=[src_b, hi_b], writes=[lo_b])


class NS:
    def __init__(self, **kw):
        self.__dict__.update(kw)


def declare_drams(nc, cfg):
    d = NS()
    L = cfg.L
    TOWN = L // 2
    ein, eout, intl = "ExternalInput", "ExternalOutput", "Internal"
    d.x_seq = nc.dram_tensor("x_seq", [L, HID], F32, kind=ein).ap()
    d.x_own = nc.dram_tensor("x_own", [TOWN, HID], F32, kind=ein).ap()
    d.c_hm = nc.dram_tensor("c_hm", [128, 2], F32, kind=ein).ap()
    d.wq = nc.dram_tensor("wq", [HID, 512], F32, kind=ein).ap()
    d.wk = nc.dram_tensor("wk", [HID, 512], F32, kind=ein).ap()
    d.wv = nc.dram_tensor("wv", [HID, 512], F32, kind=ein).ap()
    d.wgk1 = nc.dram_tensor("wgk1", [HID, 16], F32, kind=ein).ap()
    d.wgk2 = nc.dram_tensor("wgk2", [16, 512], F32, kind=ein).ap()
    d.bgk2t = nc.dram_tensor("bgk2t", [128, 4], F32, kind=ein).ap()
    d.wgp = nc.dram_tensor("wgp", [HID, HID], F32, kind=ein).ap()
    d.wop = nc.dram_tensor("wop", [HID, HID], F32, kind=ein).ap()
    d.router = nc.dram_tensor("router", [HID, 16], F32, kind=ein).ap()
    d.gate_up = nc.dram_tensor("gate_up", [16, HID, HID], F32, kind=ein).ap()
    d.down = nc.dram_tensor("down", [16, 512, HID], F32, kind=ein).ap()
    d.gubias = nc.dram_tensor("gubias", [128, 128], F32, kind=ein).ap()
    d.dbias = nc.dram_tensor("dbias", [16, HID], F32, kind=ein).ap()
    d.c_idf = nc.dram_tensor("c_idf", [128, 128], F32, kind=ein).ap()
    d.c_idb = nc.dram_tensor("c_idb", [128, 128], BF16, kind=ein).ap()
    d.c_triu = nc.dram_tensor("c_triu", [128, 128], F32, kind=ein).ap()
    d.c_ones = nc.dram_tensor("c_ones", [1, 128], BF16, kind=ein).ap()
    d.c_sel = nc.dram_tensor("c_sel", [16, 2048], BF16, kind=ein).ap()
    d.c_eps = nc.dram_tensor("c_eps", [128, 1], F32, kind=ein).ap()
    d.c_iota16 = nc.dram_tensor("c_iota16", [128, 16], F32, kind=ein).ap()
    d.c_cap16 = nc.dram_tensor("c_cap16", [16, 1], F32, kind=ein).ap()
    d.c_l17 = nc.dram_tensor("c_l17", [17, 17], F32, kind=ein).ap()
    d.c_gamma = nc.dram_tensor("c_gamma", [128, HID], F32, kind=ein).ap()
    d.c_beta = nc.dram_tensor("c_beta", [128, HID], F32, kind=ein).ap()
    d.y = nc.dram_tensor("y", [TOWN, HID], F32, kind=eout).ap()
    if cfg.upto == "attn":
        d.cc_in = nc.dram_tensor("cc_in", [L, 512], F32, kind=eout).ap()
        d.cc_out = None
    else:
        d.cc_in = nc.dram_tensor("cc_in", [L, 512], F32, kind=intl).ap()
        d.npc = 4 if L >= 8192 else 2
        d.psz = L // d.npc
        d.cc_out = [nc.dram_tensor("cc_out%d" % i, [2 * d.psz, 512], F32, kind=intl).ap()
                    for i in range(d.npc)]
    d.wsgu = nc.dram_tensor("wsgu", [16, KCH, 128, 2048], BF16, kind=intl).ap()
    d.wsdn = nc.dram_tensor("wsdn", [16, 4, 128, 2048], BF16, kind=intl).ap()
    d.x1h_sp = nc.dram_tensor("x1h_sp", [KCH, 128, TOWN], BF16, kind=intl).ap()
    d.scth_sp = nc.dram_tensor("scth_sp", [16, TOWN], BF16, kind=intl).ap()
    d.sctl_sp = nc.dram_tensor("sctl_sp", [16, TOWN], BF16, kind=intl).ap()
    d.x1tm_sp = nc.dram_tensor("x1tm_sp", [TOWN, HID], F32, kind=intl).ap()
    d.weq_sp = nc.dram_tensor("weq_sp", [TOWN, 4], F32, kind=intl).ap()
    CAP_ = 768 if TOWN >= 2048 else TOWN
    d.y_sp = nc.dram_tensor("y_sp", [16 * CAP_, HID], F32, kind=intl).ap()
    d.x1g_all = nc.dram_tensor("x1g_all", [16 * CAP_, HID], F32, kind=intl).ap()
    d.slotf_sp = nc.dram_tensor("slotf_sp", [16, TOWN], F32, kind=intl).ap()
    d.idx_sp = nc.dram_tensor("idx_sp", [16 * 48, 16], mybir.dt.int32, kind=intl).ap()
    d.x1l_sp = nc.dram_tensor("x1l_sp", [KCH, 128, TOWN], BF16, kind=intl).ap()
    if cfg.dbg:
        d.dbg_x1 = nc.dram_tensor("dbg_x1", [TOWN, HID], F32, kind=eout).ap()
        d.dbg_sc = nc.dram_tensor("dbg_sc", [TOWN, 16], F32, kind=eout).ap()
    return d


def emit_consts(P, d, cfg):
    c = NS()
    csem = P.new_sem("c_ld")
    nload = [0]

    def ld(ap_dst, ap_src):
        nload[0] += 1
        P.sync.op(lambda e: e.dma_start(out=ap_dst, in_=ap_src), sem=csem, inc=16)

    c.idf = P.f32(128); ld(c.idf, d.c_idf[:, :])
    c.idb = P.bf16(128); ld(c.idb, d.c_idb[:, :])
    c.triu = P.f32(128); ld(c.triu, d.c_triu[:, :])
    c.ones = P.bf16(128, parts=1); ld(c.ones, d.c_ones[:, :])
    c.eps = P.f32(1); ld(c.eps, d.c_eps[:, :])
    c.hm = P.f32(2); ld(c.hm, d.c_hm[:, :])
    c.iota16 = P.f32(16); ld(c.iota16, d.c_iota16[:, :])
    c.cap16 = P.f32(1, parts=16); ld(c.cap16, d.c_cap16[:, :])
    c.l17 = P.f32(17, parts=17); ld(c.l17, d.c_l17[:, :])

    c.bgk2 = P.f32(4); ld(c.bgk2, d.bgk2t[:, :])
    c.gub = P.f32(128); ld(c.gub, d.gubias[:, :])
    tok = (csem, 16 * nload[0])
    raw_buf = Buf("craw")
    raw_buf.last_w = tok
    c.ld_tok = tok
    raw = P.f32(HID)
    rawsem = P.new_sem("c_raw")

    def spl(cols, parts, loads):
        for dst_sl, src_ap in loads:
            P.sync.op(lambda e, dst_sl=dst_sl, src_ap=src_ap: e.dma_start(
                out=raw[0:parts, dst_sl], in_=src_ap), sem=rawsem, inc=16,
                reads=[], writes=[raw_buf])
        hi = P.bf16(cols, parts); lo = P.bf16(cols, parts)
        hb, lb = Buf(), Buf()
        split_sbuf(P, raw[0:parts, 0:cols], raw_buf, hi, lo, hb, lb, lo_eng=P.gpsimd)
        return hi, lo, hb, lb

    c.wgk1h, c.wgk1l, c.wgk1h_b, c.wgk1l_b = spl(
        128, 128, [(slice(16 * k, 16 * k + 16), d.wgk1[128 * k:128 * (k + 1), :])
                   for k in range(KCH)])
    c.wgk2h, c.wgk2l, c.wgk2h_b, c.wgk2l_b = spl(512, 16, [(slice(0, 512), d.wgk2[:, :])])
    c.rth, c.rtl, c.rth_b, c.rtl_b = spl(
        128, 128, [(slice(16 * k, 16 * k + 16), d.router[128 * k:128 * (k + 1), :])
                   for k in range(KCH)])
    c.gub1 = P.f32(128)
    c.gub1_b = Buf()
    cb = Buf("consts")
    cb.last_w = (csem, 16 * nload[0])
    c.cb = cb
    P.scalar.op(lambda e: e.add(out=c.gub1, in_=c.gub, add=1.0),
                reads=[cb], writes=[c.gub1_b])
    return c


def emit_attn(P, d, c, cfg, split_factory=None):
    L = cfg.L
    nsc = L // SC
    nch = L // C
    ten, vec, act, gps, syn = P.tensor, P.vector, P.scalar, P.gpsimd, P.sync

    stg = [P.f32(HID) for _ in range(2)]
    stg_sem = [P.new_sem("stg%d" % i) for i in range(2)]
    stg_buf = [Buf("stg%d" % i) for i in range(2)]
    wts = {}
    for wname, src in (("q", d.wq), ("k", d.wk), ("v", d.wv)):
        hi = P.bf16(KCH * 512); lo = P.bf16(KCH * 512)
        hb, lb = Buf(wname + "h"), Buf(wname + "l")
        for k in range(KCH):
            s = (len(wts) * KCH + k) % 2
            syn.op(lambda e, k=k, s=s, src=src: e.dma_start(
                out=stg[s][:, 0:512], in_=src[128 * k:128 * (k + 1), :]),
                reads=[], writes=[stg_buf[s]], sem=stg_sem[s], inc=16)
            act.op(lambda e, k=k, s=s, hi=hi: e.copy(out=hi[:, 512 * k:512 * (k + 1)],
                                                     in_=stg[s][:, 0:512]),
                   reads=[stg_buf[s]], writes=[hb])
            gps.op(lambda e, k=k, s=s, hi=hi, lo=lo: e.tensor_tensor(
                out=lo[:, 512 * k:512 * (k + 1)], in0=stg[s][:, 0:512],
                in1=hi[:, 512 * k:512 * (k + 1)], op=ALU.subtract),
                reads=[stg_buf[s], hb], writes=[lb])
        wts[wname] = (hi, lo, hb, lb)

    if split_factory is not None:
        wst_sem, split_q = split_factory(stg, stg_sem, stg_buf)
    else:
        wst_sem, split_q = None, []

    if cfg.upto != "attn":
        CAPa = 768 if (L // 2) >= 2048 else (L // 2)
        zz0 = P.f32(HID)
        zz0_b = Buf()
        vec.op(lambda e: e.memset(zz0, 0.0), writes=[zz0_b])
        c.x1g_buf = Buf("x1g")
        x1g_pre = P.new_sem("x1g_pre")
        P.scalar.dma_batch([lambda e, r=r: e.dma_start(
            out=d.x1g_all[128 * r:128 * (r + 1), :], in_=zz0)
            for r in range(16 * CAPa // 128)],
            reads=[zz0_b], writes=[c.x1g_buf], sem=x1g_pre)
        c.x1g_pre = x1g_pre

    x_t = [P.f32(HID) for _ in range(2)]
    x_sem = [P.new_sem("x%d" % i) for i in range(2)]
    x_buf = [Buf("x%d" % i) for i in range(2)]
    xTh = [P.bf16(KCH * 512)]; xTh.append(xTh[0])
    xTl = [P.bf16(KCH * 512)]; xTl.append(xTl[0])
    xT_hb = [Buf()]; xT_hb.append(xT_hb[0])
    xT_lb = [Buf()]; xT_lb.append(xT_lb[0])
    qh = [P.bf16(2048) for _ in range(2)]; ql = [P.bf16(2048) for _ in range(2)]
    kh = [P.bf16(2048) for _ in range(2)]; kl = [P.bf16(2048) for _ in range(2)]
    vv = [P.f32(2048) for _ in range(2)]
    gg = [P.f32(2048) for _ in range(2)]
    qh_b = [Buf() for _ in range(2)]; ql_b = [Buf() for _ in range(2)]
    kh_b = [Buf() for _ in range(2)]; kl_b = [Buf() for _ in range(2)]
    vv_b = [Buf() for _ in range(2)]; gg_b = [Buf() for _ in range(2)]
    z1h = [P.bf16(512, parts=16)]; z1h.append(z1h[0])
    z1l = [P.bf16(512, parts=16)]; z1l.append(z1l[0])
    z1h_b = [Buf()]; z1h_b.append(z1h_b[0])
    z1l_b = [Buf()]; z1l_b.append(z1l_b[0])
    sgt = [P.f32(512) for _ in range(2)]
    sgt_b = [Buf() for _ in range(2)]

    A = [P.f32(C) for _ in range(2)]; A_b = [Buf() for _ in range(2)]
    eNA = [P.f32(C) for _ in range(2)]; eNA_b = [Buf() for _ in range(2)]
    eATM = [P.f32(C) for _ in range(2)]; eATM_b = [Buf() for _ in range(2)]
    eacc = [P.f32(1) for _ in range(2)]; eacc_b = [Buf() for _ in range(2)]
    Vt = [P.f32(C) for _ in range(2)]; Vt_b = [Buf() for _ in range(2)]
    Wc = [P.f32(C) for _ in range(2)]; Wc_b = [Buf() for _ in range(2)]
    VTh = [P.bf16(C) for _ in range(2)]; VTh_b = [Buf() for _ in range(2)]
    VTl = [P.bf16(C) for _ in range(2)]; VTl_b = [Buf() for _ in range(2)]
    WTh = [P.bf16(C) for _ in range(2)]; WTh_b = [Buf() for _ in range(2)]
    WTl = [P.bf16(C) for _ in range(2)]; WTl_b = [Buf() for _ in range(2)]
    kT2h = [P.bf16(C) for _ in range(2)]; kT2h_b = [Buf() for _ in range(2)]
    kT2l = [P.bf16(C) for _ in range(2)]; kT2l_b = [Buf() for _ in range(2)]
    PTa = [P.f32(C) for _ in range(2)]; PTa_b = [Buf() for _ in range(2)]
    PTb = [P.f32(C) for _ in range(2)]; PTb_b = [Buf() for _ in range(2)]
    PTah = [P.bf16(C) for _ in range(2)]; PTah_b = [Buf() for _ in range(2)]
    PTal = [P.bf16(C) for _ in range(2)]; PTal_b = [Buf() for _ in range(2)]
    PTbh = [P.bf16(C) for _ in range(2)]; PTbh_b = [Buf() for _ in range(2)]
    PTbl = [P.bf16(C) for _ in range(2)]; PTbl_b = [Buf() for _ in range(2)]
    eacr = [P.f32(C, parts=1) for _ in range(2)]; eacr_b = [Buf() for _ in range(2)]
    eacrh = [P.bf16(C, parts=1) for _ in range(2)]; eacrh_b = [Buf() for _ in range(2)]
    eacrl = [P.bf16(C, parts=1) for _ in range(2)]; eacrl_b = [Buf() for _ in range(2)]
    tmpI = [P.f32(C) for _ in range(2)]; tmpI_b = [Buf() for _ in range(2)]
    tmpS = [P.f32(HD) for _ in range(2)]; tmpS_b = [Buf() for _ in range(2)]
    tmpO = [P.f32(C) for _ in range(2)]; tmpO_b = [Buf() for _ in range(2)]
    STf = [[P.f32(HD) for _ in range(2)] for _ in range(4)]
    STh = [[P.bf16(HD) for _ in range(2)] for _ in range(4)]
    STl = [[P.bf16(HD) for _ in range(2)] for _ in range(4)]
    STf_b = [[Buf() for _ in range(2)] for _ in range(4)]
    STh_b = [[Buf() for _ in range(2)] for _ in range(4)]
    STl_b = [[Buf() for _ in range(2)] for _ in range(4)]
    for hp in range(4):
        for pz in range(2):
            vec.op(lambda e, t=STf[hp][pz]: e.memset(t, 0.0), writes=[STf_b[hp][pz]])
            vec.op(lambda e, t=STh[hp][pz]: e.memset(t, 0.0), writes=[STh_b[hp][pz]])
            vec.op(lambda e, t=STl[hp][pz]: e.memset(t, 0.0), writes=[STl_b[hp][pz]])

    out_asm = [P.f32(512) for _ in range(2)]
    oa_buf = [Buf("oa%d" % i) for i in range(2)]
    oa_sem = [P.new_sem("oa%d" % i) for i in range(2)]
    cc_tot = [0]
    npc = getattr(d, "npc", 0) if cfg.upto != "attn" else 0
    cc_sems = [P.new_sem("ccp%d" % i) for i in range(npc)]
    ccout_bufs = [Buf("cco%d" % i) for i in range(npc)]

    pk = [0]

    def proj_bank():
        b = pk[0] % 2
        pk[0] += 1
        return b

    def emit_proj(sc):
        pi = sc % 2
        for t4 in range(4):
            slot = (sc * 4 + t4) % 2
            syn.op(lambda e, sc=sc, t4=t4, slot=slot: e.dma_start(
                out=x_t[slot], in_=d.x_seq[sc * SC + t4 * 128: sc * SC + t4 * 128 + 128, :]),
                writes=[x_buf[slot]], sem=x_sem[slot], inc=16)
            for khalf in range(2):
                b = proj_bank()
                fns = []
                for kk in range(4 * khalf, 4 * khalf + 4):
                    fns.append(lambda e, kk=kk, b=b, slot=slot: e.transpose(
                        P.psum[b][:, (kk % 4) * 128:(kk % 4) * 128 + 128],
                        x_t[slot][:, 128 * kk:128 * kk + 128], c.idf))
                ten.group(fns, reads=[x_buf[slot], c.cb], writes=[P.bank[b]])
                for kk in range(4 * khalf, 4 * khalf + 4):
                    ps = P.psum[b][:, (kk % 4) * 128:(kk % 4) * 128 + 128]
                    co = 512 * kk + 128 * t4
                    act.op(lambda e, ps=ps, co=co, pi=pi: e.copy(
                        out=xTh[pi][:, co:co + 128], in_=ps),
                        reads=[P.bank[b]], writes=[xT_hb[pi]])
                    vec.op(lambda e, ps=ps, co=co, pi=pi: e.tensor_tensor(
                        out=xTl[pi][:, co:co + 128], in0=ps,
                        in1=xTh[pi][:, co:co + 128], op=ALU.subtract),
                        reads=[P.bank[b], xT_hb[pi]], writes=[xT_lb[pi]])
        for wname, oh, ol, ob_h, ob_l, of32, ob_f in (
                ("q", qh, ql, qh_b, ql_b, None, None),
                ("k", kh, kl, kh_b, kl_b, None, None),
                ("v", None, None, None, None, vv, vv_b)):
            whi, wlo, whb, wlb = wts[wname]
            for m in range(4):
                b = proj_bank()
                fns = []
                for kk in range(KCH):
                    fns += mm3(P.psum[b][:, :],
                               whi[:, 512 * kk + 128 * m: 512 * kk + 128 * m + 128],
                               wlo[:, 512 * kk + 128 * m: 512 * kk + 128 * m + 128],
                               xTh[pi][:, 512 * kk:512 * kk + 512],
                               xTl[pi][:, 512 * kk:512 * kk + 512],
                               start=(kk == 0), stop=(kk == KCH - 1))
                ten.group(fns, reads=[whb, wlb, xT_hb[pi], xT_lb[pi]], writes=[P.bank[b]])
                if of32 is None:
                    split_psum(P, P.psum[b][:, :], P.bank[b],
                               oh[pi][:, 512 * m:512 * m + 512],
                               ol[pi][:, 512 * m:512 * m + 512], ob_h[pi], ob_l[pi])
                else:
                    act.op(lambda e, b=b, m=m: e.copy(
                        out=of32[pi][:, 512 * m:512 * m + 512], in_=P.psum[b][:, :]),
                        reads=[P.bank[b]], writes=[ob_f[pi]])
        b = proj_bank()
        fns = []
        for kk in range(KCH):
            fns += mm3(P.psum[b][0:16, :],
                       c.wgk1h[:, 16 * kk:16 * kk + 16], c.wgk1l[:, 16 * kk:16 * kk + 16],
                       xTh[pi][:, 512 * kk:512 * kk + 512], xTl[pi][:, 512 * kk:512 * kk + 512],
                       start=(kk == 0), stop=(kk == KCH - 1))
        ten.group(fns, reads=[c.wgk1h_b, c.wgk1l_b, xT_hb[pi], xT_lb[pi], c.cb],
                  writes=[P.bank[b]])
        split_psum(P, P.psum[b][0:16, :], P.bank[b], z1h[pi], z1l[pi], z1h_b[pi], z1l_b[pi])
        for m in range(4):
            b = proj_bank()
            ten.group(mm3(P.psum[b][:, :],
                          c.wgk2h[:, 128 * m:128 * m + 128], c.wgk2l[:, 128 * m:128 * m + 128],
                          z1h[pi], z1l[pi]),
                      reads=[c.wgk2h_b, c.wgk2l_b, z1h_b[pi], z1l_b[pi], c.cb],
                      writes=[P.bank[b]])
            act.op(lambda e, b=b, m=m: e.activation(
                out=sgt[pi], in_=P.psum[b][:, :], func=AF.Sigmoid,
                bias=c.bgk2[:, m:m + 1], scale=1.0),
                reads=[P.bank[b], c.cb], writes=[sgt_b[pi]])
            act.op(lambda e, m=m: e.activation(
                out=gg[pi][:, 512 * m:512 * m + 512], in_=sgt[pi], func=AF.Ln),
                reads=[sgt_b[pi]], writes=[gg_b[pi]])

    def scan_chunk(cg, hp):
        sc = cg // 4
        pi = sc % 2
        pz = cg % 2
        po = 1 - pz
        sl = (cg % 4) * 128
        o = 512 * hp + sl
        q_hi = qh[pi][:, o:o + 128]; q_lo = ql[pi][:, o:o + 128]
        k_hi = kh[pi][:, o:o + 128]; k_lo = kl[pi][:, o:o + 128]
        v_f = vv[pi][:, o:o + 128]
        g_f = gg[pi][:, o:o + 128]
        bT = P.bank[2 + pz]
        psT = P.psum[2 + pz]
        bPa, bPb = P.bank[4], P.bank[5]
        bO = P.bank[6]
        vec.op(lambda e: e.tensor_tensor_scan(out=A[pz], data0=g_f, data1=g_f,
                                              initial=0.0, op0=ALU.add, op1=ALU.bypass),
               reads=[gg_b[pi]], writes=[A_b[pz]])
        act.op(lambda e: e.activation(out=eNA[pz], in_=A[pz], func=AF.Exp, scale=-1.0 / 16),
               reads=[A_b[pz]], writes=[eNA_b[pz]])
        act.op(lambda e: e.activation(out=eacc[pz], in_=A[pz][:, 127:128], func=AF.Exp,
                                      scale=1.0 / 16),
               reads=[A_b[pz]], writes=[eacc_b[pz]])
        gps.op(lambda e: e.tensor_tensor(out=Vt[pz], in0=v_f, in1=eNA[pz], op=ALU.mult),
               reads=[vv_b[pi], eNA_b[pz]], writes=[Vt_b[pz]])
        gps.op(lambda e: e.tensor_scalar(out=Wc[pz], in0=Vt[pz], scalar1=eacc[pz],
                                         scalar2=None, op0=ALU.mult),
               reads=[Vt_b[pz], eacc_b[pz]], writes=[Wc_b[pz]])
        ten.op(lambda e: e.transpose(psT[:, 0:128], A[pz], c.idf),
               reads=[A_b[pz], c.cb], writes=[P.bank[2 + pz]])
        ten.op(lambda e: e.transpose(psT[0:1, 384:512], A[pz][:, 127:128], c.idf),
               reads=[A_b[pz], c.cb], writes=[P.bank[2 + pz]])
        ten.op(lambda e: e.transpose(psT[:, 128:256], Vt[pz], c.idf),
               reads=[Vt_b[pz], c.cb], writes=[P.bank[2 + pz]])
        ten.op(lambda e: e.transpose(psT[:, 256:384], Wc[pz], c.idf),
               reads=[Wc_b[pz], c.cb], writes=[P.bank[2 + pz]])
        act.op(lambda e: e.activation(out=eATM[pz], in_=psT[:, 0:128], func=AF.Exp,
                                      scale=1.0 / 16),
               reads=[P.bank[2 + pz]], writes=[eATM_b[pz]])
        act.op(lambda e: e.activation(out=eacr[pz], in_=psT[0:1, 384:512], func=AF.Exp,
                                      scale=1.0 / 16),
               reads=[P.bank[2 + pz]], writes=[eacr_b[pz]])
        split_sbuf(P, eacr[pz], eacr_b[pz], eacrh[pz], eacrl[pz], eacrh_b[pz], eacrl_b[pz])
        split_psum(P, psT[:, 128:256], P.bank[2 + pz], VTh[pz], VTl[pz], VTh_b[pz], VTl_b[pz])
        split_psum(P, psT[:, 256:384], P.bank[2 + pz], WTh[pz], WTl[pz], WTh_b[pz], WTl_b[pz])
        cb7 = 512 * pz
        ten.op(lambda e: e.transpose(P.psum_bf[:, cb7:cb7 + 128], k_hi, c.idb),
               reads=[kh_b[pi], c.cb], writes=[P.bank[7]])
        ten.op(lambda e: e.transpose(P.psum_bf[:, cb7 + 128:cb7 + 256], k_lo, c.idb),
               reads=[kl_b[pi], c.cb], writes=[P.bank[7]])
        act.op(lambda e: e.copy(out=kT2h[pz], in_=P.psum_bf[:, cb7:cb7 + 128]),
               reads=[P.bank[7]], writes=[kT2h_b[pz]])
        vec.op(lambda e: e.tensor_copy(kT2l[pz], P.psum_bf[:, cb7 + 128:cb7 + 256]),
               reads=[P.bank[7]], writes=[kT2l_b[pz]])
        ten.group(mm3(P.psum[4][:, 0:128], k_hi[0:64, :], k_lo[0:64, :],
                      q_hi[0:64, :], q_lo[0:64, :], tp=(0, 0)),
                  reads=[kh_b[pi], kl_b[pi], qh_b[pi], ql_b[pi]], writes=[bPa])
        ten.group(mm3(P.psum[5][:, 0:128], k_hi[64:128, :], k_lo[64:128, :],
                      q_hi[64:128, :], q_lo[64:128, :], tp=(64, 0)),
                  reads=[kh_b[pi], kl_b[pi], qh_b[pi], ql_b[pi]], writes=[bPb])
        vec.op(lambda e: e.tensor_tensor(out=PTa[pz], in0=P.psum[4][:, 0:128],
                                         in1=c.triu, op=ALU.mult),
               reads=[bPa, c.cb], writes=[PTa_b[pz]])
        vec.op(lambda e: e.tensor_tensor(out=PTb[pz], in0=P.psum[5][:, 0:128],
                                         in1=c.triu, op=ALU.mult),
               reads=[bPb, c.cb], writes=[PTb_b[pz]])
        split_sbuf(P, PTa[pz], PTa_b[pz], PTah[pz], PTal[pz], PTah_b[pz], PTal_b[pz],
                   lo_eng=P.gpsimd)
        split_sbuf(P, PTb[pz], PTb_b[pz], PTbh[pz], PTbl[pz], PTbh_b[pz], PTbl_b[pz],
                   lo_eng=P.gpsimd)
        fns = mm3(P.psum[6][:, 0:64], PTah[pz], PTal[pz], VTh[pz][:, 0:64], VTl[pz][:, 0:64])
        fns += mm3(P.psum[6][:, 64:128], PTbh[pz], PTbl[pz], VTh[pz][:, 64:128], VTl[pz][:, 64:128])
        ten.group(fns, reads=[PTah_b[pz], PTal_b[pz], PTbh_b[pz], PTbl_b[pz],
                              VTh_b[pz], VTl_b[pz]], writes=[bO])
        ten.group(mm3(P.psum[4][:, 0:64], q_hi[0:64, :], q_lo[0:64, :],
                      STh[hp][po][0:64, :], STl[hp][po][0:64, :], tp=(0, 0)),
                  reads=[qh_b[pi], ql_b[pi], STh_b[hp][po], STl_b[hp][po]], writes=[bPa])
        ten.group(mm3(P.psum[5][:, 0:64], q_hi[64:128, :], q_lo[64:128, :],
                      STh[hp][po][64:128, :], STl[hp][po][64:128, :], tp=(64, 0)),
                  reads=[qh_b[pi], ql_b[pi], STh_b[hp][po], STl_b[hp][po]], writes=[bPb])
        fns = mm3(P.psum[6][0:64, 128:192], kT2h[pz][:, 0:64], kT2l[pz][:, 0:64],
                  WTh[pz][:, 0:64], WTl[pz][:, 0:64], tp=(0, 0))
        fns += mm3(P.psum[6][64:128, 128:192], kT2h[pz][:, 64:128], kT2l[pz][:, 64:128],
                   WTh[pz][:, 64:128], WTl[pz][:, 64:128], tp=(0, 64))
        ten.group(fns, reads=[kT2h_b[pz], kT2l_b[pz], WTh_b[pz], WTl_b[pz]], writes=[bO])
        fns = [
            lambda e: e.matmul(P.psum[6][0:64, 192:256], c.ones[0:1, 0:64],
                               eacrh[pz][0:1, 0:64], start=True, stop=False, tile_position=(0, 0)),
            lambda e: e.matmul(P.psum[6][0:64, 192:256], c.ones[0:1, 0:64],
                               eacrl[pz][0:1, 0:64], start=False, stop=True, tile_position=(0, 0)),
            lambda e: e.matmul(P.psum[6][64:128, 192:256], c.ones[0:1, 64:128],
                               eacrh[pz][0:1, 64:128], start=True, stop=False, tile_position=(0, 64)),
            lambda e: e.matmul(P.psum[6][64:128, 192:256], c.ones[0:1, 64:128],
                               eacrl[pz][0:1, 64:128], start=False, stop=True, tile_position=(0, 64)),
        ]
        ten.group(fns, reads=[eacrh_b[pz], eacrl_b[pz], c.cb], writes=[bO])
        vec.op(lambda e: e.tensor_tensor(out=tmpS[pz], in0=STf[hp][po],
                                         in1=P.psum[6][:, 192:256], op=ALU.mult),
               reads=[STf_b[hp][po], bO], writes=[tmpS_b[pz]])
        vec.op(lambda e: e.tensor_tensor(out=STf[hp][pz], in0=tmpS[pz],
                                         in1=P.psum[6][:, 128:192], op=ALU.add),
               reads=[tmpS_b[pz], bO], writes=[STf_b[hp][pz]])
        split_sbuf(P, STf[hp][pz], STf_b[hp][pz], STh[hp][pz], STl[hp][pz],
                   STh_b[hp][pz], STl_b[hp][pz], lo_eng=P.gpsimd)
        slot = cg % 2
        act.op(lambda e: e.copy(out=tmpI[pz][:, 0:64], in_=P.psum[4][:, 0:64]),
               reads=[bPa], writes=[tmpI_b[pz]])
        act.op(lambda e: e.copy(out=tmpI[pz][:, 64:128], in_=P.psum[5][:, 0:64]),
               reads=[bPb], writes=[tmpI_b[pz]])
        vec.op(lambda e: e.tensor_tensor(out=tmpO[pz], in0=P.psum[6][:, 0:128],
                                         in1=tmpI[pz], op=ALU.add),
               reads=[bO, tmpI_b[pz]], writes=[tmpO_b[pz]])
        vec.op(lambda e, slot=slot, hp=hp: e.tensor_tensor(
            out=out_asm[slot][:, 128 * hp:128 * hp + 128], in0=tmpO[pz],
            in1=eATM[pz], op=ALU.mult),
            reads=[tmpO_b[pz], eATM_b[pz]], writes=[oa_buf[slot]])

    emit_proj(0)
    for sc in range(nsc):
        if sc + 1 < nsc:
            emit_proj(sc + 1)
        for c4 in range(4):
            cg = sc * 4 + c4
            for hp in range(4):
                scan_chunk(cg, hp)
            slot = cg % 2
            gps.op(lambda e, cg=cg, slot=slot: e.dma_start(
                out=d.cc_in[cg * 128:cg * 128 + 128, :], in_=out_asm[slot]),
                reads=[oa_buf[slot]], sem=oa_sem[slot], inc=16)
            cc_tot[0] += 1
            if npc and cc_tot[0] % (d.psz // 128) == 0:
                pc = cc_tot[0] // (d.psz // 128) - 1
                gps.barrier_wait([(oa_sem[i], oa_sem[i].count) for i in range(2)])
                gps.op(lambda e, pc=pc: e.collective_compute(
                    "AllGather", ALU.bypass,
                    ins=[d.cc_in[pc * d.psz:(pc + 1) * d.psz, :]],
                    outs=[d.cc_out[pc][:, :]],
                    replica_groups=[[0, 1], [2, 3], [4, 5], [6, 7]]),
                    writes=[ccout_bufs[pc]], sem=cc_sems[pc], inc=1)
            for _ in range(3):
                if split_q:
                    split_q.pop(0)()
    while split_q:
        split_q.pop(0)()

    return NS(oa_sem=oa_sem, wst_sem=wst_sem, n_even=(cc_tot[0] + 1) // 2, n_odd=cc_tot[0] // 2,
              stg=stg, stg_sem=stg_sem, stg_buf=stg_buf, ccout_bufs=ccout_bufs)

def emit_ag(P, d, cfg, at):
    cc_sem = P.new_sem("cc")
    P.gpsimd.barrier_wait([(at.oa_sem[i], at.oa_sem[i].count) for i in range(2)])
    ccout_buf = Buf("ccout")
    P.gpsimd.op(lambda e: e.collective_compute(
        "AllGather", ALU.bypass, ins=[d.cc_in[:, :]], outs=[d.cc_out[:, :]],
        replica_groups=[[0, 1], [2, 3], [4, 5], [6, 7]]),
        writes=[ccout_buf], sem=cc_sem, inc=1)
    at.ccout_buf = ccout_buf
    at.cc_sem = cc_sem
    return at


def build(cfg):
    nc = bass.Bass("TRN2", target_bir_lowering=False)
    d = declare_drams(nc, cfg)
    es = contextlib.ExitStack()
    with es:
        P = Prog(nc, es, f32_cols=cfg.f32_cols, bf16_cols=cfg.bf16_cols)
        c = emit_consts(P, d, cfg)
        P.af.mark_perm()
        P.ab.mark_perm()
        sf = (None if cfg.upto == "attn" else
              (lambda stg, stg_sem, stg_buf: make_ffn_split(
                  P, d, c, cfg, stg, stg_sem, stg_buf)))
        at = emit_attn(P, d, c, cfg, sf)
        if cfg.upto == "attn":
            P.gpsimd.barrier_wait([(at.oa_sem[i], at.oa_sem[i].count) for i in range(2)])
        else:
            wst_sem = at.wst_sem
            P.barrier(extra=[at.oa_sem[0], at.oa_sem[1], wst_sem[0], wst_sem[1],
                             c.x1g_pre])
            P.af.reset()
            P.ab.reset()
            emit_post(P, d, c, cfg, at)
            if cfg.upto == "post":
                P.barrier(extra=at.post_sems)
            else:
                P.barrier(extra=at.post_sems)
                P.af.reset()
                P.ab.reset()
                emit_ffn(P, d, c, cfg, at)
        P.finalize()
    return nc


class Cfg:
    def __init__(self, **kw):
        self.L = 8192
        self.upto = "full"
        self.dbg = False
        self.f32_cols = 22000
        self.bf16_cols = 60800
        self.__dict__.update(kw)


def make_inputs(core, inp, cfg):
    p, h = core // 2, core % 2
    L = cfg.L
    f32 = np.float32
    x = np.ascontiguousarray(inp["L0_attn_x"][p, :L], dtype=f32)
    cs = slice(512 * h, 512 * h + 512)
    hm = 1.0 if h == 0 else 0.0
    chm = np.zeros((128, 2), np.float32)
    chm[:, 0] = hm
    chm[:, 1] = 1.0 - hm
    m = {
        "x_seq": x,
        "x_own": np.ascontiguousarray(x[(L // 2) * h:(L // 2) * h + L // 2]),
        "c_hm": chm,
        "wq": np.ascontiguousarray(inp["L0_attn_Wq"][:, cs]),
        "wk": np.ascontiguousarray(inp["L0_attn_Wk"][:, cs]),
        "wv": np.ascontiguousarray(inp["L0_attn_Wv"][:, cs]),
        "wgk1": np.ascontiguousarray(inp["L0_attn_Wgk1"]),
        "wgk2": np.ascontiguousarray(inp["L0_attn_Wgk2"][:, cs]),
        "bgk2t": np.ascontiguousarray(inp["L0_attn_bgk2"][cs].reshape(4, 128).T),
        "wgp": np.ascontiguousarray(inp["L0_attn_Wg"]),
        "wop": np.ascontiguousarray(inp["L0_attn_Wout"]),
        "router": np.ascontiguousarray(inp["L0_ffn_router_weight"]),
        "gate_up": np.ascontiguousarray(inp["L0_ffn_gate_up_proj"]),
        "down": np.ascontiguousarray(inp["L0_ffn_down_proj"]),
        "dbias": np.ascontiguousarray(inp["L0_ffn_down_proj_bias"]),
        "c_idf": np.eye(128, dtype=f32),
        "c_idb": np.eye(128, dtype=BF),
        "c_triu": np.triu(np.ones((128, 128), dtype=f32)),
        "c_ones": np.ones((1, 128), dtype=BF),
        "c_eps": np.full((128, 1), 1e-5, dtype=f32),
        "c_gamma": np.tile(inp["L0_attn_gamma"][None, :], (128, 1)).astype(f32),
        "c_beta": np.tile(inp["L0_attn_beta"][None, :], (128, 1)).astype(f32),
    }
    sel = np.zeros((16, 2048), dtype=BF)
    for e in range(16):
        sel[e, e * 128:(e + 1) * 128] = 1
    m["c_sel"] = sel
    m["c_iota16"] = np.tile(np.arange(16, dtype=f32)[None, :], (128, 1))
    l17 = np.zeros((17, 17), dtype=f32)
    for jj in range(16):
        l17[:jj, jj] = 1.0
        l17[16, jj] = 1.0
        l17[jj, 16] = 1.0
    m["c_l17"] = l17
    TOWN_ = L // 2
    CAP_ = 768 if TOWN_ >= 2048 else TOWN_
    m["c_cap16"] = (np.arange(16, dtype=f32) * CAP_).reshape(16, 1)
    gub = inp["L0_ffn_gate_up_proj_bias"]
    gb = gub[:, 0::2].reshape(16, 4, 128)
    ub = gub[:, 1::2].reshape(16, 4, 128)
    m["gubias"] = np.concatenate([gb, ub], axis=1).transpose(2, 0, 1).reshape(128, 128).copy()
    return m


_BUILD_CACHE = {}


def kernel(**inputs):
    cfg = Cfg()
    key = ("full", cfg.L)
    if key not in _BUILD_CACHE:
        _BUILD_CACHE[key] = build(cfg)
    nc = _BUILD_CACHE[key]
    in_maps = [make_inputs(cc, inputs, cfg) for cc in range(8)]
    res = run_bass_kernel_spmd(nc, in_maps, core_ids=list(range(8)))
    TOWN = cfg.L // 2
    out = np.empty((4, 8192, 1024), dtype=np.float32)
    for cc in range(8):
        p, h = cc // 2, cc % 2
        out[p, TOWN * h:TOWN * h + TOWN] = res.results[cc]["y"]
    return out

def emit_post(P, d, c, cfg, at):
    L = cfg.L
    TOWN = L // 2
    ntile = TOWN // 128
    ten, vec, act, gps, syn, scl = P.tensor, P.vector, P.scalar, P.gpsimd, P.sync, P.scalar
    c.scTres = P.bf16(TOWN, parts=16); c.scTres_b = Buf("scTres")
    scTl = [P.bf16(128, parts=16) for _ in range(2)]
    scTl_b = [Buf("scTl%d" % i) for i in range(2)]
    P.af.mark_perm(); P.ab.mark_perm()
    sct_st = P.new_sem("sct_st")
    sct_st2 = [P.new_sem("sct2_%d" % i) for i in range(2)]
    c.gamma = P.f32(HID)
    c.beta = P.f32(HID)
    gb_sem = P.new_sem("gb_ld")
    gb_buf = Buf("gb")
    syn.op(lambda e: e.dma_start(out=c.gamma, in_=d.c_gamma[:, :]), sem=gb_sem, inc=16,
           writes=[gb_buf])
    syn.op(lambda e: e.dma_start(out=c.beta, in_=d.c_beta[:, :]), sem=gb_sem, inc=16,
           writes=[gb_buf])
    c.gb_buf = gb_buf
    weq_st = [P.new_sem("weq_st%d" % i) for i in range(2)]
    x1tm_st = [P.new_sem("x1tm_st%d" % i) for i in range(2)]
    TOWN_ = TOWN
    CAPp = 768 if TOWN_ >= 2048 else TOWN_
    carry = P.f32(1, parts=16); carry_b = Buf()
    x1sc_sem = [[P.new_sem("x1sc%d_%d" % (i, j)) for j in range(2)] for i in range(2)]
    vec.op(lambda e: e.memset(carry, 0.0), writes=[carry_b])

    att0 = [P.f32(HID) for _ in range(2)]
    att1 = [P.f32(HID) for _ in range(2)]
    att0_sem = [P.new_sem("at0%d" % i) for i in range(2)]
    att1_sem = [P.new_sem("at1%d" % i) for i in range(2)]
    att0_buf = [Buf() for _ in range(2)]
    att1_buf = [Buf() for _ in range(2)]
    cc2 = [co.rearrange("(s r) c -> r s c", s=2) for co in d.cc_out]
    xt2 = [P.f32(HID) for _ in range(2)]
    xt2_sem = [P.new_sem("xt2_%d" % i) for i in range(2)]
    xt2_buf = [Buf() for _ in range(2)]

    stg, stg_sem, stg_buf = xt2, xt2_sem, xt2_buf
    wgh = P.bf16(KCH * HID); wgl = P.bf16(KCH * HID)
    woh = P.bf16(KCH * HID); wol = P.bf16(KCH * HID)
    wgh_b, wgl_b, woh_b, wol_b = Buf(), Buf(), Buf(), Buf()
    for wi, (src, hi, lo, hb, lb) in enumerate(
            ((d.wgp, wgh, wgl, wgh_b, wgl_b), (d.wop, woh, wol, woh_b, wol_b))):
        for k in range(KCH):
            s = (wi * KCH + k) % 2
            syn.op(lambda e, k=k, s=s, src=src: e.dma_start(
                out=stg[s], in_=src[128 * k:128 * (k + 1), :]),
                writes=[stg_buf[s]], sem=stg_sem[s], inc=16)
            act.op(lambda e, k=k, s=s, hi=hi: e.copy(
                out=hi[:, HID * k:HID * (k + 1)], in_=stg[s]),
                reads=[stg_buf[s]], writes=[hb])
            gps.op(lambda e, k=k, s=s, hi=hi, lo=lo: e.tensor_tensor(
                out=lo[:, HID * k:HID * (k + 1)], in0=stg[s],
                in1=hi[:, HID * k:HID * (k + 1)], op=ALU.subtract),
                reads=[stg_buf[s], hb], writes=[lb])

    def mkslot(i):
        S = NS()
        S.st6 = P.f32(12); S.st6_b = Buf()
        S.mv = P.f32(2); S.mv_b = Buf()
        S.sd = P.f32(1); S.sd_b = Buf()
        S.istd = P.f32(1); S.istd_b = Buf()
        S.attg = P.f32(HID); S.attg_b = Buf()
        S.sg = P.f32(HID); S.sg_b = Buf()
        S.gated = P.f32(HID); S.gated_b = Buf()
        S.rr = P.f32(HID); S.rr_b = Buf()
        S.x1 = P.f32(HID); S.x1_b = Buf()
        S.xTh2 = P.bf16(HID); S.xTh2_b = Buf()
        S.xTl2 = P.bf16(HID); S.xTl2_b = Buf()
        S.gTh = P.bf16(HID); S.gTh_b = Buf()
        S.gTl = P.bf16(HID); S.gTl_b = Buf()
        S.x1gh = P.bf16(HID); S.x1gh_b = Buf()
        S.x1gl = P.bf16(HID); S.x1gl_b = Buf()
        S.lg = P.f32(16); S.lg_b = Buf()
        S.m1 = P.f32(1); S.m1_b = Buf()
        S.m2 = P.f32(1); S.m2_b = Buf()
        S.eq1 = P.f32(16); S.eq1_b = Buf()
        S.eq2 = P.f32(16); S.eq2_b = Buf()
        S.mskd = P.f32(16); S.mskd_b = Buf()
        S.dd = P.f32(1); S.dd_b = Buf()
        S.w1 = P.f32(1); S.w1_b = Buf()
        S.w2 = P.f32(1); S.w2_b = Buf()
        S.sct = P.f32(16); S.sct_b = Buf()
        S.scores = P.f32(16); S.scores_b = Buf()
        S.maskp = P.f32(128, parts=16); S.maskp_b = Buf()
        S.cump = P.f32(128, parts=16); S.cump_b = Buf()
        S.slotm = P.f32(128, parts=16); S.slotm_b = Buf()
        S.slotT = P.f32(16); S.slotT_b = Buf()
        S.t16a = P.f32(16); S.t16a_b = Buf()
        S.f1p = P.f32(1); S.f1p_b = Buf()
        S.f2p = P.f32(1); S.f2p_b = Buf()
        S.if1p = P.sb("if1p_%d" % i, [128, 1], mybir.dt.int32); S.if1p_b = Buf()
        S.if2p = P.sb("if2p_%d" % i, [128, 1], mybir.dt.int32); S.if2p_b = Buf()
        S.weq = P.f32(4); S.weq_b = Buf()
        S.scTl = scTl[i]; S.scTl_b = scTl_b[i]
        return S

    slots = [mkslot(0), mkslot(1)]
    dbg_sem = [P.new_sem("dbgp%d" % i) for i in range(2)]
    dbg_sem2 = [P.new_sem("dbgq%d" % i) for i in range(2)]

    pk = [0]

    def bankn():
        b = pk[0] % 6
        pk[0] += 1
        return b

    def trsplit(src_ap, src_b, hi, lo, hi_b, lo_b):
        for half in range(2):
            b = bankn()
            fns = []
            for k in range(4 * half, 4 * half + 4):
                fns.append(lambda e, k=k, b=b: e.transpose(
                    P.psum[b][:, (k % 4) * 128:(k % 4) * 128 + 128],
                    src_ap[:, 128 * k:128 * k + 128], c.idf))
            ten.group(fns, reads=[src_b, c.cb], writes=[P.bank[b]])
            for k in range(4 * half, 4 * half + 4):
                ps = P.psum[b][:, (k % 4) * 128:(k % 4) * 128 + 128]
                act.op(lambda e, ps=ps, k=k: e.copy(out=hi[:, 128 * k:128 * k + 128], in_=ps),
                       reads=[P.bank[b]], writes=[hi_b])
                vec.op(lambda e, ps=ps, k=k: e.tensor_tensor(
                    out=lo[:, 128 * k:128 * k + 128], in0=ps,
                    in1=hi[:, 128 * k:128 * k + 128], op=ALU.subtract),
                    reads=[P.bank[b], hi_b], writes=[lo_b])

    def layernorm(S, src_ap, src_b, out_ap, out_b, gamma_beta):
        vec.op(lambda e: e.bn_stats(out=S.st6[:, 0:6], in_=src_ap[:, 0:512]),
               reads=[src_b], writes=[S.st6_b])
        vec.op(lambda e: e.bn_stats(out=S.st6[:, 6:12], in_=src_ap[:, 512:1024]),
               reads=[src_b], writes=[S.st6_b])
        vec.op(lambda e: e.bn_aggr(out=S.mv, in_=S.st6), reads=[S.st6_b], writes=[S.mv_b])
        act.op(lambda e: e.activation(out=S.sd, in_=S.mv[:, 1:2], func=AF.Sqrt,
                                      bias=c.eps[:, 0:1]),
               reads=[S.mv_b, c.cb], writes=[S.sd_b])
        vec.op(lambda e: e.reciprocal(out=S.istd, in_=S.sd),
               reads=[S.sd_b], writes=[S.istd_b])
        vec.op(lambda e: e.tensor_scalar(
            out=out_ap, in0=src_ap, scalar1=S.mv[:, 0:1], scalar2=S.istd[:, 0:1],
            op0=ALU.subtract, op1=ALU.mult),
            reads=[src_b, S.mv_b, S.istd_b], writes=[out_b])
        if gamma_beta:
            gps.op(lambda e: e.tensor_tensor(out=out_ap, in0=out_ap, in1=c.gamma,
                                             op=ALU.mult),
                   reads=[out_b, c.gb_buf], writes=[out_b])
            vec.op(lambda e: e.tensor_tensor(out=out_ap, in0=out_ap, in1=c.beta,
                                             op=ALU.add),
                   reads=[out_b, c.gb_buf], writes=[out_b])

    def tile_body(t, pi, S):
        g0, r0 = (t * 128) // d.psz, (t * 128) % d.psz
        g1, r1 = (TOWN + t * 128) // d.psz, (TOWN + t * 128) % d.psz
        syn.op(lambda e, g0=g0, r0=r0, pi=pi: e.dma_start(
            out=att0[pi], in_=cc2[g0][r0:r0 + 128, :, :]),
            reads=[at.ccout_bufs[g0]], writes=[att0_buf[pi]], sem=att0_sem[pi], inc=16)
        syn.op(lambda e, g1=g1, r1=r1, pi=pi: e.dma_start(
            out=att1[pi], in_=cc2[g1][r1:r1 + 128, :, :]),
            reads=[at.ccout_bufs[g1]], writes=[att1_buf[pi]], sem=att1_sem[pi], inc=16)
        syn.op(lambda e, t=t, pi=pi: e.dma_start(
            out=xt2[pi], in_=d.x_own[t * 128:t * 128 + 128, :]),
            writes=[xt2_buf[pi]], sem=xt2_sem[pi], inc=16)
        vec.op(lambda e, pi=pi: e.tensor_scalar(
            out=att0[pi], in0=att0[pi], scalar1=c.hm[:, 0:1], scalar2=None, op0=ALU.mult),
            reads=[att0_buf[pi], c.cb], writes=[att0_buf[pi]])
        vec.op(lambda e, pi=pi: e.scalar_tensor_tensor(
            out=att0[pi], in0=att1[pi], scalar=c.hm[:, 1:2], in1=att0[pi],
            op0=ALU.mult, op1=ALU.add),
            reads=[att1_buf[pi], att0_buf[pi], c.cb], writes=[att0_buf[pi]])
        layernorm(S, att0[pi], att0_buf[pi], S.attg, S.attg_b, True)
        trsplit(xt2[pi], xt2_buf[pi], S.xTh2, S.xTl2, S.xTh2_b, S.xTl2_b)
        for half in range(2):
            b = bankn()
            fns = []
            for k in range(KCH):
                fns += mm3(P.psum[b][:, :],
                           S.xTh2[:, 128 * k:128 * k + 128],
                           S.xTl2[:, 128 * k:128 * k + 128],
                           wgh[:, HID * k + 512 * half: HID * k + 512 * half + 512],
                           wgl[:, HID * k + 512 * half: HID * k + 512 * half + 512],
                           start=(k == 0), stop=(k == KCH - 1))
            ten.group(fns, reads=[S.xTh2_b, S.xTl2_b, wgh_b, wgl_b], writes=[P.bank[b]])
            act.op(lambda e, b=b, half=half: e.activation(
                out=S.sg[:, 512 * half:512 * half + 512], in_=P.psum[b][:, :], func=AF.Silu),
                reads=[P.bank[b]], writes=[S.sg_b])
        vec.op(lambda e: e.tensor_tensor(out=S.gated, in0=S.attg, in1=S.sg, op=ALU.mult),
               reads=[S.attg_b, S.sg_b], writes=[S.gated_b])
        trsplit(S.gated, S.gated_b, S.gTh, S.gTl, S.gTh_b, S.gTl_b)
        for half in range(2):
            b = bankn()
            fns = []
            for k in range(KCH):
                fns += mm3(P.psum[b][:, :],
                           S.gTh[:, 128 * k:128 * k + 128],
                           S.gTl[:, 128 * k:128 * k + 128],
                           woh[:, HID * k + 512 * half: HID * k + 512 * half + 512],
                           wol[:, HID * k + 512 * half: HID * k + 512 * half + 512],
                           start=(k == 0), stop=(k == KCH - 1))
            ten.group(fns, reads=[S.gTh_b, S.gTl_b, woh_b, wol_b], writes=[P.bank[b]])
            vec.op(lambda e, b=b, half=half, pi=pi: e.tensor_tensor(
                out=S.rr[:, 512 * half:512 * half + 512], in0=P.psum[b][:, :],
                in1=xt2[pi][:, 512 * half:512 * half + 512], op=ALU.add),
                reads=[P.bank[b], xt2_buf[pi]], writes=[S.rr_b])
        layernorm(S, S.rr, S.rr_b, S.x1, S.x1_b, False)
        for half in range(2):
            b = bankn()
            fns = []
            for k in range(4 * half, 4 * half + 4):
                fns.append(lambda e, k=k, b=b: e.transpose(
                    P.psum[b][:, (k % 4) * 128:(k % 4) * 128 + 128],
                    S.x1[:, 128 * k:128 * k + 128], c.idf))
            ten.group(fns, reads=[S.x1_b, c.cb], writes=[P.bank[b]])
            for k in range(4 * half, 4 * half + 4):
                ps = P.psum[b][:, (k % 4) * 128:(k % 4) * 128 + 128]
                co = 128 * k
                act.op(lambda e, ps=ps, co=co: e.copy(out=S.x1gh[:, co:co + 128], in_=ps),
                       reads=[P.bank[b]], writes=[S.x1gh_b])
                vec.op(lambda e, ps=ps, co=co: e.tensor_tensor(
                    out=S.x1gl[:, co:co + 128], in0=ps, in1=S.x1gh[:, co:co + 128],
                    op=ALU.subtract),
                    reads=[P.bank[b], S.x1gh_b], writes=[S.x1gl_b])
        b = bankn()
        fns = []
        for k in range(KCH):
            fns += mm3(P.psum[b][:, 0:16],
                       S.x1gh[:, 128 * k:128 * k + 128],
                       S.x1gl[:, 128 * k:128 * k + 128],
                       c.rth[:, 16 * k:16 * k + 16], c.rtl[:, 16 * k:16 * k + 16],
                       start=(k == 0), stop=(k == KCH - 1))
        ten.group(fns, reads=[S.x1gh_b, S.x1gl_b, c.rth_b, c.rtl_b, c.cb],
                  writes=[P.bank[b]])
        vec.op(lambda e, b=b: e.tensor_copy(S.lg, P.psum[b][:, 0:16]),
               reads=[P.bank[b]], writes=[S.lg_b])
        vec.op(lambda e: e.reduce_max(out=S.m1, in_=S.lg, axis=AX.X),
               reads=[S.lg_b], writes=[S.m1_b])
        vec.op(lambda e: e.tensor_scalar(out=S.eq1, in0=S.lg, scalar1=S.m1[:, 0:1],
                                         scalar2=None, op0=ALU.is_equal),
               reads=[S.lg_b, S.m1_b], writes=[S.eq1_b])
        vec.op(lambda e: e.scalar_tensor_tensor(out=S.mskd, in0=S.eq1, scalar=-1e30,
                                                in1=S.lg, op0=ALU.mult, op1=ALU.add),
               reads=[S.eq1_b, S.lg_b], writes=[S.mskd_b])
        vec.op(lambda e: e.reduce_max(out=S.m2, in_=S.mskd, axis=AX.X),
               reads=[S.mskd_b], writes=[S.m2_b])
        vec.op(lambda e: e.tensor_scalar(out=S.eq2, in0=S.mskd, scalar1=S.m2[:, 0:1],
                                         scalar2=None, op0=ALU.is_equal),
               reads=[S.mskd_b, S.m2_b], writes=[S.eq2_b])
        vec.op(lambda e: e.tensor_tensor(out=S.dd, in0=S.m1, in1=S.m2, op=ALU.subtract),
               reads=[S.m1_b, S.m2_b], writes=[S.dd_b])
        act.op(lambda e: e.activation(out=S.w1, in_=S.dd, func=AF.Sigmoid),
               reads=[S.dd_b], writes=[S.w1_b])
        act.op(lambda e: e.activation(out=S.w2, in_=S.dd, func=AF.Sigmoid, scale=-1.0),
               reads=[S.dd_b], writes=[S.w2_b])
        vec.op(lambda e: e.tensor_scalar(out=S.sct, in0=S.eq1, scalar1=S.w1[:, 0:1],
                                         scalar2=None, op0=ALU.mult),
               reads=[S.eq1_b, S.w1_b], writes=[S.sct_b])
        vec.op(lambda e: e.scalar_tensor_tensor(out=S.scores, in0=S.eq2,
                                                scalar=S.w2[:, 0:1],
                                                in1=S.sct, op0=ALU.mult, op1=ALU.add),
               reads=[S.eq2_b, S.w2_b, S.sct_b], writes=[S.scores_b])
        b = bankn()
        ten.op(lambda e, b=b: e.transpose(P.psum[b][0:16, 0:128], S.scores, c.idf),
               reads=[S.scores_b, c.cb], writes=[P.bank[b]])
        split_psum(P, P.psum[b][0:16, 0:128], P.bank[b],
                   c.scTres[:, t * 128:t * 128 + 128], S.scTl,
                   c.scTres_b, S.scTl_b)
        scl.op(lambda e, t=t: e.dma_start(out=d.scth_sp[:, t * 128:t * 128 + 128],
                                          in_=c.scTres[:, t * 128:t * 128 + 128]),
               reads=[c.scTres_b], sem=sct_st, inc=16)
        scl.op(lambda e, t=t: e.dma_start(out=d.sctl_sp[:, t * 128:t * 128 + 128],
                                          in_=S.scTl),
               reads=[S.scTl_b], sem=sct_st2[pi], inc=16)
        b2 = bankn()
        vec.op(lambda e, b2=b2, b=b: e.tensor_scalar(
            out=S.maskp, in0=P.psum[b][0:16, 0:128], scalar1=0.0, scalar2=None,
            op0=ALU.is_gt),
            reads=[P.bank[b]], writes=[S.maskp_b])
        vec.op(lambda e: e.tensor_tensor_scan(
            out=S.cump, data0=S.maskp, data1=S.maskp, initial=carry[:, 0:1],
            op0=ALU.add, op1=ALU.bypass),
            reads=[S.maskp_b, carry_b], writes=[S.cump_b])
        vec.op(lambda e: e.tensor_copy(carry, S.cump[:, 127:128]),
               reads=[S.cump_b], writes=[carry_b])
        vec.op(lambda e: e.tensor_scalar(out=S.slotm, in0=S.cump, scalar1=c.cap16,
                                         scalar2=1.0, op0=ALU.add, op1=ALU.subtract),
               reads=[S.cump_b, c.cb], writes=[S.slotm_b])
        ten.group([lambda e, b2=b2: e.transpose(P.psum[b2][:, 0:16], S.slotm,
                                                c.idf[0:16, 0:16])],
                  reads=[S.slotm_b, c.cb], writes=[P.bank[b2]])
        act.op(lambda e, b2=b2: e.copy(out=S.slotT, in_=P.psum[b2][:, 0:16]),
               reads=[P.bank[b2]], writes=[S.slotT_b])
        vec.op(lambda e: e.tensor_tensor(out=S.t16a, in0=S.eq1, in1=S.slotT, op=ALU.mult),
               reads=[S.eq1_b, S.slotT_b], writes=[S.t16a_b])
        vec.op(lambda e: e.reduce_sum(out=S.f1p, in_=S.t16a, axis=AX.X),
               reads=[S.t16a_b], writes=[S.f1p_b])
        vec.op(lambda e: e.tensor_tensor(out=S.t16a, in0=S.eq2, in1=S.slotT, op=ALU.mult),
               reads=[S.eq2_b, S.slotT_b], writes=[S.t16a_b])
        vec.op(lambda e: e.reduce_sum(out=S.f2p, in_=S.t16a, axis=AX.X),
               reads=[S.t16a_b], writes=[S.f2p_b])
        vec.op(lambda e, CAPp=CAPp: e.tensor_scalar(
            out=S.f1p, in0=S.f1p, scalar1=float(16 * CAPp - 1), scalar2=0.0,
            op0=ALU.min, op1=ALU.max), reads=[S.f1p_b], writes=[S.f1p_b])
        vec.op(lambda e, CAPp=CAPp: e.tensor_scalar(
            out=S.f2p, in0=S.f2p, scalar1=float(16 * CAPp - 1), scalar2=0.0,
            op0=ALU.min, op1=ALU.max), reads=[S.f2p_b], writes=[S.f2p_b])
        vec.op(lambda e: e.tensor_copy(S.if1p[:, :], S.f1p),
               reads=[S.f1p_b], writes=[S.if1p_b])
        vec.op(lambda e: e.tensor_copy(S.if2p[:, :], S.f2p),
               reads=[S.f2p_b], writes=[S.if2p_b])
        gps.op(lambda e: e.indirect_dma_start(
            out=d.x1g_all[:, :],
            out_offset=bass.IndirectOffsetOnAxis(ap=S.if1p[:, :], axis=0),
            in_=S.x1, in_offset=None),
            reads=[S.x1_b, S.if1p_b, c.x1g_buf], sem=x1sc_sem[pi][0], inc=16)
        gps.op(lambda e: e.indirect_dma_start(
            out=d.x1g_all[:, :],
            out_offset=bass.IndirectOffsetOnAxis(ap=S.if2p[:, :], axis=0),
            in_=S.x1, in_offset=None),
            reads=[S.x1_b, S.if2p_b, c.x1g_buf], sem=x1sc_sem[pi][1], inc=16)
        vec.op(lambda e: e.tensor_copy(S.weq[:, 0:1], S.w1), reads=[S.w1_b], writes=[S.weq_b])
        vec.op(lambda e: e.tensor_copy(S.weq[:, 1:2], S.w2), reads=[S.w2_b], writes=[S.weq_b])
        vec.op(lambda e: e.tensor_copy(S.weq[:, 2:3], S.f1p), reads=[S.f1p_b], writes=[S.weq_b])
        vec.op(lambda e: e.tensor_copy(S.weq[:, 3:4], S.f2p), reads=[S.f2p_b], writes=[S.weq_b])
        scl.op(lambda e, t=t: e.dma_start(out=d.weq_sp[t * 128:t * 128 + 128, :],
                                          in_=S.weq[:, 0:4]),
               reads=[S.weq_b], sem=weq_st[pi], inc=16)
        scl.op(lambda e, t=t: e.dma_start(out=d.x1tm_sp[t * 128:t * 128 + 128, :], in_=S.x1),
               reads=[S.x1_b], sem=x1tm_st[pi], inc=16)
        if cfg.dbg:
            gps.op(lambda e, t=t: e.dma_start(out=d.dbg_x1[t * 128:t * 128 + 128, :],
                                              in_=S.x1),
                   reads=[S.x1_b], sem=dbg_sem[pi], inc=16)
            gps.op(lambda e, t=t: e.dma_start(out=d.dbg_sc[t * 128:t * 128 + 128, :],
                                              in_=S.scores),
                   reads=[S.scores_b], sem=dbg_sem2[pi], inc=16)

    for t in range(ntile):
        tile_body(t, t % 2, slots[t % 2])
    at.post_sems = (weq_st + x1tm_st + [sct_st] + sct_st2
                    + x1sc_sem[0] + x1sc_sem[1])

def make_ffn_split(P, d, c, cfg, stg, stg_sem, stg_buf):
    syn, act, vec, gps = P.sync, P.scalar, P.vector, P.gpsimd
    wsp = [P.bf16(2048) for _ in range(2)]
    wsp_buf = [Buf() for _ in range(2)]
    wst_sem = [P.new_sem("fst%d" % i) for i in range(2)]
    si = [0]

    def do(load_src, store_dst, deint):
        s = si[0] % 2
        si[0] += 1
        syn.op(lambda e: e.dma_start(out=stg[s], in_=load_src),
               writes=[stg_buf[s]], sem=stg_sem[s], inc=16)
        if deint:
            act.op(lambda e: e.copy(out=wsp[s][:, 0:512], in_=stg[s][:, 0:1024:2]),
                   reads=[stg_buf[s]], writes=[wsp_buf[s]])
            vec.op(lambda e: e.tensor_tensor(out=wsp[s][:, 512:1024], in0=stg[s][:, 0:1024:2],
                                             in1=wsp[s][:, 0:512], op=ALU.subtract),
                   reads=[stg_buf[s], wsp_buf[s]], writes=[wsp_buf[s]])
            act.op(lambda e: e.copy(out=wsp[s][:, 1024:1536], in_=stg[s][:, 1:1024:2]),
                   reads=[stg_buf[s], wsp_buf[s]], writes=[wsp_buf[s]])
            vec.op(lambda e: e.tensor_tensor(out=wsp[s][:, 1536:2048], in0=stg[s][:, 1:1024:2],
                                             in1=wsp[s][:, 1024:1536], op=ALU.subtract),
                   reads=[stg_buf[s], wsp_buf[s]], writes=[wsp_buf[s]])
        else:
            act.op(lambda e: e.copy(out=wsp[s][:, 0:1024], in_=stg[s]),
                   reads=[stg_buf[s]], writes=[wsp_buf[s]])
            vec.op(lambda e: e.tensor_tensor(out=wsp[s][:, 1024:2048], in0=stg[s],
                                             in1=wsp[s][:, 0:1024], op=ALU.subtract),
                   reads=[stg_buf[s], wsp_buf[s]], writes=[wsp_buf[s]])
        gps.op(lambda e: e.dma_start(out=store_dst, in_=wsp[s]),
               reads=[wsp_buf[s]], sem=wst_sem[s], inc=16)

    steps = []
    for e_ in range(16):
        for k in range(KCH):
            steps.append(lambda e_=e_, k=k: do(
                d.gate_up[e_, 128 * k:128 * (k + 1), :], d.wsgu[e_, k, :, :], True))
        for k2 in range(4):
            steps.append(lambda e_=e_, k2=k2: do(
                d.down[e_, 128 * k2:128 * (k2 + 1), :], d.wsdn[e_, k2, :, :], False))
    return wst_sem, steps


def emit_ffn(P, d, c, cfg, at):
    from concourse import library_config
    I32 = mybir.dt.int32
    U32 = mybir.dt.uint32
    L = cfg.L
    TOWN = L // 2
    CAP = 768 if TOWN >= 2048 else TOWN
    NTT = CAP // 256
    T16 = TOWN // 16
    ten, vec, act, gps, syn, scl = P.tensor, P.vector, P.scalar, P.gpsimd, P.sync, P.scalar

    csem3 = P.new_sem("f_c2")
    rawf = P.f32(HID)
    rawf_b = Buf()
    syn.op(lambda e: e.dma_start(out=rawf[0:16, :], in_=d.dbias[:, :]), sem=csem3, inc=16,
           writes=[rawf_b])
    c.dbh = P.bf16(HID, parts=16); c.dbl = P.bf16(HID, parts=16)
    c.dbh_b, c.dbl_b = Buf(), Buf()
    split_sbuf(P, rawf[0:16, :], rawf_b, c.dbh, c.dbl, c.dbh_b, c.dbl_b, lo_eng=P.gpsimd)

    xg = [P.f32(HID) for _ in range(2)]
    xg_b = [Buf() for _ in range(2)]
    xg_sem = [P.new_sem("xg%d" % i) for i in range(2)]
    x1Teh = P.bf16(KCH * CAP); x1Teh_b = Buf()
    x1Tel = P.bf16(KCH * CAP); x1Tel_b = Buf()
    wge = P.bf16(KCH * 2048); wge_b = Buf()
    wge_sem = P.new_sem("wge")
    wde = [P.bf16(4 * 2048) for _ in range(2)]
    wde_b = [Buf() for _ in range(2)]
    wde_sem = [P.new_sem("wde%d" % i) for i in range(2)]
    gatec = P.f32(256); gatec_b = Buf()
    sigc = P.f32(256); sigc_b = Buf()
    gluc = P.f32(256); gluc_b = Buf()
    upc = P.f32(256); upc_b = Buf()
    actf = P.f32(256); actf_b = Buf()
    acth = P.bf16(1024); acth_b = Buf()
    actl = P.bf16(1024); actl_b = Buf()
    yfm = P.f32(256); yfm_b = Buf()
    ytm = [P.f32(HID) for _ in range(2)]
    ytm_b = [Buf() for _ in range(2)]
    ysp_st = [P.new_sem("ysp_st0"), P.new_sem("ysp_st1")]

    def load_wge(e_):
        syn.dma_batch([lambda e, e_=e_, k=k: e.dma_start(
            out=wge[:, 2048 * k:2048 * (k + 1)], in_=d.wsgu[e_, k, :, :])
            for k in range(KCH)], writes=[wge_b], sem=wge_sem)

    def load_wde(e_):
        s = e_ % 2
        syn.dma_batch([lambda e, e_=e_, k2=k2: e.dma_start(
            out=wde[s][:, 2048 * k2:2048 * (k2 + 1)], in_=d.wsdn[e_, k2, :, :])
            for k2 in range(4)], writes=[wde_b[s]], sem=wde_sem[s])

    load_wge(0)
    load_wde(0)
    tk = [0]

    def tbank():
        b = 2 + tk[0] % 2
        tk[0] += 1
        return b

    for e_ in range(16):
        ws = e_ % 2
        for j in range(CAP // 128):
            sj = j % 2
            r0 = CAP * e_ + 128 * j
            syn.op(lambda e, r0=r0, sj=sj: e.dma_start(
                out=xg[sj], in_=d.x1g_all[r0:r0 + 128, :]),
                reads=[c.x1g_buf], writes=[xg_b[sj]], sem=xg_sem[sj], inc=16)
            for half in range(2):
                b = tbank()
                fns = []
                for k in range(4 * half, 4 * half + 4):
                    fns.append(lambda e, k=k, b=b, sj=sj: e.transpose(
                        P.psum[b][:, (k % 4) * 128:(k % 4) * 128 + 128],
                        xg[sj][:, 128 * k:128 * k + 128], c.idf))
                ten.group(fns, reads=[xg_b[sj], c.cb], writes=[P.bank[b]])
                for k in range(4 * half, 4 * half + 4):
                    ps = P.psum[b][:, (k % 4) * 128:(k % 4) * 128 + 128]
                    co = CAP * k + 128 * j
                    act.op(lambda e, ps=ps, co=co: e.copy(out=x1Teh[:, co:co + 128], in_=ps),
                           reads=[P.bank[b]], writes=[x1Teh_b])
                    vec.op(lambda e, ps=ps, co=co: e.tensor_tensor(
                        out=x1Tel[:, co:co + 128], in0=ps, in1=x1Teh[:, co:co + 128],
                        op=ALU.subtract),
                        reads=[P.bank[b], x1Teh_b], writes=[x1Tel_b])
        for tt_ in range(NTT):
            ts_ = tt_ * 256
            for g in range(4):
                bgt = 4 if (g % 2 == 0) else 6
                bup = 5
                fns = []
                for k in range(KCH):
                    fns += mm3(P.psum[bgt][:, 0:256],
                               wge[:, 2048 * k + 128 * g:2048 * k + 128 * g + 128],
                               wge[:, 2048 * k + 512 + 128 * g:2048 * k + 512 + 128 * g + 128],
                               x1Teh[:, CAP * k + ts_:CAP * k + ts_ + 256],
                               x1Tel[:, CAP * k + ts_:CAP * k + ts_ + 256],
                               start=(k == 0), stop=(k == KCH - 1))
                ten.group(fns, reads=[wge_b, x1Teh_b, x1Tel_b], writes=[P.bank[bgt]])
                fns = []
                for k in range(KCH):
                    fns += mm3(P.psum[bup][:, 0:256],
                               wge[:, 2048 * k + 1024 + 128 * g:2048 * k + 1024 + 128 * g + 128],
                               wge[:, 2048 * k + 1536 + 128 * g:2048 * k + 1536 + 128 * g + 128],
                               x1Teh[:, CAP * k + ts_:CAP * k + ts_ + 256],
                               x1Tel[:, CAP * k + ts_:CAP * k + ts_ + 256],
                               start=(k == 0), stop=(k == KCH - 1))
                ten.group(fns, reads=[wge_b, x1Teh_b, x1Tel_b], writes=[P.bank[bup]])
                bcol = 8 * e_ + g
                vec.op(lambda e, bgt=bgt, bcol=bcol: e.tensor_scalar(
                    out=gatec, in0=P.psum[bgt][:, 0:256], scalar1=c.gub[:, bcol:bcol + 1],
                    scalar2=7.0, op0=ALU.add, op1=ALU.min),
                    reads=[P.bank[bgt], c.cb], writes=[gatec_b])
                act.op(lambda e: e.activation(out=sigc, in_=gatec, func=AF.Sigmoid, scale=1.702),
                       reads=[gatec_b], writes=[sigc_b])
                vec.op(lambda e: e.tensor_tensor(out=gluc, in0=gatec, in1=sigc, op=ALU.mult),
                       reads=[gatec_b, sigc_b], writes=[gluc_b])
                ucol = 8 * e_ + 4 + g
                vec.op(lambda e, bup=bup, ucol=ucol: e.tensor_scalar(
                    out=upc, in0=P.psum[bup][:, 0:256], scalar1=c.gub1[:, ucol:ucol + 1],
                    scalar2=8.0, op0=ALU.add, op1=ALU.min),
                    reads=[P.bank[bup], c.cb, c.gub1_b], writes=[upc_b])
                vec.op(lambda e: e.scalar_tensor_tensor(
                    out=actf, in0=upc, scalar=-6.0, in1=gluc, op0=ALU.max, op1=ALU.mult),
                    reads=[upc_b, gluc_b], writes=[actf_b])
                act.op(lambda e, g=g: e.copy(out=acth[:, 256 * g:256 * (g + 1)], in_=actf),
                       reads=[actf_b], writes=[acth_b])
                vec.op(lambda e, g=g: e.tensor_tensor(
                    out=actl[:, 256 * g:256 * (g + 1)], in0=actf,
                    in1=acth[:, 256 * g:256 * (g + 1)], op=ALU.subtract),
                    reads=[actf_b, acth_b], writes=[actl_b])
            if tt_ == NTT - 1 and e_ + 1 < 16:
                load_wge(e_ + 1)
            for oc in range(KCH):
                b = oc % 2
                fns = []
                for k2 in range(4):
                    fns += mm3(P.psum[b][:, 0:256],
                               wde[ws][:, 2048 * k2 + 128 * oc:2048 * k2 + 128 * oc + 128],
                               wde[ws][:, 2048 * k2 + 1024 + 128 * oc:2048 * k2 + 1024 + 128 * oc + 128],
                               acth[:, 256 * k2:256 * (k2 + 1)],
                               actl[:, 256 * k2:256 * (k2 + 1)],
                               start=(k2 == 0), stop=(k2 == 3))
                ten.group(fns, reads=[wde_b[ws], acth_b, actl_b], writes=[P.bank[b]])
                act.op(lambda e, b=b: e.copy(out=yfm, in_=P.psum[b][:, 0:256]),
                       reads=[P.bank[b]], writes=[yfm_b])
                for sub in range(2):
                    b2 = tbank()
                    ten.group([lambda e, b2=b2, sub=sub: e.transpose(
                        P.psum[b2][:, 0:128], yfm[:, 128 * sub:128 * sub + 128], c.idf)],
                        reads=[yfm_b, c.cb], writes=[P.bank[b2]])
                    act.op(lambda e, b2=b2, sub=sub, oc=oc: e.copy(
                        out=ytm[sub][:, 128 * oc:128 * oc + 128], in_=P.psum[b2][:, 0:128]),
                        reads=[P.bank[b2]], writes=[ytm_b[sub]])
            for sub in range(2):
                r = e_ * CAP + ts_ + 128 * sub
                gps.op(lambda e, r=r, sub=sub: e.dma_start(
                    out=d.y_sp[r:r + 128, :], in_=ytm[sub]),
                    reads=[ytm_b[sub]], sem=ysp_st[sub], inc=16)
        if e_ + 1 < 16:
            load_wde(e_ + 1)

    gps.barrier_wait([(ysp_st[0], ysp_st[0].count), (ysp_st[1], ysp_st[1].count)])
    x1f = [P.f32(HID) for _ in range(2)]
    x1f_b = [Buf() for _ in range(2)]
    x1f_sem = [P.new_sem("x1f%d" % i) for i in range(2)]
    weqf = [P.f32(4) for _ in range(2)]
    weqf_b = [Buf() for _ in range(2)]
    weqf_sem = [P.new_sem("wq%d" % i) for i in range(2)]
    sch = [P.bf16(128, parts=16) for _ in range(2)]
    sch_b = [Buf() for _ in range(2)]
    sch_sem = [P.new_sem("sh%d" % i) for i in range(2)]
    scl2 = [P.bf16(128, parts=16) for _ in range(2)]
    scl2_b = [Buf() for _ in range(2)]
    scl2_sem = [P.new_sem("sc%d" % i) for i in range(2)]
    if1 = P.sb("if1", [128, 1], I32); if1_b = Buf()
    if2 = P.sb("if2", [128, 1], I32); if2_b = Buf()
    y1t = [P.f32(HID) for _ in range(2)]
    y1t_b = [Buf() for _ in range(2)]
    y1_sem = [P.new_sem("y1%d" % i) for i in range(2)]
    y2t = [P.f32(HID) for _ in range(2)]
    y2t_b = [Buf() for _ in range(2)]
    y2_sem = [P.new_sem("y2%d" % i) for i in range(2)]
    tc1 = P.f32(HID); tc1_b = Buf()
    rr2 = P.f32(HID); rr2_b = Buf()
    yt = P.f32(HID); yt_b = Buf()
    st6f = P.f32(12); st6f_b = Buf()
    mvf = P.f32(2); mvf_b = Buf()
    sdf = P.f32(1); sdf_b = Buf()
    istdf = P.f32(1); istdf_b = Buf()
    yst = P.new_sem("yst")

    for t in range(TOWN // 128):
        pi = t % 2
        syn.op(lambda e, t=t, pi=pi: e.dma_start(
            out=x1f[pi], in_=d.x1tm_sp[t * 128:t * 128 + 128, :]),
            writes=[x1f_b[pi]], sem=x1f_sem[pi], inc=16)
        syn.op(lambda e, t=t, pi=pi: e.dma_start(
            out=weqf[pi], in_=d.weq_sp[t * 128:t * 128 + 128, :]),
            writes=[weqf_b[pi]], sem=weqf_sem[pi], inc=16)
        syn.op(lambda e, t=t, pi=pi: e.dma_start(
            out=sch[pi], in_=d.scth_sp[:, t * 128:t * 128 + 128]),
            writes=[sch_b[pi]], sem=sch_sem[pi], inc=16)
        syn.op(lambda e, t=t, pi=pi: e.dma_start(
            out=scl2[pi], in_=d.sctl_sp[:, t * 128:t * 128 + 128]),
            writes=[scl2_b[pi]], sem=scl2_sem[pi], inc=16)
        vec.op(lambda e, pi=pi: e.tensor_copy(if1[:, :], weqf[pi][:, 2:3]),
               reads=[weqf_b[pi]], writes=[if1_b])
        vec.op(lambda e, pi=pi: e.tensor_copy(if2[:, :], weqf[pi][:, 3:4]),
               reads=[weqf_b[pi]], writes=[if2_b])
        gps.op(lambda e, pi=pi: e.indirect_dma_start(
            out=y1t[pi], out_offset=None, in_=d.y_sp[:, :],
            in_offset=bass.IndirectOffsetOnAxis(ap=if1[:, :], axis=0)),
            reads=[if1_b], writes=[y1t_b[pi]], sem=y1_sem[pi], inc=16)
        gps.op(lambda e, pi=pi: e.indirect_dma_start(
            out=y2t[pi], out_offset=None, in_=d.y_sp[:, :],
            in_offset=bass.IndirectOffsetOnAxis(ap=if2[:, :], axis=0)),
            reads=[if2_b], writes=[y2t_b[pi]], sem=y2_sem[pi], inc=16)
        for half in range(2):
            b2 = half
            ten.group(mm3(P.psum[b2][:, 0:512],
                          sch[pi], scl2[pi],
                          c.dbh[:, 512 * half:512 * half + 512],
                          c.dbl[:, 512 * half:512 * half + 512]),
                      reads=[sch_b[pi], scl2_b[pi], c.dbh_b, c.dbl_b], writes=[P.bank[b2]])
        vec.op(lambda e, pi=pi: e.tensor_scalar(
            out=tc1, in0=y1t[pi], scalar1=weqf[pi][:, 0:1], scalar2=None, op0=ALU.mult),
            reads=[y1t_b[pi], weqf_b[pi]], writes=[tc1_b])
        vec.op(lambda e, pi=pi: e.scalar_tensor_tensor(
            out=tc1, in0=y2t[pi], scalar=weqf[pi][:, 1:2], in1=tc1,
            op0=ALU.mult, op1=ALU.add),
            reads=[y2t_b[pi], weqf_b[pi], tc1_b], writes=[tc1_b])
        vec.op(lambda e, pi=pi: e.tensor_tensor(out=rr2, in0=tc1, in1=x1f[pi], op=ALU.add),
               reads=[tc1_b, x1f_b[pi]], writes=[rr2_b])
        for half in range(2):
            b2 = half
            vec.op(lambda e, b2=b2, half=half: e.tensor_tensor(
                out=rr2[:, 512 * half:512 * half + 512],
                in0=rr2[:, 512 * half:512 * half + 512],
                in1=P.psum[b2][:, 0:512], op=ALU.add),
                reads=[P.bank[b2], rr2_b], writes=[rr2_b])
        vec.op(lambda e: e.bn_stats(out=st6f[:, 0:6], in_=rr2[:, 0:512]),
               reads=[rr2_b], writes=[st6f_b])
        vec.op(lambda e: e.bn_stats(out=st6f[:, 6:12], in_=rr2[:, 512:1024]),
               reads=[rr2_b], writes=[st6f_b])
        vec.op(lambda e: e.bn_aggr(out=mvf, in_=st6f), reads=[st6f_b], writes=[mvf_b])
        act.op(lambda e: e.activation(out=sdf, in_=mvf[:, 1:2], func=AF.Sqrt,
                                      bias=c.eps[:, 0:1]),
               reads=[mvf_b, c.cb], writes=[sdf_b])
        vec.op(lambda e: e.reciprocal(out=istdf, in_=sdf), reads=[sdf_b], writes=[istdf_b])
        vec.op(lambda e: e.tensor_scalar(
            out=yt, in0=rr2, scalar1=mvf[:, 0:1], scalar2=istdf[:, 0:1],
            op0=ALU.subtract, op1=ALU.mult),
            reads=[rr2_b, mvf_b, istdf_b], writes=[yt_b])
        gps.op(lambda e, t=t: e.dma_start(out=d.y[t * 128:t * 128 + 128, :], in_=yt),
               reads=[yt_b], sem=yst, inc=16)
    P.gpsimd.barrier_wait([(yst, yst.count)])







# version 16 (final)
# speedup vs baseline: 1.0216x; 1.0012x over previous
import contextlib

import numpy as np
import ml_dtypes

import concourse.bass as bass
import concourse.mybir as mybir
from concourse.bass_utils import run_bass_kernel_spmd

F32 = mybir.dt.float32
BF16 = mybir.dt.bfloat16
AF = mybir.ActivationFunctionType
ALU = mybir.AluOpType
AX = mybir.AxisListType

HID = 1024
NHL = 8
HD = 64
C = 128
SC = 512
KCH = HID // 128

BF = ml_dtypes.bfloat16


class Sem:
    __slots__ = ("h", "count", "name")
    def __init__(self, h, name):
        self.h = h
        self.count = 0
        self.name = name


class Buf:
    __slots__ = ("last_w", "readers", "name")
    def __init__(self, name=""):
        self.last_w = None
        self.readers = []
        self.name = name


class Eng:
    def __init__(self, prog, name):
        self.prog = prog
        self.name = name
        self.sem = prog.new_sem("ctr_" + name)
        self.marks = {}
        self.q = []

    def _wait(self, toks):
        need = {}
        for sem, val in toks:
            sid = id(sem)
            if self.marks.get(sid, 0) >= val:
                continue
            cur = need.get(sid)
            if cur is None or cur[1] < val:
                need[sid] = (sem, val)
        for sid, (sem, val) in need.items():
            self.marks[sid] = val
            self.q.append(lambda e, s=sem.h, v=val: e.wait_ge(s, v))

    def _collect(self, reads, writes):
        toks = []
        for b in reads:
            if b.last_w:
                toks.append(b.last_w)
        for b in writes:
            if b.last_w:
                toks.append(b.last_w)
            toks.extend(b.readers)
        return toks

    def _record(self, tok, reads, writes):
        for b in reads:
            b.readers.append(tok)
        for b in writes:
            b.last_w = tok
            b.readers = []

    def op(self, fn, reads=(), writes=(), sem=None, inc=1):
        self._wait(self._collect(reads, writes))
        usem = sem if sem is not None else self.sem
        usem.count += inc
        cnt = usem.count
        def run(e, fn=fn, s=usem.h, v=inc):
            fn(e).then_inc(s, v)
        self.q.append(run)
        tok = (usem, cnt)
        self._record(tok, reads, writes)
        return tok

    def group(self, fns, reads=(), writes=()):
        self._wait(self._collect(reads, writes))
        self.sem.count += 1
        cnt = self.sem.count
        last = len(fns) - 1
        for i, fn in enumerate(fns):
            if i == last:
                self.q.append(lambda e, fn=fn, s=self.sem.h: fn(e).then_inc(s, 1))
            else:
                self.q.append(lambda e, fn=fn: fn(e))
        tok = (self.sem, cnt)
        self._record(tok, reads, writes)
        return tok

    def dma_batch(self, fns, reads=(), writes=(), sem=None):
        self._wait(self._collect(reads, writes))
        for fn in fns:
            sem.count += 16
            self.q.append(lambda e, fn=fn, s=sem.h: fn(e).then_inc(s, 16))
        tok = (sem, sem.count)
        self._record(tok, reads, writes)
        return tok

    def barrier_wait(self, toks):
        self._wait(toks)


class Arena:
    def __init__(self, t, ncols):
        self.t = t
        self.ncols = ncols
        self.off = 0
        self.perm_end = 0
        self.hwm = 0

    def alloc(self, cols, parts=128):
        o = self.off
        assert o + cols <= self.ncols, "arena overflow (%d + %d > %d)" % (o, cols, self.ncols)
        self.off += cols
        self.hwm = max(self.hwm, self.off)
        return self.t[0:parts, o:o + cols]

    def mark_perm(self):
        self.perm_end = self.off

    def push(self):
        return self.off

    def pop(self, saved):
        self.off = saved

    def reset(self):
        self.off = self.perm_end


class Prog:
    def __init__(self, nc, es, f32_cols=28000, bf16_cols=96000):
        self.nc = nc
        self.es = es
        self.all_sems = []
        self.sync = Eng(self, "sync")
        self.scalar = Eng(self, "scalar")
        self.vector = Eng(self, "vector")
        self.tensor = Eng(self, "tensor")
        self.gpsimd = Eng(self, "gpsimd")
        self.engines = [self.sync, self.scalar, self.vector, self.tensor, self.gpsimd]
        self.af = Arena(es.enter_context(nc.sbuf_tensor("arena_f32", [128, f32_cols], F32)), f32_cols)
        self.ab = Arena(es.enter_context(nc.sbuf_tensor("arena_bf16", [128, bf16_cols], BF16)), bf16_cols)
        self.psum = [es.enter_context(nc.psum_tensor("pb%d" % i, [128, 512], F32)) for i in range(7)]
        self.psum_bf = es.enter_context(nc.psum_tensor("pb7", [128, 1024], BF16))
        self.bank = [Buf("bank%d" % i) for i in range(8)]

    def sb(self, name, shape, dtype):
        return self.es.enter_context(self.nc.sbuf_tensor(name, list(shape), dtype))

    def new_sem(self, name):
        s = Sem(self.es.enter_context(self.nc.semaphore(name)), name)
        self.all_sems.append(s)
        return s

    def f32(self, cols, parts=128):
        return self.af.alloc(cols, parts)

    def bf16(self, cols, parts=128):
        return self.ab.alloc(cols, parts)

    def barrier(self, extra=()):
        toks = [(e.sem, e.sem.count) for e in self.engines]
        toks += [(s, s.count) for s in extra]
        for e in self.engines:
            e.barrier_wait(toks)

    def finalize(self):
        with self.nc.Block() as block:
            @block.sync
            def _(e):
                for f in self.sync.q:
                    f(e)
            @block.scalar
            def _(e):
                for f in self.scalar.q:
                    f(e)
            @block.vector
            def _(e):
                for f in self.vector.q:
                    f(e)
            @block.tensor
            def _(e):
                for f in self.tensor.q:
                    f(e)
            @block.gpsimd
            def _(e):
                for f in self.gpsimd.q:
                    f(e)


def mm3(psum_ap, a_hi, a_lo, b_hi, b_lo, tp=None, start=True, stop=True):
    def mk(lhs, rhs, st, sp):
        return lambda e: e.matmul(psum_ap, lhs, rhs, start=st, stop=sp, tile_position=tp)
    return [mk(a_hi, b_hi, start, False),
            mk(a_hi, b_lo, False, False),
            mk(a_lo, b_hi, False, stop)]


def split_psum(P, psum_ap, bank, hi_ap, lo_ap, hi_b, lo_b, extra=()):
    P.scalar.op(lambda e: e.copy(out=hi_ap, in_=psum_ap),
                reads=[bank, *extra], writes=[hi_b])
    P.vector.op(lambda e: e.tensor_tensor(out=lo_ap, in0=psum_ap, in1=hi_ap, op=ALU.subtract),
                reads=[bank, hi_b], writes=[lo_b])


def split_sbuf(P, src_ap, src_b, hi_ap, lo_ap, hi_b, lo_b, lo_eng=None):
    P.scalar.op(lambda e: e.copy(out=hi_ap, in_=src_ap), reads=[src_b], writes=[hi_b])
    (lo_eng or P.vector).op(
        lambda e: e.tensor_tensor(out=lo_ap, in0=src_ap, in1=hi_ap, op=ALU.subtract),
        reads=[src_b, hi_b], writes=[lo_b])


class NS:
    def __init__(self, **kw):
        self.__dict__.update(kw)


def declare_drams(nc, cfg):
    d = NS()
    L = cfg.L
    TOWN = L // 2
    ein, eout, intl = "ExternalInput", "ExternalOutput", "Internal"
    d.x_seq = nc.dram_tensor("x_seq", [L, HID], F32, kind=ein).ap()
    d.x_own = nc.dram_tensor("x_own", [TOWN, HID], F32, kind=ein).ap()
    d.c_hm = nc.dram_tensor("c_hm", [128, 2], F32, kind=ein).ap()
    d.wq = nc.dram_tensor("wq", [HID, 512], F32, kind=ein).ap()
    d.wk = nc.dram_tensor("wk", [HID, 512], F32, kind=ein).ap()
    d.wv = nc.dram_tensor("wv", [HID, 512], F32, kind=ein).ap()
    d.wgk1 = nc.dram_tensor("wgk1", [HID, 16], F32, kind=ein).ap()
    d.wgk2 = nc.dram_tensor("wgk2", [16, 512], F32, kind=ein).ap()
    d.bgk2t = nc.dram_tensor("bgk2t", [128, 4], F32, kind=ein).ap()
    d.wgp = nc.dram_tensor("wgp", [HID, HID], F32, kind=ein).ap()
    d.wop = nc.dram_tensor("wop", [HID, HID], F32, kind=ein).ap()
    d.router = nc.dram_tensor("router", [HID, 16], F32, kind=ein).ap()
    d.gate_up = nc.dram_tensor("gate_up", [16, HID, HID], F32, kind=ein).ap()
    d.down = nc.dram_tensor("down", [16, 512, HID], F32, kind=ein).ap()
    d.gubias = nc.dram_tensor("gubias", [128, 128], F32, kind=ein).ap()
    d.dbias = nc.dram_tensor("dbias", [16, HID], F32, kind=ein).ap()
    d.c_idf = nc.dram_tensor("c_idf", [128, 128], F32, kind=ein).ap()
    d.c_idb = nc.dram_tensor("c_idb", [128, 128], BF16, kind=ein).ap()
    d.c_triu = nc.dram_tensor("c_triu", [128, 128], F32, kind=ein).ap()
    d.c_ones = nc.dram_tensor("c_ones", [1, 128], BF16, kind=ein).ap()
    d.c_sel = nc.dram_tensor("c_sel", [16, 2048], BF16, kind=ein).ap()
    d.c_eps = nc.dram_tensor("c_eps", [128, 1], F32, kind=ein).ap()
    d.c_iota16 = nc.dram_tensor("c_iota16", [128, 16], F32, kind=ein).ap()
    d.c_cap16 = nc.dram_tensor("c_cap16", [16, 1], F32, kind=ein).ap()
    d.c_l17 = nc.dram_tensor("c_l17", [17, 17], F32, kind=ein).ap()
    d.c_gamma = nc.dram_tensor("c_gamma", [128, HID], F32, kind=ein).ap()
    d.c_beta = nc.dram_tensor("c_beta", [128, HID], F32, kind=ein).ap()
    d.y = nc.dram_tensor("y", [TOWN, HID], F32, kind=eout).ap()
    if cfg.upto == "attn":
        d.cc_in = nc.dram_tensor("cc_in", [L, 512], F32, kind=eout).ap()
        d.cc_out = None
    else:
        d.cc_in = nc.dram_tensor("cc_in", [L, 512], F32, kind=intl).ap()
        d.npc = 4 if L >= 8192 else 2
        d.psz = L // d.npc
        d.cc_out = [nc.dram_tensor("cc_out%d" % i, [2 * d.psz, 512], F32, kind=intl).ap()
                    for i in range(d.npc)]
    d.wsgu = nc.dram_tensor("wsgu", [16, KCH, 128, 2048], BF16, kind=intl).ap()
    d.wsdn = nc.dram_tensor("wsdn", [16, 4, 128, 2048], BF16, kind=intl).ap()
    d.x1h_sp = nc.dram_tensor("x1h_sp", [KCH, 128, TOWN], BF16, kind=intl).ap()
    d.scth_sp = nc.dram_tensor("scth_sp", [16, TOWN], BF16, kind=intl).ap()
    d.sctl_sp = nc.dram_tensor("sctl_sp", [16, TOWN], BF16, kind=intl).ap()
    d.x1tm_sp = nc.dram_tensor("x1tm_sp", [TOWN, HID], F32, kind=intl).ap()
    d.weq_sp = nc.dram_tensor("weq_sp", [TOWN, 4], F32, kind=intl).ap()
    CAP_ = 768 if TOWN >= 2048 else TOWN
    d.y_sp = nc.dram_tensor("y_sp", [16 * CAP_, HID], F32, kind=intl).ap()
    d.x1g_all = nc.dram_tensor("x1g_all", [16 * CAP_, HID], F32, kind=intl).ap()
    d.slotf_sp = nc.dram_tensor("slotf_sp", [16, TOWN], F32, kind=intl).ap()
    d.idx_sp = nc.dram_tensor("idx_sp", [16 * 48, 16], mybir.dt.int32, kind=intl).ap()
    d.x1l_sp = nc.dram_tensor("x1l_sp", [KCH, 128, TOWN], BF16, kind=intl).ap()
    if cfg.dbg:
        d.dbg_x1 = nc.dram_tensor("dbg_x1", [TOWN, HID], F32, kind=eout).ap()
        d.dbg_sc = nc.dram_tensor("dbg_sc", [TOWN, 16], F32, kind=eout).ap()
    return d


def emit_consts(P, d, cfg):
    c = NS()
    csem = P.new_sem("c_ld")
    nload = [0]

    def ld(ap_dst, ap_src):
        nload[0] += 1
        P.sync.op(lambda e: e.dma_start(out=ap_dst, in_=ap_src), sem=csem, inc=16)

    c.idf = P.f32(128); ld(c.idf, d.c_idf[:, :])
    c.idb = P.bf16(128); ld(c.idb, d.c_idb[:, :])
    c.triu = P.f32(128); ld(c.triu, d.c_triu[:, :])
    c.ones = P.bf16(128, parts=1); ld(c.ones, d.c_ones[:, :])
    c.eps = P.f32(1); ld(c.eps, d.c_eps[:, :])
    c.hm = P.f32(2); ld(c.hm, d.c_hm[:, :])
    c.iota16 = P.f32(16); ld(c.iota16, d.c_iota16[:, :])
    c.cap16 = P.f32(1, parts=16); ld(c.cap16, d.c_cap16[:, :])
    c.l17 = P.f32(17, parts=17); ld(c.l17, d.c_l17[:, :])

    c.bgk2 = P.f32(4); ld(c.bgk2, d.bgk2t[:, :])
    c.gub = P.f32(128); ld(c.gub, d.gubias[:, :])
    tok = (csem, 16 * nload[0])
    raw_buf = Buf("craw")
    raw_buf.last_w = tok
    c.ld_tok = tok
    raw = P.f32(HID)
    rawsem = P.new_sem("c_raw")

    def spl(cols, parts, loads):
        for dst_sl, src_ap in loads:
            P.sync.op(lambda e, dst_sl=dst_sl, src_ap=src_ap: e.dma_start(
                out=raw[0:parts, dst_sl], in_=src_ap), sem=rawsem, inc=16,
                reads=[], writes=[raw_buf])
        hi = P.bf16(cols, parts); lo = P.bf16(cols, parts)
        hb, lb = Buf(), Buf()
        split_sbuf(P, raw[0:parts, 0:cols], raw_buf, hi, lo, hb, lb, lo_eng=P.gpsimd)
        return hi, lo, hb, lb

    c.wgk1h, c.wgk1l, c.wgk1h_b, c.wgk1l_b = spl(
        128, 128, [(slice(16 * k, 16 * k + 16), d.wgk1[128 * k:128 * (k + 1), :])
                   for k in range(KCH)])
    c.wgk2h, c.wgk2l, c.wgk2h_b, c.wgk2l_b = spl(512, 16, [(slice(0, 512), d.wgk2[:, :])])
    c.rth, c.rtl, c.rth_b, c.rtl_b = spl(
        128, 128, [(slice(16 * k, 16 * k + 16), d.router[128 * k:128 * (k + 1), :])
                   for k in range(KCH)])
    c.gub1 = P.f32(128)
    c.gub1_b = Buf()
    cb = Buf("consts")
    cb.last_w = (csem, 16 * nload[0])
    c.cb = cb
    P.scalar.op(lambda e: e.add(out=c.gub1, in_=c.gub, add=1.0),
                reads=[cb], writes=[c.gub1_b])
    return c


def emit_attn(P, d, c, cfg, split_factory=None):
    L = cfg.L
    nsc = L // SC
    nch = L // C
    ten, vec, act, gps, syn = P.tensor, P.vector, P.scalar, P.gpsimd, P.sync

    stg = [P.f32(HID) for _ in range(2)]
    stg_sem = [P.new_sem("stg%d" % i) for i in range(2)]
    stg_buf = [Buf("stg%d" % i) for i in range(2)]
    wts = {}
    for wname, src in (("q", d.wq), ("k", d.wk), ("v", d.wv)):
        hi = P.bf16(KCH * 512); lo = P.bf16(KCH * 512)
        hb, lb = Buf(wname + "h"), Buf(wname + "l")
        for k in range(KCH):
            s = (len(wts) * KCH + k) % 2
            syn.op(lambda e, k=k, s=s, src=src: e.dma_start(
                out=stg[s][:, 0:512], in_=src[128 * k:128 * (k + 1), :]),
                reads=[], writes=[stg_buf[s]], sem=stg_sem[s], inc=16)
            act.op(lambda e, k=k, s=s, hi=hi: e.copy(out=hi[:, 512 * k:512 * (k + 1)],
                                                     in_=stg[s][:, 0:512]),
                   reads=[stg_buf[s]], writes=[hb])
            gps.op(lambda e, k=k, s=s, hi=hi, lo=lo: e.tensor_tensor(
                out=lo[:, 512 * k:512 * (k + 1)], in0=stg[s][:, 0:512],
                in1=hi[:, 512 * k:512 * (k + 1)], op=ALU.subtract),
                reads=[stg_buf[s], hb], writes=[lb])
        wts[wname] = (hi, lo, hb, lb)

    if split_factory is not None:
        wst_sem, split_q = split_factory(stg, stg_sem, stg_buf)
    else:
        wst_sem, split_q = None, []

    if cfg.upto != "attn":
        CAPa = 768 if (L // 2) >= 2048 else (L // 2)
        zz0 = P.f32(HID)
        zz0_b = Buf()
        vec.op(lambda e: e.memset(zz0, 0.0), writes=[zz0_b])
        c.x1g_buf = Buf("x1g")
        x1g_pre = P.new_sem("x1g_pre")
        P.scalar.dma_batch([lambda e, r=r: e.dma_start(
            out=d.x1g_all[128 * r:128 * (r + 1), :], in_=zz0)
            for r in range(16 * CAPa // 128)],
            reads=[zz0_b], writes=[c.x1g_buf], sem=x1g_pre)
        c.x1g_pre = x1g_pre

    x_t = [P.f32(HID) for _ in range(2)]
    x_sem = [P.new_sem("x%d" % i) for i in range(2)]
    x_buf = [Buf("x%d" % i) for i in range(2)]
    xTh = [P.bf16(KCH * 512)]; xTh.append(xTh[0])
    xTl = [P.bf16(KCH * 512)]; xTl.append(xTl[0])
    xT_hb = [Buf()]; xT_hb.append(xT_hb[0])
    xT_lb = [Buf()]; xT_lb.append(xT_lb[0])
    qh = [P.bf16(2048) for _ in range(2)]; ql = [P.bf16(2048) for _ in range(2)]
    kh = [P.bf16(2048) for _ in range(2)]; kl = [P.bf16(2048) for _ in range(2)]
    vv = [P.f32(2048) for _ in range(2)]
    gg = [P.f32(2048) for _ in range(2)]
    qh_b = [Buf() for _ in range(2)]; ql_b = [Buf() for _ in range(2)]
    kh_b = [Buf() for _ in range(2)]; kl_b = [Buf() for _ in range(2)]
    vv_b = [Buf() for _ in range(2)]; gg_b = [Buf() for _ in range(2)]
    z1h = [P.bf16(512, parts=16)]; z1h.append(z1h[0])
    z1l = [P.bf16(512, parts=16)]; z1l.append(z1l[0])
    z1h_b = [Buf()]; z1h_b.append(z1h_b[0])
    z1l_b = [Buf()]; z1l_b.append(z1l_b[0])
    sgt = [P.f32(512) for _ in range(2)]
    sgt_b = [Buf() for _ in range(2)]

    A = [P.f32(C) for _ in range(2)]; A_b = [Buf() for _ in range(2)]
    eNA = [P.f32(C) for _ in range(2)]; eNA_b = [Buf() for _ in range(2)]
    eATM = [P.f32(C) for _ in range(2)]; eATM_b = [Buf() for _ in range(2)]
    eacc = [P.f32(1) for _ in range(2)]; eacc_b = [Buf() for _ in range(2)]
    Vt = [P.f32(C) for _ in range(2)]; Vt_b = [Buf() for _ in range(2)]
    Wc = [P.f32(C) for _ in range(2)]; Wc_b = [Buf() for _ in range(2)]
    VTh = [P.bf16(C) for _ in range(2)]; VTh_b = [Buf() for _ in range(2)]
    VTl = [P.bf16(C) for _ in range(2)]; VTl_b = [Buf() for _ in range(2)]
    WTh = [P.bf16(C) for _ in range(2)]; WTh_b = [Buf() for _ in range(2)]
    WTl = [P.bf16(C) for _ in range(2)]; WTl_b = [Buf() for _ in range(2)]
    kT2h = [P.bf16(C) for _ in range(2)]; kT2h_b = [Buf() for _ in range(2)]
    kT2l = [P.bf16(C) for _ in range(2)]; kT2l_b = [Buf() for _ in range(2)]
    PTa = [P.f32(C) for _ in range(2)]; PTa_b = [Buf() for _ in range(2)]
    PTb = [P.f32(C) for _ in range(2)]; PTb_b = [Buf() for _ in range(2)]
    PTah = [P.bf16(C) for _ in range(2)]; PTah_b = [Buf() for _ in range(2)]
    PTal = [P.bf16(C) for _ in range(2)]; PTal_b = [Buf() for _ in range(2)]
    PTbh = [P.bf16(C) for _ in range(2)]; PTbh_b = [Buf() for _ in range(2)]
    PTbl = [P.bf16(C) for _ in range(2)]; PTbl_b = [Buf() for _ in range(2)]
    eacr = [P.f32(C, parts=1) for _ in range(2)]; eacr_b = [Buf() for _ in range(2)]
    eacrh = [P.bf16(C, parts=1) for _ in range(2)]; eacrh_b = [Buf() for _ in range(2)]
    eacrl = [P.bf16(C, parts=1) for _ in range(2)]; eacrl_b = [Buf() for _ in range(2)]
    tmpI = [P.f32(C) for _ in range(2)]; tmpI_b = [Buf() for _ in range(2)]
    tmpS = [P.f32(HD) for _ in range(2)]; tmpS_b = [Buf() for _ in range(2)]
    tmpO = [P.f32(C) for _ in range(2)]; tmpO_b = [Buf() for _ in range(2)]
    STf = [[P.f32(HD) for _ in range(2)] for _ in range(4)]
    STh = [[P.bf16(HD) for _ in range(2)] for _ in range(4)]
    STl = [[P.bf16(HD) for _ in range(2)] for _ in range(4)]
    STf_b = [[Buf() for _ in range(2)] for _ in range(4)]
    STh_b = [[Buf() for _ in range(2)] for _ in range(4)]
    STl_b = [[Buf() for _ in range(2)] for _ in range(4)]
    for hp in range(4):
        for pz in range(2):
            vec.op(lambda e, t=STf[hp][pz]: e.memset(t, 0.0), writes=[STf_b[hp][pz]])
            vec.op(lambda e, t=STh[hp][pz]: e.memset(t, 0.0), writes=[STh_b[hp][pz]])
            vec.op(lambda e, t=STl[hp][pz]: e.memset(t, 0.0), writes=[STl_b[hp][pz]])

    out_asm = [P.f32(512) for _ in range(2)]
    oa_buf = [Buf("oa%d" % i) for i in range(2)]
    oa_sem = [P.new_sem("oa%d" % i) for i in range(2)]
    cc_tot = [0]
    npc = getattr(d, "npc", 0) if cfg.upto != "attn" else 0
    cc_sems = [P.new_sem("ccp%d" % i) for i in range(npc)]
    ccout_bufs = [Buf("cco%d" % i) for i in range(npc)]

    pk = [0]

    def proj_bank():
        b = pk[0] % 2
        pk[0] += 1
        return b

    def emit_proj(sc):
        pi = sc % 2
        for t4 in range(4):
            slot = (sc * 4 + t4) % 2
            syn.op(lambda e, sc=sc, t4=t4, slot=slot: e.dma_start(
                out=x_t[slot], in_=d.x_seq[sc * SC + t4 * 128: sc * SC + t4 * 128 + 128, :]),
                writes=[x_buf[slot]], sem=x_sem[slot], inc=16)
            for khalf in range(2):
                b = proj_bank()
                fns = []
                for kk in range(4 * khalf, 4 * khalf + 4):
                    fns.append(lambda e, kk=kk, b=b, slot=slot: e.transpose(
                        P.psum[b][:, (kk % 4) * 128:(kk % 4) * 128 + 128],
                        x_t[slot][:, 128 * kk:128 * kk + 128], c.idf))
                ten.group(fns, reads=[x_buf[slot], c.cb], writes=[P.bank[b]])
                for kk in range(4 * khalf, 4 * khalf + 4):
                    ps = P.psum[b][:, (kk % 4) * 128:(kk % 4) * 128 + 128]
                    co = 512 * kk + 128 * t4
                    act.op(lambda e, ps=ps, co=co, pi=pi: e.copy(
                        out=xTh[pi][:, co:co + 128], in_=ps),
                        reads=[P.bank[b]], writes=[xT_hb[pi]])
                    vec.op(lambda e, ps=ps, co=co, pi=pi: e.tensor_tensor(
                        out=xTl[pi][:, co:co + 128], in0=ps,
                        in1=xTh[pi][:, co:co + 128], op=ALU.subtract),
                        reads=[P.bank[b], xT_hb[pi]], writes=[xT_lb[pi]])
        for wname, oh, ol, ob_h, ob_l, of32, ob_f in (
                ("q", qh, ql, qh_b, ql_b, None, None),
                ("k", kh, kl, kh_b, kl_b, None, None),
                ("v", None, None, None, None, vv, vv_b)):
            whi, wlo, whb, wlb = wts[wname]
            for m in range(4):
                b = proj_bank()
                fns = []
                for kk in range(KCH):
                    fns += mm3(P.psum[b][:, :],
                               whi[:, 512 * kk + 128 * m: 512 * kk + 128 * m + 128],
                               wlo[:, 512 * kk + 128 * m: 512 * kk + 128 * m + 128],
                               xTh[pi][:, 512 * kk:512 * kk + 512],
                               xTl[pi][:, 512 * kk:512 * kk + 512],
                               start=(kk == 0), stop=(kk == KCH - 1))
                ten.group(fns, reads=[whb, wlb, xT_hb[pi], xT_lb[pi]], writes=[P.bank[b]])
                if of32 is None:
                    split_psum(P, P.psum[b][:, :], P.bank[b],
                               oh[pi][:, 512 * m:512 * m + 512],
                               ol[pi][:, 512 * m:512 * m + 512], ob_h[pi], ob_l[pi])
                else:
                    act.op(lambda e, b=b, m=m: e.copy(
                        out=of32[pi][:, 512 * m:512 * m + 512], in_=P.psum[b][:, :]),
                        reads=[P.bank[b]], writes=[ob_f[pi]])
        b = proj_bank()
        fns = []
        for kk in range(KCH):
            fns += mm3(P.psum[b][0:16, :],
                       c.wgk1h[:, 16 * kk:16 * kk + 16], c.wgk1l[:, 16 * kk:16 * kk + 16],
                       xTh[pi][:, 512 * kk:512 * kk + 512], xTl[pi][:, 512 * kk:512 * kk + 512],
                       start=(kk == 0), stop=(kk == KCH - 1))
        ten.group(fns, reads=[c.wgk1h_b, c.wgk1l_b, xT_hb[pi], xT_lb[pi], c.cb],
                  writes=[P.bank[b]])
        split_psum(P, P.psum[b][0:16, :], P.bank[b], z1h[pi], z1l[pi], z1h_b[pi], z1l_b[pi])
        for m in range(4):
            b = proj_bank()
            ten.group(mm3(P.psum[b][:, :],
                          c.wgk2h[:, 128 * m:128 * m + 128], c.wgk2l[:, 128 * m:128 * m + 128],
                          z1h[pi], z1l[pi]),
                      reads=[c.wgk2h_b, c.wgk2l_b, z1h_b[pi], z1l_b[pi], c.cb],
                      writes=[P.bank[b]])
            act.op(lambda e, b=b, m=m: e.activation(
                out=sgt[pi], in_=P.psum[b][:, :], func=AF.Sigmoid,
                bias=c.bgk2[:, m:m + 1], scale=1.0),
                reads=[P.bank[b], c.cb], writes=[sgt_b[pi]])
            act.op(lambda e, m=m: e.activation(
                out=gg[pi][:, 512 * m:512 * m + 512], in_=sgt[pi], func=AF.Ln),
                reads=[sgt_b[pi]], writes=[gg_b[pi]])

    def scan_chunk(cg, hp):
        sc = cg // 4
        pi = sc % 2
        pz = cg % 2
        po = 1 - pz
        sl = (cg % 4) * 128
        o = 512 * hp + sl
        q_hi = qh[pi][:, o:o + 128]; q_lo = ql[pi][:, o:o + 128]
        k_hi = kh[pi][:, o:o + 128]; k_lo = kl[pi][:, o:o + 128]
        v_f = vv[pi][:, o:o + 128]
        g_f = gg[pi][:, o:o + 128]
        bT = P.bank[2 + pz]
        psT = P.psum[2 + pz]
        bPa, bPb = P.bank[4], P.bank[5]
        bO = P.bank[6]
        vec.op(lambda e: e.tensor_tensor_scan(out=A[pz], data0=g_f, data1=g_f,
                                              initial=0.0, op0=ALU.add, op1=ALU.bypass),
               reads=[gg_b[pi]], writes=[A_b[pz]])
        act.op(lambda e: e.activation(out=eNA[pz], in_=A[pz], func=AF.Exp, scale=-1.0 / 16),
               reads=[A_b[pz]], writes=[eNA_b[pz]])
        act.op(lambda e: e.activation(out=eacc[pz], in_=A[pz][:, 127:128], func=AF.Exp,
                                      scale=1.0 / 16),
               reads=[A_b[pz]], writes=[eacc_b[pz]])
        gps.op(lambda e: e.tensor_tensor(out=Vt[pz], in0=v_f, in1=eNA[pz], op=ALU.mult),
               reads=[vv_b[pi], eNA_b[pz]], writes=[Vt_b[pz]])
        gps.op(lambda e: e.tensor_scalar(out=Wc[pz], in0=Vt[pz], scalar1=eacc[pz],
                                         scalar2=None, op0=ALU.mult),
               reads=[Vt_b[pz], eacc_b[pz]], writes=[Wc_b[pz]])
        ten.op(lambda e: e.transpose(psT[:, 0:128], A[pz], c.idf),
               reads=[A_b[pz], c.cb], writes=[P.bank[2 + pz]])
        ten.op(lambda e: e.transpose(psT[0:1, 384:512], A[pz][:, 127:128], c.idf),
               reads=[A_b[pz], c.cb], writes=[P.bank[2 + pz]])
        ten.op(lambda e: e.transpose(psT[:, 128:256], Vt[pz], c.idf),
               reads=[Vt_b[pz], c.cb], writes=[P.bank[2 + pz]])
        ten.op(lambda e: e.transpose(psT[:, 256:384], Wc[pz], c.idf),
               reads=[Wc_b[pz], c.cb], writes=[P.bank[2 + pz]])
        act.op(lambda e: e.activation(out=eATM[pz], in_=psT[:, 0:128], func=AF.Exp,
                                      scale=1.0 / 16),
               reads=[P.bank[2 + pz]], writes=[eATM_b[pz]])
        act.op(lambda e: e.activation(out=eacr[pz], in_=psT[0:1, 384:512], func=AF.Exp,
                                      scale=1.0 / 16),
               reads=[P.bank[2 + pz]], writes=[eacr_b[pz]])
        split_sbuf(P, eacr[pz], eacr_b[pz], eacrh[pz], eacrl[pz], eacrh_b[pz], eacrl_b[pz])
        split_psum(P, psT[:, 128:256], P.bank[2 + pz], VTh[pz], VTl[pz], VTh_b[pz], VTl_b[pz])
        split_psum(P, psT[:, 256:384], P.bank[2 + pz], WTh[pz], WTl[pz], WTh_b[pz], WTl_b[pz])
        cb7 = 512 * pz
        ten.op(lambda e: e.transpose(P.psum_bf[:, cb7:cb7 + 128], k_hi, c.idb),
               reads=[kh_b[pi], c.cb], writes=[P.bank[7]])
        ten.op(lambda e: e.transpose(P.psum_bf[:, cb7 + 128:cb7 + 256], k_lo, c.idb),
               reads=[kl_b[pi], c.cb], writes=[P.bank[7]])
        act.op(lambda e: e.copy(out=kT2h[pz], in_=P.psum_bf[:, cb7:cb7 + 128]),
               reads=[P.bank[7]], writes=[kT2h_b[pz]])
        vec.op(lambda e: e.tensor_copy(kT2l[pz], P.psum_bf[:, cb7 + 128:cb7 + 256]),
               reads=[P.bank[7]], writes=[kT2l_b[pz]])
        ten.group(mm3(P.psum[4][:, 0:128], k_hi[0:64, :], k_lo[0:64, :],
                      q_hi[0:64, :], q_lo[0:64, :], tp=(0, 0)),
                  reads=[kh_b[pi], kl_b[pi], qh_b[pi], ql_b[pi]], writes=[bPa])
        ten.group(mm3(P.psum[5][:, 0:128], k_hi[64:128, :], k_lo[64:128, :],
                      q_hi[64:128, :], q_lo[64:128, :], tp=(64, 0)),
                  reads=[kh_b[pi], kl_b[pi], qh_b[pi], ql_b[pi]], writes=[bPb])
        vec.op(lambda e: e.tensor_tensor(out=PTa[pz], in0=P.psum[4][:, 0:128],
                                         in1=c.triu, op=ALU.mult),
               reads=[bPa, c.cb], writes=[PTa_b[pz]])
        vec.op(lambda e: e.tensor_tensor(out=PTb[pz], in0=P.psum[5][:, 0:128],
                                         in1=c.triu, op=ALU.mult),
               reads=[bPb, c.cb], writes=[PTb_b[pz]])
        split_sbuf(P, PTa[pz], PTa_b[pz], PTah[pz], PTal[pz], PTah_b[pz], PTal_b[pz],
                   lo_eng=P.gpsimd)
        split_sbuf(P, PTb[pz], PTb_b[pz], PTbh[pz], PTbl[pz], PTbh_b[pz], PTbl_b[pz],
                   lo_eng=P.gpsimd)
        fns = mm3(P.psum[6][:, 0:64], PTah[pz], PTal[pz], VTh[pz][:, 0:64], VTl[pz][:, 0:64])
        fns += mm3(P.psum[6][:, 64:128], PTbh[pz], PTbl[pz], VTh[pz][:, 64:128], VTl[pz][:, 64:128])
        ten.group(fns, reads=[PTah_b[pz], PTal_b[pz], PTbh_b[pz], PTbl_b[pz],
                              VTh_b[pz], VTl_b[pz]], writes=[bO])
        ten.group(mm3(P.psum[4][:, 0:64], q_hi[0:64, :], q_lo[0:64, :],
                      STh[hp][po][0:64, :], STl[hp][po][0:64, :], tp=(0, 0)),
                  reads=[qh_b[pi], ql_b[pi], STh_b[hp][po], STl_b[hp][po]], writes=[bPa])
        ten.group(mm3(P.psum[5][:, 0:64], q_hi[64:128, :], q_lo[64:128, :],
                      STh[hp][po][64:128, :], STl[hp][po][64:128, :], tp=(64, 0)),
                  reads=[qh_b[pi], ql_b[pi], STh_b[hp][po], STl_b[hp][po]], writes=[bPb])
        fns = mm3(P.psum[6][0:64, 128:192], kT2h[pz][:, 0:64], kT2l[pz][:, 0:64],
                  WTh[pz][:, 0:64], WTl[pz][:, 0:64], tp=(0, 0))
        fns += mm3(P.psum[6][64:128, 128:192], kT2h[pz][:, 64:128], kT2l[pz][:, 64:128],
                   WTh[pz][:, 64:128], WTl[pz][:, 64:128], tp=(0, 64))
        ten.group(fns, reads=[kT2h_b[pz], kT2l_b[pz], WTh_b[pz], WTl_b[pz]], writes=[bO])
        fns = [
            lambda e: e.matmul(P.psum[6][0:64, 192:256], c.ones[0:1, 0:64],
                               eacrh[pz][0:1, 0:64], start=True, stop=False, tile_position=(0, 0)),
            lambda e: e.matmul(P.psum[6][0:64, 192:256], c.ones[0:1, 0:64],
                               eacrl[pz][0:1, 0:64], start=False, stop=True, tile_position=(0, 0)),
            lambda e: e.matmul(P.psum[6][64:128, 192:256], c.ones[0:1, 64:128],
                               eacrh[pz][0:1, 64:128], start=True, stop=False, tile_position=(0, 64)),
            lambda e: e.matmul(P.psum[6][64:128, 192:256], c.ones[0:1, 64:128],
                               eacrl[pz][0:1, 64:128], start=False, stop=True, tile_position=(0, 64)),
        ]
        ten.group(fns, reads=[eacrh_b[pz], eacrl_b[pz], c.cb], writes=[bO])
        vec.op(lambda e: e.tensor_tensor(out=tmpS[pz], in0=STf[hp][po],
                                         in1=P.psum[6][:, 192:256], op=ALU.mult),
               reads=[STf_b[hp][po], bO], writes=[tmpS_b[pz]])
        vec.op(lambda e: e.tensor_tensor(out=STf[hp][pz], in0=tmpS[pz],
                                         in1=P.psum[6][:, 128:192], op=ALU.add),
               reads=[tmpS_b[pz], bO], writes=[STf_b[hp][pz]])
        split_sbuf(P, STf[hp][pz], STf_b[hp][pz], STh[hp][pz], STl[hp][pz],
                   STh_b[hp][pz], STl_b[hp][pz], lo_eng=P.gpsimd)
        slot = cg % 2
        act.op(lambda e: e.copy(out=tmpI[pz][:, 0:64], in_=P.psum[4][:, 0:64]),
               reads=[bPa], writes=[tmpI_b[pz]])
        act.op(lambda e: e.copy(out=tmpI[pz][:, 64:128], in_=P.psum[5][:, 0:64]),
               reads=[bPb], writes=[tmpI_b[pz]])
        vec.op(lambda e: e.tensor_tensor(out=tmpO[pz], in0=P.psum[6][:, 0:128],
                                         in1=tmpI[pz], op=ALU.add),
               reads=[bO, tmpI_b[pz]], writes=[tmpO_b[pz]])
        vec.op(lambda e, slot=slot, hp=hp: e.tensor_tensor(
            out=out_asm[slot][:, 128 * hp:128 * hp + 128], in0=tmpO[pz],
            in1=eATM[pz], op=ALU.mult),
            reads=[tmpO_b[pz], eATM_b[pz]], writes=[oa_buf[slot]])

    emit_proj(0)
    for sc in range(nsc):
        if sc + 1 < nsc:
            emit_proj(sc + 1)
        for c4 in range(4):
            cg = sc * 4 + c4
            for hp in range(4):
                scan_chunk(cg, hp)
            slot = cg % 2
            gps.op(lambda e, cg=cg, slot=slot: e.dma_start(
                out=d.cc_in[cg * 128:cg * 128 + 128, :], in_=out_asm[slot]),
                reads=[oa_buf[slot]], sem=oa_sem[slot], inc=16)
            cc_tot[0] += 1
            if npc and cc_tot[0] % (d.psz // 128) == 0:
                pc = cc_tot[0] // (d.psz // 128) - 1
                gps.barrier_wait([(oa_sem[i], oa_sem[i].count) for i in range(2)])
                gps.op(lambda e, pc=pc: e.collective_compute(
                    "AllGather", ALU.bypass,
                    ins=[d.cc_in[pc * d.psz:(pc + 1) * d.psz, :]],
                    outs=[d.cc_out[pc][:, :]],
                    replica_groups=[[0, 1], [2, 3], [4, 5], [6, 7]]),
                    writes=[ccout_bufs[pc]], sem=cc_sems[pc], inc=1)
            for _ in range(3):
                if split_q:
                    split_q.pop(0)()
    while split_q:
        split_q.pop(0)()

    return NS(oa_sem=oa_sem, wst_sem=wst_sem, n_even=(cc_tot[0] + 1) // 2, n_odd=cc_tot[0] // 2,
              stg=stg, stg_sem=stg_sem, stg_buf=stg_buf, ccout_bufs=ccout_bufs)

def emit_ag(P, d, cfg, at):
    cc_sem = P.new_sem("cc")
    P.gpsimd.barrier_wait([(at.oa_sem[i], at.oa_sem[i].count) for i in range(2)])
    ccout_buf = Buf("ccout")
    P.gpsimd.op(lambda e: e.collective_compute(
        "AllGather", ALU.bypass, ins=[d.cc_in[:, :]], outs=[d.cc_out[:, :]],
        replica_groups=[[0, 1], [2, 3], [4, 5], [6, 7]]),
        writes=[ccout_buf], sem=cc_sem, inc=1)
    at.ccout_buf = ccout_buf
    at.cc_sem = cc_sem
    return at


def build(cfg):
    nc = bass.Bass("TRN2", target_bir_lowering=False)
    d = declare_drams(nc, cfg)
    es = contextlib.ExitStack()
    with es:
        P = Prog(nc, es, f32_cols=cfg.f32_cols, bf16_cols=cfg.bf16_cols)
        c = emit_consts(P, d, cfg)
        P.af.mark_perm()
        P.ab.mark_perm()
        sf = (None if cfg.upto == "attn" else
              (lambda stg, stg_sem, stg_buf: make_ffn_split(
                  P, d, c, cfg, stg, stg_sem, stg_buf)))
        at = emit_attn(P, d, c, cfg, sf)
        if cfg.upto == "attn":
            P.gpsimd.barrier_wait([(at.oa_sem[i], at.oa_sem[i].count) for i in range(2)])
        else:
            wst_sem = at.wst_sem
            P.barrier(extra=[at.oa_sem[0], at.oa_sem[1], wst_sem[0], wst_sem[1],
                             c.x1g_pre])
            P.af.reset()
            P.ab.reset()
            emit_post(P, d, c, cfg, at)
            if cfg.upto == "post":
                P.barrier(extra=at.post_sems)
            else:
                P.barrier(extra=at.post_sems)
                P.af.reset()
                P.ab.reset()
                emit_ffn(P, d, c, cfg, at)
        P.finalize()
    return nc


class Cfg:
    def __init__(self, **kw):
        self.L = 8192
        self.upto = "full"
        self.dbg = False
        self.f32_cols = 22000
        self.bf16_cols = 60800
        self.__dict__.update(kw)


def make_inputs(core, inp, cfg):
    p, h = core // 2, core % 2
    L = cfg.L
    f32 = np.float32
    x = np.ascontiguousarray(inp["L0_attn_x"][p, :L], dtype=f32)
    cs = slice(512 * h, 512 * h + 512)
    hm = 1.0 if h == 0 else 0.0
    chm = np.zeros((128, 2), np.float32)
    chm[:, 0] = hm
    chm[:, 1] = 1.0 - hm
    m = {
        "x_seq": x,
        "x_own": np.ascontiguousarray(x[(L // 2) * h:(L // 2) * h + L // 2]),
        "c_hm": chm,
        "wq": np.ascontiguousarray(inp["L0_attn_Wq"][:, cs]),
        "wk": np.ascontiguousarray(inp["L0_attn_Wk"][:, cs]),
        "wv": np.ascontiguousarray(inp["L0_attn_Wv"][:, cs]),
        "wgk1": np.ascontiguousarray(inp["L0_attn_Wgk1"]),
        "wgk2": np.ascontiguousarray(inp["L0_attn_Wgk2"][:, cs]),
        "bgk2t": np.ascontiguousarray(inp["L0_attn_bgk2"][cs].reshape(4, 128).T),
        "wgp": np.ascontiguousarray(inp["L0_attn_Wg"]),
        "wop": np.ascontiguousarray(inp["L0_attn_Wout"]),
        "router": np.ascontiguousarray(inp["L0_ffn_router_weight"]),
        "gate_up": np.ascontiguousarray(inp["L0_ffn_gate_up_proj"]),
        "down": np.ascontiguousarray(inp["L0_ffn_down_proj"]),
        "dbias": np.ascontiguousarray(inp["L0_ffn_down_proj_bias"]),
        "c_idf": np.eye(128, dtype=f32),
        "c_idb": np.eye(128, dtype=BF),
        "c_triu": np.triu(np.ones((128, 128), dtype=f32)),
        "c_ones": np.ones((1, 128), dtype=BF),
        "c_eps": np.full((128, 1), 1e-5, dtype=f32),
        "c_gamma": np.tile(inp["L0_attn_gamma"][None, :], (128, 1)).astype(f32),
        "c_beta": np.tile(inp["L0_attn_beta"][None, :], (128, 1)).astype(f32),
    }
    sel = np.zeros((16, 2048), dtype=BF)
    for e in range(16):
        sel[e, e * 128:(e + 1) * 128] = 1
    m["c_sel"] = sel
    m["c_iota16"] = np.tile(np.arange(16, dtype=f32)[None, :], (128, 1))
    l17 = np.zeros((17, 17), dtype=f32)
    for jj in range(16):
        l17[:jj, jj] = 1.0
        l17[16, jj] = 1.0
        l17[jj, 16] = 1.0
    m["c_l17"] = l17
    TOWN_ = L // 2
    CAP_ = 768 if TOWN_ >= 2048 else TOWN_
    m["c_cap16"] = (np.arange(16, dtype=f32) * CAP_).reshape(16, 1)
    gub = inp["L0_ffn_gate_up_proj_bias"]
    gb = gub[:, 0::2].reshape(16, 4, 128)
    ub = gub[:, 1::2].reshape(16, 4, 128)
    m["gubias"] = np.concatenate([gb, ub], axis=1).transpose(2, 0, 1).reshape(128, 128).copy()
    return m


_BUILD_CACHE = {}


def kernel(**inputs):
    cfg = Cfg()
    key = ("full", cfg.L)
    if key not in _BUILD_CACHE:
        _BUILD_CACHE[key] = build(cfg)
    nc = _BUILD_CACHE[key]
    in_maps = [make_inputs(cc, inputs, cfg) for cc in range(8)]
    res = run_bass_kernel_spmd(nc, in_maps, core_ids=list(range(8)))
    TOWN = cfg.L // 2
    out = np.empty((4, 8192, 1024), dtype=np.float32)
    for cc in range(8):
        p, h = cc // 2, cc % 2
        out[p, TOWN * h:TOWN * h + TOWN] = res.results[cc]["y"]
    return out

def emit_post(P, d, c, cfg, at):
    L = cfg.L
    TOWN = L // 2
    ntile = TOWN // 128
    ten, vec, act, gps, syn, scl = P.tensor, P.vector, P.scalar, P.gpsimd, P.sync, P.scalar
    c.scTres = P.bf16(TOWN, parts=16); c.scTres_b = Buf("scTres")
    scTl = [P.bf16(128, parts=16) for _ in range(2)]
    scTl_b = [Buf("scTl%d" % i) for i in range(2)]
    P.af.mark_perm(); P.ab.mark_perm()
    sct_st = P.new_sem("sct_st")
    sct_st2 = [P.new_sem("sct2_%d" % i) for i in range(2)]
    c.gamma = P.f32(HID)
    c.beta = P.f32(HID)
    gb_sem = P.new_sem("gb_ld")
    gb_buf = Buf("gb")
    syn.op(lambda e: e.dma_start(out=c.gamma, in_=d.c_gamma[:, :]), sem=gb_sem, inc=16,
           writes=[gb_buf])
    syn.op(lambda e: e.dma_start(out=c.beta, in_=d.c_beta[:, :]), sem=gb_sem, inc=16,
           writes=[gb_buf])
    c.gb_buf = gb_buf
    weq_st = [P.new_sem("weq_st%d" % i) for i in range(2)]
    x1tm_st = [P.new_sem("x1tm_st%d" % i) for i in range(2)]
    TOWN_ = TOWN
    CAPp = 768 if TOWN_ >= 2048 else TOWN_
    carry = P.f32(1, parts=16); carry_b = Buf()
    x1sc_sem = [[P.new_sem("x1sc%d_%d" % (i, j)) for j in range(2)] for i in range(2)]
    vec.op(lambda e: e.memset(carry, 0.0), writes=[carry_b])

    att0 = [P.f32(HID) for _ in range(2)]
    att1 = [P.f32(HID) for _ in range(2)]
    att0_sem = [P.new_sem("at0%d" % i) for i in range(2)]
    att1_sem = [P.new_sem("at1%d" % i) for i in range(2)]
    att0_buf = [Buf() for _ in range(2)]
    att1_buf = [Buf() for _ in range(2)]
    cc2 = [co.rearrange("(s r) c -> r s c", s=2) for co in d.cc_out]
    xt2 = [P.f32(HID) for _ in range(2)]
    xt2_sem = [P.new_sem("xt2_%d" % i) for i in range(2)]
    xt2_buf = [Buf() for _ in range(2)]

    stg, stg_sem, stg_buf = xt2, xt2_sem, xt2_buf
    wgh = P.bf16(KCH * HID); wgl = P.bf16(KCH * HID)
    woh = P.bf16(KCH * HID); wol = P.bf16(KCH * HID)
    wgh_b, wgl_b, woh_b, wol_b = Buf(), Buf(), Buf(), Buf()
    for wi, (src, hi, lo, hb, lb) in enumerate(
            ((d.wgp, wgh, wgl, wgh_b, wgl_b), (d.wop, woh, wol, woh_b, wol_b))):
        for k in range(KCH):
            s = (wi * KCH + k) % 2
            syn.op(lambda e, k=k, s=s, src=src: e.dma_start(
                out=stg[s], in_=src[128 * k:128 * (k + 1), :]),
                writes=[stg_buf[s]], sem=stg_sem[s], inc=16)
            act.op(lambda e, k=k, s=s, hi=hi: e.copy(
                out=hi[:, HID * k:HID * (k + 1)], in_=stg[s]),
                reads=[stg_buf[s]], writes=[hb])
            gps.op(lambda e, k=k, s=s, hi=hi, lo=lo: e.tensor_tensor(
                out=lo[:, HID * k:HID * (k + 1)], in0=stg[s],
                in1=hi[:, HID * k:HID * (k + 1)], op=ALU.subtract),
                reads=[stg_buf[s], hb], writes=[lb])

    def mkslot(i):
        S = NS()
        S.st6 = P.f32(12); S.st6_b = Buf()
        S.mv = P.f32(2); S.mv_b = Buf()
        S.sd = P.f32(1); S.sd_b = Buf()
        S.istd = P.f32(1); S.istd_b = Buf()
        S.attg = P.f32(HID); S.attg_b = Buf()
        S.sg = P.f32(HID); S.sg_b = Buf()
        S.gated = P.f32(HID); S.gated_b = Buf()
        S.rr = P.f32(HID); S.rr_b = Buf()
        S.x1 = P.f32(HID); S.x1_b = Buf()
        S.xTh2 = P.bf16(HID); S.xTh2_b = Buf()
        S.xTl2 = P.bf16(HID); S.xTl2_b = Buf()
        S.gTh = P.bf16(HID); S.gTh_b = Buf()
        S.gTl = P.bf16(HID); S.gTl_b = Buf()
        S.x1gh = P.bf16(HID); S.x1gh_b = Buf()
        S.x1gl = P.bf16(HID); S.x1gl_b = Buf()
        S.lg = P.f32(16); S.lg_b = Buf()
        S.m1 = P.f32(1); S.m1_b = Buf()
        S.m2 = P.f32(1); S.m2_b = Buf()
        S.eq1 = P.f32(16); S.eq1_b = Buf()
        S.eq2 = P.f32(16); S.eq2_b = Buf()
        S.mskd = P.f32(16); S.mskd_b = Buf()
        S.dd = P.f32(1); S.dd_b = Buf()
        S.w1 = P.f32(1); S.w1_b = Buf()
        S.w2 = P.f32(1); S.w2_b = Buf()
        S.sct = P.f32(16); S.sct_b = Buf()
        S.scores = P.f32(16); S.scores_b = Buf()
        S.maskp = P.f32(128, parts=16); S.maskp_b = Buf()
        S.cump = P.f32(128, parts=16); S.cump_b = Buf()
        S.slotm = P.f32(128, parts=16); S.slotm_b = Buf()
        S.slotT = P.f32(16); S.slotT_b = Buf()
        S.t16a = P.f32(16); S.t16a_b = Buf()
        S.f1p = P.f32(1); S.f1p_b = Buf()
        S.f2p = P.f32(1); S.f2p_b = Buf()
        S.if1p = P.sb("if1p_%d" % i, [128, 1], mybir.dt.int32); S.if1p_b = Buf()
        S.if2p = P.sb("if2p_%d" % i, [128, 1], mybir.dt.int32); S.if2p_b = Buf()
        S.weq = P.f32(4); S.weq_b = Buf()
        S.scTl = scTl[i]; S.scTl_b = scTl_b[i]
        return S

    slots = [mkslot(0), mkslot(1)]
    dbg_sem = [P.new_sem("dbgp%d" % i) for i in range(2)]
    dbg_sem2 = [P.new_sem("dbgq%d" % i) for i in range(2)]

    pk = [0]

    def bankn():
        b = pk[0] % 6
        pk[0] += 1
        return b

    def trsplit(src_ap, src_b, hi, lo, hi_b, lo_b):
        for half in range(2):
            b = bankn()
            fns = []
            for k in range(4 * half, 4 * half + 4):
                fns.append(lambda e, k=k, b=b: e.transpose(
                    P.psum[b][:, (k % 4) * 128:(k % 4) * 128 + 128],
                    src_ap[:, 128 * k:128 * k + 128], c.idf))
            ten.group(fns, reads=[src_b, c.cb], writes=[P.bank[b]])
            for k in range(4 * half, 4 * half + 4):
                ps = P.psum[b][:, (k % 4) * 128:(k % 4) * 128 + 128]
                act.op(lambda e, ps=ps, k=k: e.copy(out=hi[:, 128 * k:128 * k + 128], in_=ps),
                       reads=[P.bank[b]], writes=[hi_b])
                vec.op(lambda e, ps=ps, k=k: e.tensor_tensor(
                    out=lo[:, 128 * k:128 * k + 128], in0=ps,
                    in1=hi[:, 128 * k:128 * k + 128], op=ALU.subtract),
                    reads=[P.bank[b], hi_b], writes=[lo_b])

    def layernorm(S, src_ap, src_b, out_ap, out_b, gamma_beta):
        vec.op(lambda e: e.bn_stats(out=S.st6[:, 0:6], in_=src_ap[:, 0:512]),
               reads=[src_b], writes=[S.st6_b])
        vec.op(lambda e: e.bn_stats(out=S.st6[:, 6:12], in_=src_ap[:, 512:1024]),
               reads=[src_b], writes=[S.st6_b])
        vec.op(lambda e: e.bn_aggr(out=S.mv, in_=S.st6), reads=[S.st6_b], writes=[S.mv_b])
        act.op(lambda e: e.activation(out=S.sd, in_=S.mv[:, 1:2], func=AF.Sqrt,
                                      bias=c.eps[:, 0:1]),
               reads=[S.mv_b, c.cb], writes=[S.sd_b])
        vec.op(lambda e: e.reciprocal(out=S.istd, in_=S.sd),
               reads=[S.sd_b], writes=[S.istd_b])
        vec.op(lambda e: e.tensor_scalar(
            out=out_ap, in0=src_ap, scalar1=S.mv[:, 0:1], scalar2=S.istd[:, 0:1],
            op0=ALU.subtract, op1=ALU.mult),
            reads=[src_b, S.mv_b, S.istd_b], writes=[out_b])
        if gamma_beta:
            gps.op(lambda e: e.tensor_tensor(out=out_ap, in0=out_ap, in1=c.gamma,
                                             op=ALU.mult),
                   reads=[out_b, c.gb_buf], writes=[out_b])
            vec.op(lambda e: e.tensor_tensor(out=out_ap, in0=out_ap, in1=c.beta,
                                             op=ALU.add),
                   reads=[out_b, c.gb_buf], writes=[out_b])

    def tile_body(t, pi, S):
        g0, r0 = (t * 128) // d.psz, (t * 128) % d.psz
        g1, r1 = (TOWN + t * 128) // d.psz, (TOWN + t * 128) % d.psz
        syn.op(lambda e, g0=g0, r0=r0, pi=pi: e.dma_start(
            out=att0[pi], in_=cc2[g0][r0:r0 + 128, :, :]),
            reads=[at.ccout_bufs[g0]], writes=[att0_buf[pi]], sem=att0_sem[pi], inc=16)
        syn.op(lambda e, g1=g1, r1=r1, pi=pi: e.dma_start(
            out=att1[pi], in_=cc2[g1][r1:r1 + 128, :, :]),
            reads=[at.ccout_bufs[g1]], writes=[att1_buf[pi]], sem=att1_sem[pi], inc=16)
        syn.op(lambda e, t=t, pi=pi: e.dma_start(
            out=xt2[pi], in_=d.x_own[t * 128:t * 128 + 128, :]),
            writes=[xt2_buf[pi]], sem=xt2_sem[pi], inc=16)
        vec.op(lambda e, pi=pi: e.tensor_scalar(
            out=att0[pi], in0=att0[pi], scalar1=c.hm[:, 0:1], scalar2=None, op0=ALU.mult),
            reads=[att0_buf[pi], c.cb], writes=[att0_buf[pi]])
        vec.op(lambda e, pi=pi: e.scalar_tensor_tensor(
            out=att0[pi], in0=att1[pi], scalar=c.hm[:, 1:2], in1=att0[pi],
            op0=ALU.mult, op1=ALU.add),
            reads=[att1_buf[pi], att0_buf[pi], c.cb], writes=[att0_buf[pi]])
        layernorm(S, att0[pi], att0_buf[pi], S.attg, S.attg_b, True)
        trsplit(xt2[pi], xt2_buf[pi], S.xTh2, S.xTl2, S.xTh2_b, S.xTl2_b)
        for half in range(2):
            b = bankn()
            fns = []
            for k in range(KCH):
                fns += mm3(P.psum[b][:, :],
                           S.xTh2[:, 128 * k:128 * k + 128],
                           S.xTl2[:, 128 * k:128 * k + 128],
                           wgh[:, HID * k + 512 * half: HID * k + 512 * half + 512],
                           wgl[:, HID * k + 512 * half: HID * k + 512 * half + 512],
                           start=(k == 0), stop=(k == KCH - 1))
            ten.group(fns, reads=[S.xTh2_b, S.xTl2_b, wgh_b, wgl_b], writes=[P.bank[b]])
            act.op(lambda e, b=b, half=half: e.activation(
                out=S.sg[:, 512 * half:512 * half + 512], in_=P.psum[b][:, :], func=AF.Silu),
                reads=[P.bank[b]], writes=[S.sg_b])
        vec.op(lambda e: e.tensor_tensor(out=S.gated, in0=S.attg, in1=S.sg, op=ALU.mult),
               reads=[S.attg_b, S.sg_b], writes=[S.gated_b])
        trsplit(S.gated, S.gated_b, S.gTh, S.gTl, S.gTh_b, S.gTl_b)
        for half in range(2):
            b = bankn()
            fns = []
            for k in range(KCH):
                fns += mm3(P.psum[b][:, :],
                           S.gTh[:, 128 * k:128 * k + 128],
                           S.gTl[:, 128 * k:128 * k + 128],
                           woh[:, HID * k + 512 * half: HID * k + 512 * half + 512],
                           wol[:, HID * k + 512 * half: HID * k + 512 * half + 512],
                           start=(k == 0), stop=(k == KCH - 1))
            ten.group(fns, reads=[S.gTh_b, S.gTl_b, woh_b, wol_b], writes=[P.bank[b]])
            vec.op(lambda e, b=b, half=half, pi=pi: e.tensor_tensor(
                out=S.rr[:, 512 * half:512 * half + 512], in0=P.psum[b][:, :],
                in1=xt2[pi][:, 512 * half:512 * half + 512], op=ALU.add),
                reads=[P.bank[b], xt2_buf[pi]], writes=[S.rr_b])
        layernorm(S, S.rr, S.rr_b, S.x1, S.x1_b, False)
        for half in range(2):
            b = bankn()
            fns = []
            for k in range(4 * half, 4 * half + 4):
                fns.append(lambda e, k=k, b=b: e.transpose(
                    P.psum[b][:, (k % 4) * 128:(k % 4) * 128 + 128],
                    S.x1[:, 128 * k:128 * k + 128], c.idf))
            ten.group(fns, reads=[S.x1_b, c.cb], writes=[P.bank[b]])
            for k in range(4 * half, 4 * half + 4):
                ps = P.psum[b][:, (k % 4) * 128:(k % 4) * 128 + 128]
                co = 128 * k
                act.op(lambda e, ps=ps, co=co: e.copy(out=S.x1gh[:, co:co + 128], in_=ps),
                       reads=[P.bank[b]], writes=[S.x1gh_b])
                vec.op(lambda e, ps=ps, co=co: e.tensor_tensor(
                    out=S.x1gl[:, co:co + 128], in0=ps, in1=S.x1gh[:, co:co + 128],
                    op=ALU.subtract),
                    reads=[P.bank[b], S.x1gh_b], writes=[S.x1gl_b])
        b = bankn()
        fns = []
        for k in range(KCH):
            fns += mm3(P.psum[b][:, 0:16],
                       S.x1gh[:, 128 * k:128 * k + 128],
                       S.x1gl[:, 128 * k:128 * k + 128],
                       c.rth[:, 16 * k:16 * k + 16], c.rtl[:, 16 * k:16 * k + 16],
                       start=(k == 0), stop=(k == KCH - 1))
        ten.group(fns, reads=[S.x1gh_b, S.x1gl_b, c.rth_b, c.rtl_b, c.cb],
                  writes=[P.bank[b]])
        vec.op(lambda e, b=b: e.tensor_copy(S.lg, P.psum[b][:, 0:16]),
               reads=[P.bank[b]], writes=[S.lg_b])
        vec.op(lambda e: e.reduce_max(out=S.m1, in_=S.lg, axis=AX.X),
               reads=[S.lg_b], writes=[S.m1_b])
        vec.op(lambda e: e.tensor_scalar(out=S.eq1, in0=S.lg, scalar1=S.m1[:, 0:1],
                                         scalar2=None, op0=ALU.is_equal),
               reads=[S.lg_b, S.m1_b], writes=[S.eq1_b])
        vec.op(lambda e: e.scalar_tensor_tensor(out=S.mskd, in0=S.eq1, scalar=-1e30,
                                                in1=S.lg, op0=ALU.mult, op1=ALU.add),
               reads=[S.eq1_b, S.lg_b], writes=[S.mskd_b])
        vec.op(lambda e: e.reduce_max(out=S.m2, in_=S.mskd, axis=AX.X),
               reads=[S.mskd_b], writes=[S.m2_b])
        vec.op(lambda e: e.tensor_scalar(out=S.eq2, in0=S.mskd, scalar1=S.m2[:, 0:1],
                                         scalar2=None, op0=ALU.is_equal),
               reads=[S.mskd_b, S.m2_b], writes=[S.eq2_b])
        vec.op(lambda e: e.tensor_tensor(out=S.dd, in0=S.m1, in1=S.m2, op=ALU.subtract),
               reads=[S.m1_b, S.m2_b], writes=[S.dd_b])
        act.op(lambda e: e.activation(out=S.w1, in_=S.dd, func=AF.Sigmoid),
               reads=[S.dd_b], writes=[S.w1_b])
        act.op(lambda e: e.activation(out=S.w2, in_=S.dd, func=AF.Sigmoid, scale=-1.0),
               reads=[S.dd_b], writes=[S.w2_b])
        vec.op(lambda e: e.tensor_scalar(out=S.sct, in0=S.eq1, scalar1=S.w1[:, 0:1],
                                         scalar2=None, op0=ALU.mult),
               reads=[S.eq1_b, S.w1_b], writes=[S.sct_b])
        vec.op(lambda e: e.scalar_tensor_tensor(out=S.scores, in0=S.eq2,
                                                scalar=S.w2[:, 0:1],
                                                in1=S.sct, op0=ALU.mult, op1=ALU.add),
               reads=[S.eq2_b, S.w2_b, S.sct_b], writes=[S.scores_b])
        b = bankn()
        ten.op(lambda e, b=b: e.transpose(P.psum[b][0:16, 0:128], S.scores, c.idf),
               reads=[S.scores_b, c.cb], writes=[P.bank[b]])
        split_psum(P, P.psum[b][0:16, 0:128], P.bank[b],
                   c.scTres[:, t * 128:t * 128 + 128], S.scTl,
                   c.scTres_b, S.scTl_b)
        scl.op(lambda e, t=t: e.dma_start(out=d.scth_sp[:, t * 128:t * 128 + 128],
                                          in_=c.scTres[:, t * 128:t * 128 + 128]),
               reads=[c.scTres_b], sem=sct_st, inc=16)
        scl.op(lambda e, t=t: e.dma_start(out=d.sctl_sp[:, t * 128:t * 128 + 128],
                                          in_=S.scTl),
               reads=[S.scTl_b], sem=sct_st2[pi], inc=16)
        b2 = bankn()
        vec.op(lambda e, b2=b2, b=b: e.tensor_scalar(
            out=S.maskp, in0=P.psum[b][0:16, 0:128], scalar1=0.0, scalar2=None,
            op0=ALU.is_gt),
            reads=[P.bank[b]], writes=[S.maskp_b])
        vec.op(lambda e: e.tensor_tensor_scan(
            out=S.cump, data0=S.maskp, data1=S.maskp, initial=carry[:, 0:1],
            op0=ALU.add, op1=ALU.bypass),
            reads=[S.maskp_b, carry_b], writes=[S.cump_b])
        vec.op(lambda e: e.tensor_copy(carry, S.cump[:, 127:128]),
               reads=[S.cump_b], writes=[carry_b])
        vec.op(lambda e: e.tensor_scalar(out=S.slotm, in0=S.cump, scalar1=c.cap16,
                                         scalar2=1.0, op0=ALU.add, op1=ALU.subtract),
               reads=[S.cump_b, c.cb], writes=[S.slotm_b])
        ten.group([lambda e, b2=b2: e.transpose(P.psum[b2][:, 0:16], S.slotm,
                                                c.idf[0:16, 0:16])],
                  reads=[S.slotm_b, c.cb], writes=[P.bank[b2]])
        act.op(lambda e, b2=b2: e.copy(out=S.slotT, in_=P.psum[b2][:, 0:16]),
               reads=[P.bank[b2]], writes=[S.slotT_b])
        vec.op(lambda e: e.tensor_tensor(out=S.t16a, in0=S.eq1, in1=S.slotT, op=ALU.mult),
               reads=[S.eq1_b, S.slotT_b], writes=[S.t16a_b])
        vec.op(lambda e: e.reduce_sum(out=S.f1p, in_=S.t16a, axis=AX.X),
               reads=[S.t16a_b], writes=[S.f1p_b])
        vec.op(lambda e: e.tensor_tensor(out=S.t16a, in0=S.eq2, in1=S.slotT, op=ALU.mult),
               reads=[S.eq2_b, S.slotT_b], writes=[S.t16a_b])
        vec.op(lambda e: e.reduce_sum(out=S.f2p, in_=S.t16a, axis=AX.X),
               reads=[S.t16a_b], writes=[S.f2p_b])
        vec.op(lambda e, CAPp=CAPp: e.tensor_scalar(
            out=S.f1p, in0=S.f1p, scalar1=float(16 * CAPp - 1), scalar2=0.0,
            op0=ALU.min, op1=ALU.max), reads=[S.f1p_b], writes=[S.f1p_b])
        vec.op(lambda e, CAPp=CAPp: e.tensor_scalar(
            out=S.f2p, in0=S.f2p, scalar1=float(16 * CAPp - 1), scalar2=0.0,
            op0=ALU.min, op1=ALU.max), reads=[S.f2p_b], writes=[S.f2p_b])
        vec.op(lambda e: e.tensor_copy(S.if1p[:, :], S.f1p),
               reads=[S.f1p_b], writes=[S.if1p_b])
        vec.op(lambda e: e.tensor_copy(S.if2p[:, :], S.f2p),
               reads=[S.f2p_b], writes=[S.if2p_b])
        gps.op(lambda e: e.indirect_dma_start(
            out=d.x1g_all[:, :],
            out_offset=bass.IndirectOffsetOnAxis(ap=S.if1p[:, :], axis=0),
            in_=S.x1, in_offset=None),
            reads=[S.x1_b, S.if1p_b, c.x1g_buf], sem=x1sc_sem[pi][0], inc=16)
        gps.op(lambda e: e.indirect_dma_start(
            out=d.x1g_all[:, :],
            out_offset=bass.IndirectOffsetOnAxis(ap=S.if2p[:, :], axis=0),
            in_=S.x1, in_offset=None),
            reads=[S.x1_b, S.if2p_b, c.x1g_buf], sem=x1sc_sem[pi][1], inc=16)
        vec.op(lambda e: e.tensor_copy(S.weq[:, 0:1], S.w1), reads=[S.w1_b], writes=[S.weq_b])
        vec.op(lambda e: e.tensor_copy(S.weq[:, 1:2], S.w2), reads=[S.w2_b], writes=[S.weq_b])
        vec.op(lambda e: e.tensor_copy(S.weq[:, 2:3], S.f1p), reads=[S.f1p_b], writes=[S.weq_b])
        vec.op(lambda e: e.tensor_copy(S.weq[:, 3:4], S.f2p), reads=[S.f2p_b], writes=[S.weq_b])
        scl.op(lambda e, t=t: e.dma_start(out=d.weq_sp[t * 128:t * 128 + 128, :],
                                          in_=S.weq[:, 0:4]),
               reads=[S.weq_b], sem=weq_st[pi], inc=16)
        scl.op(lambda e, t=t: e.dma_start(out=d.x1tm_sp[t * 128:t * 128 + 128, :], in_=S.x1),
               reads=[S.x1_b], sem=x1tm_st[pi], inc=16)
        if cfg.dbg:
            gps.op(lambda e, t=t: e.dma_start(out=d.dbg_x1[t * 128:t * 128 + 128, :],
                                              in_=S.x1),
                   reads=[S.x1_b], sem=dbg_sem[pi], inc=16)
            gps.op(lambda e, t=t: e.dma_start(out=d.dbg_sc[t * 128:t * 128 + 128, :],
                                              in_=S.scores),
                   reads=[S.scores_b], sem=dbg_sem2[pi], inc=16)

    for t in range(ntile):
        tile_body(t, t % 2, slots[t % 2])
    at.post_sems = (weq_st + x1tm_st + [sct_st] + sct_st2
                    + x1sc_sem[0] + x1sc_sem[1])

def make_ffn_split(P, d, c, cfg, stg, stg_sem, stg_buf):
    syn, act, vec, gps = P.sync, P.scalar, P.vector, P.gpsimd
    wsp = [P.bf16(2048) for _ in range(2)]
    wsp_buf = [Buf() for _ in range(2)]
    wst_sem = [P.new_sem("fst%d" % i) for i in range(2)]
    si = [0]

    def do(load_src, store_dst, deint):
        s = si[0] % 2
        si[0] += 1
        syn.op(lambda e: e.dma_start(out=stg[s], in_=load_src),
               writes=[stg_buf[s]], sem=stg_sem[s], inc=16)
        if deint:
            act.op(lambda e: e.copy(out=wsp[s][:, 0:512], in_=stg[s][:, 0:1024:2]),
                   reads=[stg_buf[s]], writes=[wsp_buf[s]])
            vec.op(lambda e: e.tensor_tensor(out=wsp[s][:, 512:1024], in0=stg[s][:, 0:1024:2],
                                             in1=wsp[s][:, 0:512], op=ALU.subtract),
                   reads=[stg_buf[s], wsp_buf[s]], writes=[wsp_buf[s]])
            act.op(lambda e: e.copy(out=wsp[s][:, 1024:1536], in_=stg[s][:, 1:1024:2]),
                   reads=[stg_buf[s], wsp_buf[s]], writes=[wsp_buf[s]])
            vec.op(lambda e: e.tensor_tensor(out=wsp[s][:, 1536:2048], in0=stg[s][:, 1:1024:2],
                                             in1=wsp[s][:, 1024:1536], op=ALU.subtract),
                   reads=[stg_buf[s], wsp_buf[s]], writes=[wsp_buf[s]])
        else:
            act.op(lambda e: e.copy(out=wsp[s][:, 0:1024], in_=stg[s]),
                   reads=[stg_buf[s]], writes=[wsp_buf[s]])
            vec.op(lambda e: e.tensor_tensor(out=wsp[s][:, 1024:2048], in0=stg[s],
                                             in1=wsp[s][:, 0:1024], op=ALU.subtract),
                   reads=[stg_buf[s], wsp_buf[s]], writes=[wsp_buf[s]])
        gps.op(lambda e: e.dma_start(out=store_dst, in_=wsp[s]),
               reads=[wsp_buf[s]], sem=wst_sem[s], inc=16)

    steps = []
    for e_ in range(16):
        for k in range(KCH):
            steps.append(lambda e_=e_, k=k: do(
                d.gate_up[e_, 128 * k:128 * (k + 1), :], d.wsgu[e_, k, :, :], True))
        for k2 in range(4):
            steps.append(lambda e_=e_, k2=k2: do(
                d.down[e_, 128 * k2:128 * (k2 + 1), :], d.wsdn[e_, k2, :, :], False))
    return wst_sem, steps


def emit_ffn(P, d, c, cfg, at):
    from concourse import library_config
    I32 = mybir.dt.int32
    U32 = mybir.dt.uint32
    L = cfg.L
    TOWN = L // 2
    CAP = 768 if TOWN >= 2048 else TOWN
    NTT = CAP // 256
    T16 = TOWN // 16
    ten, vec, act, gps, syn, scl = P.tensor, P.vector, P.scalar, P.gpsimd, P.sync, P.scalar

    csem3 = P.new_sem("f_c2")
    rawf = P.f32(HID)
    rawf_b = Buf()
    syn.op(lambda e: e.dma_start(out=rawf[0:16, :], in_=d.dbias[:, :]), sem=csem3, inc=16,
           writes=[rawf_b])
    c.dbh = P.bf16(HID, parts=16); c.dbl = P.bf16(HID, parts=16)
    c.dbh_b, c.dbl_b = Buf(), Buf()
    split_sbuf(P, rawf[0:16, :], rawf_b, c.dbh, c.dbl, c.dbh_b, c.dbl_b, lo_eng=P.gpsimd)

    xg = [P.f32(HID) for _ in range(2)]
    xg_b = [Buf() for _ in range(2)]
    xg_sem = [P.new_sem("xg%d" % i) for i in range(2)]
    x1Teh = P.bf16(KCH * CAP); x1Teh_b = Buf()
    x1Tel = P.bf16(KCH * CAP); x1Tel_b = Buf()
    wge = P.bf16(KCH * 2048); wge_b = Buf()
    wge_sem = P.new_sem("wge")
    wde = [P.bf16(4 * 2048) for _ in range(2)]
    wde_b = [Buf() for _ in range(2)]
    wde_sem = [P.new_sem("wde%d" % i) for i in range(2)]
    gatec = P.f32(256); gatec_b = Buf()
    sigc = P.f32(256); sigc_b = Buf()
    gluc = P.f32(256); gluc_b = Buf()
    upc = P.f32(256); upc_b = Buf()
    actf = P.f32(256); actf_b = Buf()
    acth = P.bf16(1024); acth_b = Buf()
    actl = P.bf16(1024); actl_b = Buf()
    yfm = [P.f32(256) for _ in range(2)]
    yfm_b = [Buf() for _ in range(2)]
    ytm = [P.f32(HID) for _ in range(2)]
    ytm_b = [Buf() for _ in range(2)]
    ysp_st = [P.new_sem("ysp_st0"), P.new_sem("ysp_st1")]

    def load_wge(e_):
        syn.dma_batch([lambda e, e_=e_, k=k: e.dma_start(
            out=wge[:, 2048 * k:2048 * (k + 1)], in_=d.wsgu[e_, k, :, :])
            for k in range(KCH)], writes=[wge_b], sem=wge_sem)

    def load_wde(e_):
        s = e_ % 2
        syn.dma_batch([lambda e, e_=e_, k2=k2: e.dma_start(
            out=wde[s][:, 2048 * k2:2048 * (k2 + 1)], in_=d.wsdn[e_, k2, :, :])
            for k2 in range(4)], writes=[wde_b[s]], sem=wde_sem[s])

    load_wge(0)
    load_wde(0)
    tk = [0]

    def tbank():
        b = 2 + tk[0] % 2
        tk[0] += 1
        return b

    for e_ in range(16):
        ws = e_ % 2
        for j in range(CAP // 128):
            sj = j % 2
            r0 = CAP * e_ + 128 * j
            syn.op(lambda e, r0=r0, sj=sj: e.dma_start(
                out=xg[sj], in_=d.x1g_all[r0:r0 + 128, :]),
                reads=[c.x1g_buf], writes=[xg_b[sj]], sem=xg_sem[sj], inc=16)
            for half in range(2):
                b = tbank()
                fns = []
                for k in range(4 * half, 4 * half + 4):
                    fns.append(lambda e, k=k, b=b, sj=sj: e.transpose(
                        P.psum[b][:, (k % 4) * 128:(k % 4) * 128 + 128],
                        xg[sj][:, 128 * k:128 * k + 128], c.idf))
                ten.group(fns, reads=[xg_b[sj], c.cb], writes=[P.bank[b]])
                for k in range(4 * half, 4 * half + 4):
                    ps = P.psum[b][:, (k % 4) * 128:(k % 4) * 128 + 128]
                    co = CAP * k + 128 * j
                    act.op(lambda e, ps=ps, co=co: e.copy(out=x1Teh[:, co:co + 128], in_=ps),
                           reads=[P.bank[b]], writes=[x1Teh_b])
                    vec.op(lambda e, ps=ps, co=co: e.tensor_tensor(
                        out=x1Tel[:, co:co + 128], in0=ps, in1=x1Teh[:, co:co + 128],
                        op=ALU.subtract),
                        reads=[P.bank[b], x1Teh_b], writes=[x1Tel_b])
        for tt_ in range(NTT):
            ts_ = tt_ * 256
            for g in range(4):
                bgt = 4 if (g % 2 == 0) else 6
                bup = 5
                fns = []
                for k in range(KCH):
                    fns += mm3(P.psum[bgt][:, 0:256],
                               wge[:, 2048 * k + 128 * g:2048 * k + 128 * g + 128],
                               wge[:, 2048 * k + 512 + 128 * g:2048 * k + 512 + 128 * g + 128],
                               x1Teh[:, CAP * k + ts_:CAP * k + ts_ + 256],
                               x1Tel[:, CAP * k + ts_:CAP * k + ts_ + 256],
                               start=(k == 0), stop=(k == KCH - 1))
                ten.group(fns, reads=[wge_b, x1Teh_b, x1Tel_b], writes=[P.bank[bgt]])
                fns = []
                for k in range(KCH):
                    fns += mm3(P.psum[bup][:, 0:256],
                               wge[:, 2048 * k + 1024 + 128 * g:2048 * k + 1024 + 128 * g + 128],
                               wge[:, 2048 * k + 1536 + 128 * g:2048 * k + 1536 + 128 * g + 128],
                               x1Teh[:, CAP * k + ts_:CAP * k + ts_ + 256],
                               x1Tel[:, CAP * k + ts_:CAP * k + ts_ + 256],
                               start=(k == 0), stop=(k == KCH - 1))
                ten.group(fns, reads=[wge_b, x1Teh_b, x1Tel_b], writes=[P.bank[bup]])
                bcol = 8 * e_ + g
                vec.op(lambda e, bgt=bgt, bcol=bcol: e.tensor_scalar(
                    out=gatec, in0=P.psum[bgt][:, 0:256], scalar1=c.gub[:, bcol:bcol + 1],
                    scalar2=7.0, op0=ALU.add, op1=ALU.min),
                    reads=[P.bank[bgt], c.cb], writes=[gatec_b])
                act.op(lambda e: e.activation(out=sigc, in_=gatec, func=AF.Sigmoid, scale=1.702),
                       reads=[gatec_b], writes=[sigc_b])
                vec.op(lambda e: e.tensor_tensor(out=gluc, in0=gatec, in1=sigc, op=ALU.mult),
                       reads=[gatec_b, sigc_b], writes=[gluc_b])
                ucol = 8 * e_ + 4 + g
                vec.op(lambda e, bup=bup, ucol=ucol: e.tensor_scalar(
                    out=upc, in0=P.psum[bup][:, 0:256], scalar1=c.gub1[:, ucol:ucol + 1],
                    scalar2=8.0, op0=ALU.add, op1=ALU.min),
                    reads=[P.bank[bup], c.cb, c.gub1_b], writes=[upc_b])
                vec.op(lambda e: e.scalar_tensor_tensor(
                    out=actf, in0=upc, scalar=-6.0, in1=gluc, op0=ALU.max, op1=ALU.mult),
                    reads=[upc_b, gluc_b], writes=[actf_b])
                act.op(lambda e, g=g: e.copy(out=acth[:, 256 * g:256 * (g + 1)], in_=actf),
                       reads=[actf_b], writes=[acth_b])
                vec.op(lambda e, g=g: e.tensor_tensor(
                    out=actl[:, 256 * g:256 * (g + 1)], in0=actf,
                    in1=acth[:, 256 * g:256 * (g + 1)], op=ALU.subtract),
                    reads=[actf_b, acth_b], writes=[actl_b])
            if tt_ == NTT - 1 and e_ + 1 < 16:
                load_wge(e_ + 1)
            for oc in range(KCH):
                b = oc % 2
                fns = []
                for k2 in range(4):
                    fns += mm3(P.psum[b][:, 0:256],
                               wde[ws][:, 2048 * k2 + 128 * oc:2048 * k2 + 128 * oc + 128],
                               wde[ws][:, 2048 * k2 + 1024 + 128 * oc:2048 * k2 + 1024 + 128 * oc + 128],
                               acth[:, 256 * k2:256 * (k2 + 1)],
                               actl[:, 256 * k2:256 * (k2 + 1)],
                               start=(k2 == 0), stop=(k2 == 3))
                ten.group(fns, reads=[wde_b[ws], acth_b, actl_b], writes=[P.bank[b]])
                yp = oc % 2
                act.op(lambda e, b=b, yp=yp: e.copy(out=yfm[yp], in_=P.psum[b][:, 0:256]),
                       reads=[P.bank[b]], writes=[yfm_b[yp]])
                for sub in range(2):
                    b2 = tbank()
                    ten.group([lambda e, b2=b2, sub=sub, yp=yp: e.transpose(
                        P.psum[b2][:, 0:128], yfm[yp][:, 128 * sub:128 * sub + 128], c.idf)],
                        reads=[yfm_b[yp], c.cb], writes=[P.bank[b2]])
                    act.op(lambda e, b2=b2, sub=sub, oc=oc: e.copy(
                        out=ytm[sub][:, 128 * oc:128 * oc + 128], in_=P.psum[b2][:, 0:128]),
                        reads=[P.bank[b2]], writes=[ytm_b[sub]])
            for sub in range(2):
                r = e_ * CAP + ts_ + 128 * sub
                gps.op(lambda e, r=r, sub=sub: e.dma_start(
                    out=d.y_sp[r:r + 128, :], in_=ytm[sub]),
                    reads=[ytm_b[sub]], sem=ysp_st[sub], inc=16)
        if e_ + 1 < 16:
            load_wde(e_ + 1)

    gps.barrier_wait([(ysp_st[0], ysp_st[0].count), (ysp_st[1], ysp_st[1].count)])
    x1f = [P.f32(HID) for _ in range(2)]
    x1f_b = [Buf() for _ in range(2)]
    x1f_sem = [P.new_sem("x1f%d" % i) for i in range(2)]
    weqf = [P.f32(4) for _ in range(2)]
    weqf_b = [Buf() for _ in range(2)]
    weqf_sem = [P.new_sem("wq%d" % i) for i in range(2)]
    sch = [P.bf16(128, parts=16) for _ in range(2)]
    sch_b = [Buf() for _ in range(2)]
    sch_sem = [P.new_sem("sh%d" % i) for i in range(2)]
    scl2 = [P.bf16(128, parts=16) for _ in range(2)]
    scl2_b = [Buf() for _ in range(2)]
    scl2_sem = [P.new_sem("sc%d" % i) for i in range(2)]
    if1 = [P.sb("ifc1_%d" % i, [128, 1], I32) for i in range(2)]
    if1_b = [Buf() for _ in range(2)]
    if2 = [P.sb("ifc2_%d" % i, [128, 1], I32) for i in range(2)]
    if2_b = [Buf() for _ in range(2)]
    y1t = [P.f32(HID) for _ in range(2)]
    y1t_b = [Buf() for _ in range(2)]
    y1_sem = [P.new_sem("y1%d" % i) for i in range(2)]
    y2t = [P.f32(HID) for _ in range(2)]
    y2t_b = [Buf() for _ in range(2)]
    y2_sem = [P.new_sem("y2%d" % i) for i in range(2)]
    tc1 = P.f32(HID); tc1_b = Buf()
    rr2 = P.f32(HID); rr2_b = Buf()
    yt = P.f32(HID); yt_b = Buf()
    st6f = P.f32(12); st6f_b = Buf()
    mvf = P.f32(2); mvf_b = Buf()
    sdf = P.f32(1); sdf_b = Buf()
    istdf = P.f32(1); istdf_b = Buf()
    yst = P.new_sem("yst")

    for t in range(TOWN // 128):
        pi = t % 2
        syn.op(lambda e, t=t, pi=pi: e.dma_start(
            out=x1f[pi], in_=d.x1tm_sp[t * 128:t * 128 + 128, :]),
            writes=[x1f_b[pi]], sem=x1f_sem[pi], inc=16)
        syn.op(lambda e, t=t, pi=pi: e.dma_start(
            out=weqf[pi], in_=d.weq_sp[t * 128:t * 128 + 128, :]),
            writes=[weqf_b[pi]], sem=weqf_sem[pi], inc=16)
        syn.op(lambda e, t=t, pi=pi: e.dma_start(
            out=sch[pi], in_=d.scth_sp[:, t * 128:t * 128 + 128]),
            writes=[sch_b[pi]], sem=sch_sem[pi], inc=16)
        syn.op(lambda e, t=t, pi=pi: e.dma_start(
            out=scl2[pi], in_=d.sctl_sp[:, t * 128:t * 128 + 128]),
            writes=[scl2_b[pi]], sem=scl2_sem[pi], inc=16)
        vec.op(lambda e, pi=pi: e.tensor_copy(if1[pi][:, :], weqf[pi][:, 2:3]),
               reads=[weqf_b[pi]], writes=[if1_b[pi]])
        vec.op(lambda e, pi=pi: e.tensor_copy(if2[pi][:, :], weqf[pi][:, 3:4]),
               reads=[weqf_b[pi]], writes=[if2_b[pi]])
        gps.op(lambda e, pi=pi: e.indirect_dma_start(
            out=y1t[pi], out_offset=None, in_=d.y_sp[:, :],
            in_offset=bass.IndirectOffsetOnAxis(ap=if1[pi][:, :], axis=0)),
            reads=[if1_b[pi]], writes=[y1t_b[pi]], sem=y1_sem[pi], inc=16)
        gps.op(lambda e, pi=pi: e.indirect_dma_start(
            out=y2t[pi], out_offset=None, in_=d.y_sp[:, :],
            in_offset=bass.IndirectOffsetOnAxis(ap=if2[pi][:, :], axis=0)),
            reads=[if2_b[pi]], writes=[y2t_b[pi]], sem=y2_sem[pi], inc=16)
        for half in range(2):
            b2 = half
            ten.group(mm3(P.psum[b2][:, 0:512],
                          sch[pi], scl2[pi],
                          c.dbh[:, 512 * half:512 * half + 512],
                          c.dbl[:, 512 * half:512 * half + 512]),
                      reads=[sch_b[pi], scl2_b[pi], c.dbh_b, c.dbl_b], writes=[P.bank[b2]])
        vec.op(lambda e, pi=pi: e.tensor_scalar(
            out=tc1, in0=y1t[pi], scalar1=weqf[pi][:, 0:1], scalar2=None, op0=ALU.mult),
            reads=[y1t_b[pi], weqf_b[pi]], writes=[tc1_b])
        vec.op(lambda e, pi=pi: e.scalar_tensor_tensor(
            out=tc1, in0=y2t[pi], scalar=weqf[pi][:, 1:2], in1=tc1,
            op0=ALU.mult, op1=ALU.add),
            reads=[y2t_b[pi], weqf_b[pi], tc1_b], writes=[tc1_b])
        vec.op(lambda e, pi=pi: e.tensor_tensor(out=rr2, in0=tc1, in1=x1f[pi], op=ALU.add),
               reads=[tc1_b, x1f_b[pi]], writes=[rr2_b])
        for half in range(2):
            b2 = half
            vec.op(lambda e, b2=b2, half=half: e.tensor_tensor(
                out=rr2[:, 512 * half:512 * half + 512],
                in0=rr2[:, 512 * half:512 * half + 512],
                in1=P.psum[b2][:, 0:512], op=ALU.add),
                reads=[P.bank[b2], rr2_b], writes=[rr2_b])
        vec.op(lambda e: e.bn_stats(out=st6f[:, 0:6], in_=rr2[:, 0:512]),
               reads=[rr2_b], writes=[st6f_b])
        vec.op(lambda e: e.bn_stats(out=st6f[:, 6:12], in_=rr2[:, 512:1024]),
               reads=[rr2_b], writes=[st6f_b])
        vec.op(lambda e: e.bn_aggr(out=mvf, in_=st6f), reads=[st6f_b], writes=[mvf_b])
        act.op(lambda e: e.activation(out=sdf, in_=mvf[:, 1:2], func=AF.Sqrt,
                                      bias=c.eps[:, 0:1]),
               reads=[mvf_b, c.cb], writes=[sdf_b])
        vec.op(lambda e: e.reciprocal(out=istdf, in_=sdf), reads=[sdf_b], writes=[istdf_b])
        vec.op(lambda e: e.tensor_scalar(
            out=yt, in0=rr2, scalar1=mvf[:, 0:1], scalar2=istdf[:, 0:1],
            op0=ALU.subtract, op1=ALU.mult),
            reads=[rr2_b, mvf_b, istdf_b], writes=[yt_b])
        gps.op(lambda e, t=t: e.dma_start(out=d.y[t * 128:t * 128 + 128, :], in_=yt),
               reads=[yt_b], sem=yst, inc=16)
    P.gpsimd.barrier_wait([(yst, yst.count)])





